# version 20
# baseline (speedup 1.0000x reference)
import numpy as np
import ml_dtypes
from contextlib import ExitStack

import concourse.bass as bass
import concourse.mybir as mybir
from concourse.bass_utils import run_bass_kernel_spmd

F32 = mybir.dt.float32
BF16 = mybir.dt.bfloat16
ALU = mybir.AluOpType
AF = mybir.ActivationFunctionType
AX = mybir.AxisListType

P = 128
D = 1024
DC = 8
NTOK = 2048
MT = 1024
TCH = 8
NEXP = 17
W = 24
K_J = 8
EPS = 1e-5


def build_program(n_mts=2, dbg_taps=None):
    nc = bass.Bass("TRN2", target_bir_lowering=False)

    xh = nc.dram_tensor("xh", [W + NTOK, D], F32, kind="ExternalInput").ap()
    w13T = nc.dram_tensor("w13T", [NEXP, D, D], F32, kind="ExternalInput").ap()
    w2T = nc.dram_tensor("w2T", [NEXP, 512, D], F32, kind="ExternalInput").ap()
    wgT = nc.dram_tensor("wgT", [D, D], F32, kind="ExternalInput").ap()
    woT_d = nc.dram_tensor("woT", [17, D], F32, kind="ExternalInput").ap()
    wsT = nc.dram_tensor("wsT", [D, 16], F32, kind="ExternalInput").ap()
    wiT = nc.dram_tensor("wiT", [D, 16], F32, kind="ExternalInput").ap()
    gwT = nc.dram_tensor("gwT", [D, 16], F32, kind="ExternalInput").ap()
    a_st = nc.dram_tensor("a_st", [64, 16], F32, kind="ExternalInput").ap()
    bsbi = nc.dram_tensor("bsbi", [16, 2], F32, kind="ExternalInput").ap()
    gb8 = nc.dram_tensor("gb8", [1, 128], F32, kind="ExternalInput").ap()
    lnw_r = nc.dram_tensor("lnw_r", [1, D], F32, kind="ExternalInput").ap()
    lnb_r = nc.dram_tensor("lnb_r", [1, D], F32, kind="ExternalInput").ap()
    m1r = nc.dram_tensor("m1r", [1, 1056], F32, kind="ExternalInput").ap()
    eye_f = nc.dram_tensor("eye_f", [P, P], F32, kind="ExternalInput").ap()
    ones_f = nc.dram_tensor("ones_f", [1, P], F32, kind="ExternalInput").ap()
    umask_d = nc.dram_tensor("umask", [16, W], F32, kind="ExternalInput").ap()
    out_d = nc.dram_tensor("out", [NTOK, D], F32, kind="ExternalOutput").ap()

    dbg_drams = {}
    if dbg_taps:
        for name, shape, dt in dbg_taps:
            dbg_drams[name] = nc.dram_tensor("dbg_" + name, shape, dt, kind="ExternalOutput").ap()

    ctx = ExitStack()
    e = ctx.enter_context

    xarena = e(nc.sbuf_tensor("xarena", [P, 3072], F32))
    xn_tm = e(nc.sbuf_tensor("xn_tm", [P, 8192], BF16))
    xn_fm = e(nc.sbuf_tensor("xn_fm", [P, 8192], BF16))
    xn2_fm = e(nc.sbuf_tensor("xn2_fm", [P, 8192], BF16))
    s_tm = e(nc.sbuf_tensor("s_tm", [P, 8192], BF16))
    y_sb = e(nc.sbuf_tensor("y_sb", [P, 8192], F32))
    w13sb = [e(nc.sbuf_tensor(f"w13sb{i}", [P, 8192], BF16)) for i in range(2)]
    w2sb = [e(nc.sbuf_tensor(f"w2sb{i}", [P, 4096], BF16)) for i in range(2)]
    hp0 = e(nc.sbuf_tensor("hp0", [P, 4096], BF16))
    hp = [hp0, hp0]
    gbuf0 = e(nc.sbuf_tensor("gbuf0", [P, 1024], BF16))
    gbuf = [gbuf0, gbuf0]
    fbuf0 = e(nc.sbuf_tensor("fbuf0", [P, 1024], BF16))
    fbuf = [fbuf0, fbuf0]
    silu_sb = [e(nc.sbuf_tensor(f"silu{i}", [P, 512], BF16)) for i in range(2)]
    h1b = [e(nc.sbuf_tensor(f"h1b{i}", [P, 512], BF16)) for i in range(2)]
    cb_sb = [e(nc.sbuf_tensor(f"cb_sb{i}", [P, 1024], BF16)) for i in range(2)]
    xn2c = [e(nc.sbuf_tensor(f"xn2c{i}", [P, 1024], BF16)) for i in range(2)]
    t2c0 = e(nc.sbuf_tensor("t2c0", [P, 1024], BF16))
    t2c = [t2c0, t2c0]
    M_t = [e(nc.sbuf_tensor(f"M{i}", [64, 1056], BF16)) for i in range(2)]
    woT = e(nc.sbuf_tensor("woTs", [17, 1024], BF16))
    gwTs = e(nc.sbuf_tensor("gwTs", [P, 128], BF16))
    wsumT = e(nc.sbuf_tensor("wsumT", [P, 128], BF16))
    wsf = xarena[:, 0:128]
    wif = xarena[:, 128:256]
    wsumf = xarena[:, 256:384]
    a_sb = e(nc.sbuf_tensor("a_sb", [64, 16], BF16))
    eye_b = e(nc.sbuf_tensor("eye_b", [P, P], BF16))
    ones_b = e(nc.sbuf_tensor("ones_b", [1, P], BF16))
    umask_sb = e(nc.sbuf_tensor("umask_sb", [48, W], BF16))
    lnw_bt = e(nc.sbuf_tensor("lnw_bt", [P, D], BF16))
    lnb_bt = e(nc.sbuf_tensor("lnb_bt", [P, D], BF16))
    gb_bt = e(nc.sbuf_tensor("gb_bt", [P, 128], F32))
    lnw_row = e(nc.sbuf_tensor("lnw_row", [1, D], BF16))
    lnb_row = e(nc.sbuf_tensor("lnb_row", [1, D], BF16))
    cbrow = [lnw_row, lnb_row]
    gb_row = e(nc.sbuf_tensor("gb_row", [1, 128], BF16))
    bsbi_sb = e(nc.sbuf_tensor("bsbi_sb", [16, 2], F32))
    epsc = e(nc.sbuf_tensor("epsc", [P, 1], F32))
    ub_sb = e(nc.sbuf_tensor("ub_sb", [16, 1], F32))
    stats = e(nc.sbuf_tensor("stats", [P, 128], F32))
    wstat = e(nc.sbuf_tensor("wstat", [32, 32], F32))
    wxn = s_tm[0:32, 0:1024]
    wxn2 = s_tm[0:32, 1024:2048]
    wt2 = s_tm[0:32, 2048:3072]
    xnw2_fm = e(nc.sbuf_tensor("xnw2_fm", [P, DC * W], BF16))
    scores_sb = e(nc.sbuf_tensor("scores_sb", [P, 128], F32))
    biased = xarena[:, 2048:2176]
    m8s = xarena[:, 2432:2496]
    masks = xarena[:, 2176:2304]
    wraw = xarena[:, 2304:2432]
    sumw = xarena[:, 2496:2504]
    rsw = xarena[:, 2504:2512]
    comb_tm = e(nc.sbuf_tensor("comb_tm", [P, 128], BF16))
    comb_fm = e(nc.sbuf_tensor("comb_fm", [16, 1024], BF16))

    SUMS, SQS, MEANB, MSQB, NVAR, STDB, RSTD, NMR = range(8)
    SUMS2, SQS2, MEAN2, MSQ2, NVAR2, STD2, RSTD2, NMR2 = range(8, 16)

    def st(i):
        return stats[:, i * 8:(i + 1) * 8]

    def stc(i, tch):
        return stats[:, i * 8 + tch: i * 8 + tch + 1]

    GA = [e(nc.psum_tensor(f"GA{i}", [P, 512], F32)) for i in range(2)]
    UP = [e(nc.psum_tensor(f"UP{i}", [P, 512], F32)) for i in range(2)]
    Y2 = [e(nc.psum_tensor(f"Y2{i}", [P, 512], F32)) for i in range(2)]
    CBp = e(nc.psum_tensor("CBp", [P, 512], F32))
    TRp = e(nc.psum_tensor("TRp", [P, 1024], BF16))

    sems = {}
    cnt = {}
    for name in ("T", "S", "V", "Pc", "dIY", "dIP", "dWX", "dX0", "dX1", "dX2",
                 "dW13a", "dW13b", "dW2a", "dW2b", "dCB0", "dCB1", "dO", "dDBG",
                 "dTR0", "dTR1", "dTR2", "dTR3", "dTR4", "dTR5", "dTR6", "dTR7"):
        sems[name] = nc.alloc_semaphore(name="sem_" + name)
        cnt[name] = 0
    ops = {"T": [], "S": [], "V": [], "P": [], "Y": []}
    ops["C"] = ops["S"]

    def op(stream, emitfn, waits=(), inc=None, inck=1):
        evt = None
        if inc is not None:
            cnt[inc] += inck
            evt = (inc, cnt[inc])
        ops[stream].append((list(waits), emitfn, inc, inck))
        return evt

    def dma(stream, sem_name, out_ap, in_ap, waits=(), transpose=False):
        return op(stream, lambda g: g.dma_start(out_ap, in_ap, transpose=transpose),
                  waits, sem_name, 16)

    def tr_batch(src_tm, dst_fm, tch, waits, rot):
        sname = f"dTR{rot[0] % 8}"
        rot[0] += 1
        ops["C"].append(([(sname, cnt[sname])], None, None, 0))
        ev = None
        for dc in range(DC):
            ev = dma("C", sname,
                     dst_fm[:, dc * 1024 + tch * 128: dc * 1024 + (tch + 1) * 128],
                     src_tm[:, dc * 128:(dc + 1) * 128],
                     waits if dc == 0 else [], transpose=True)
        return ev

    ld_wsf = dma("Y", "dIY", wsf.rearrange("p (kc n) -> p kc n", n=16),
                 wsT.rearrange("(kc p) n -> p kc n", p=P))
    ld_wif = dma("Y", "dIY", wif.rearrange("p (kc n) -> p kc n", n=16),
                 wiT.rearrange("(kc p) n -> p kc n", p=P))
    ld_bsbi = dma("Y", "dIY", bsbi_sb[:], bsbi)

    ld_a = dma("P", "dIP", a_sb[:], a_st)
    ld_gbr = dma("P", "dIP", gb_row[:], gb8)
    ld_eye = dma("P", "dIP", eye_b[:], eye_f)
    ld_ones = dma("P", "dIP", ones_b[:], ones_f)
    ld_um = dma("P", "dIP", umask_sb[32:48, :], umask_d)
    ld_lnwr = dma("P", "dIP", lnw_row[:], lnw_r)
    ld_lnbr = dma("P", "dIP", lnb_row[:], lnb_r)
    ld_gw = dma("P", "dIP", gwTs[:].rearrange("p (kc n) -> p kc n", n=16),
                gwT.rearrange("(kc p) n -> p kc n", p=P))
    ld_wo = dma("P", "dIP", woT[:], woT_d)
    ld_wg = dma("P", "dW13b", w13sb[1][:].rearrange("p (kc m) -> p kc m", m=1024),
                wgT.rearrange("(kc p) m -> p kc m", p=P))
    ld_w13_0 = dma("P", "dW13a", w13sb[0][:].rearrange("p (kc m) -> p kc m", m=1024),
                   w13T[0].rearrange("(kc p) m -> p kc m", p=P))
    ld_w2_0 = dma("P", "dW2a", w2sb[0][:].rearrange("p (ic m) -> p ic m", m=1024),
                  w2T[0].rearrange("(ic p) m -> p ic m", p=P))

    ev_initY = ("dIY", cnt["dIY"])
    ev_initP = ("dIP", cnt["dIP"])
    ev_eps = op("V", lambda v: v.memset(epsc[:], float(EPS)), [], "V")
    ev_ub = op("V", lambda v: v.tensor_tensor(ub_sb[:], bsbi_sb[:, 0:1], bsbi_sb[:, 1:2], op=ALU.add),
               [ev_initY], "V")
    ev_wsumf = op("V", lambda v: v.tensor_tensor(wsumf, wsf, wif, op=ALU.add),
                  [ev_initY], "V")
    ev_wsumT = op("S", lambda s: s.copy(wsumT[:], wsumf), [ev_wsumf], "S")

    def bc_half(dst_ap, row_ap, waits, prev_free):
        mm = op("T", (lambda: (lambda t: t.matmul(
            CBp[:, 0:row_ap.shape[-1]], ones_b[:], row_ap, start=True, stop=True)))(),
            list(waits) + ([prev_free] if prev_free else []), "T")
        return op("V", (lambda: (lambda v: v.tensor_copy(dst_ap, CBp[:, 0:row_ap.shape[-1]])))(),
                  [mm], "V")

    evb = bc_half(lnw_bt[:, 0:512], lnw_row[:, 0:512], [ev_initP], None)
    ev_lnw = bc_half(lnw_bt[:, 512:1024], lnw_row[:, 512:1024], [], evb)
    evb = bc_half(lnb_bt[:, 0:512], lnb_row[:, 0:512], [ev_initP], ev_lnw)
    ev_lnb = bc_half(lnb_bt[:, 512:1024], lnb_row[:, 512:1024], [], evb)
    ev_gb = bc_half(gb_bt[:], gb_row[:], [ev_initP], ev_lnb)

    ev_mz0 = op("S", lambda s: s.memzero(M_t[0][:]), [], "S")
    ev_mz1 = op("S", lambda s: s.memzero(M_t[1][:]), [], "S")
    dma("P", "dTR0", M_t[0][16:17, :], m1r, [ev_mz0])
    ev_mz1 = dma("P", "dTR0", M_t[1][16:17, :], m1r, [ev_mz1])
    ev_mz1 = ("dTR0", 32)

    st_ev = {
        "xslot": [ev_wsumT, None, None],
        "xslot_w": [None] * 3,
        "trp_free": None,
        "cbp_free": ev_gb,
        "ga_free": [None, None], "up_free": [None, None],
        "y2_free": [None, None],
        "w13_ld": [ld_w13_0, ld_wg],
        "w2_ld": [ld_w2_0, None],
        "w13_rdT": [None, None], "w2_rdT": [None, None],
        "wg_rdT": None,
        "xn_tm_rd": None,
        "xn_fm_rd": None,
        "xn2_rd": None,
        "s_rd": None,
        "y_free": None,
        "silu_free": [None, None],
        "h1_free": [None, None],
        "hp_free": [None, None],
        "cbrow_free": [ev_lnw, ev_lnb],
        "cb_evs": [None, None],
        "cb_lastread": [None, None],
        "rext_ev": [None, None],
        "tr_rot": [0],
        "gbuf_free": [None, None],
        "fbuf_free": [None, None],
        "xn2c_free": [None, None],
        "t2c_free": [None, None],
        "m_rd": ev_mz1,
        "stats_free": None,
        "stats_free2": None,
        "hfin_prev": None,
        "out_dma": [],
    }

    def xrow0(mt):
        return W + mt * MT

    def stats_chain(ME, MS, NV, SD, RS, NM, SU, SQ, c0, c1, last_sum, last_sq):
        def sl(i):
            return stats[:, i * 8 + c0: i * 8 + c1]
        m_ev = op("P", lambda g: g.tensor_scalar_mul(sl(ME), sl(SU), -1.0 / D), [last_sum], "Pc")
        q_ev = op("P", lambda g: g.tensor_scalar_mul(sl(MS), sl(SQ), 1.0 / D), [last_sq, m_ev], "Pc")
        m2_ev = op("P", lambda g: g.tensor_tensor(sl(NV), sl(ME), sl(ME), op=ALU.mult),
                   [m_ev, q_ev], "Pc")
        nv_ev = op("P", lambda g: g.tensor_tensor(sl(NV), sl(NV), sl(MS), op=ALU.subtract),
                   [m2_ev], "Pc")
        sd_ev = op("S", lambda s: s.activation(sl(SD), sl(NV), AF.Sqrt,
                                               bias=epsc[:, 0:1], scale=-1.0),
                   [nv_ev, ev_eps], "S")
        rs_ev = op("V", lambda v: v.reciprocal(sl(RS), sl(SD)), [sd_ev], "V")
        nm_ev = op("P", lambda g: g.tensor_tensor(sl(NM), sl(ME), sl(RS), op=ALU.mult),
                   [rs_ev], "Pc")
        return rs_ev, nm_ev

    def gen_A(mt, C):
        warm = (mt == 0)
        x_ev = [None] * TCH

        def issue_x(tch, extra):
            slot = tch % 3
            x_ev[tch] = dma("Y", f"dX{slot}",
                            xarena[:, slot * 1024:(slot + 1) * 1024],
                            xh[xrow0(mt) + tch * P: xrow0(mt) + (tch + 1) * P, :],
                            [st_ev["xslot"][slot], st_ev["xslot_w"][slot]] + extra)
            st_ev["xslot_w"][slot] = x_ev[tch]
            st_ev["xslot"][slot] = x_ev[tch]

        for t0 in range(3):
            issue_x(t0, [])
        if warm:
            wx_ev = dma("Y", "dWX", y_sb[0:W, 7168:8192], xh[0:W, :], [st_ev["y_free"]])
        yield
        wstate = {"uw": None}

        def gen_warm():
            wxs = y_sb[0:W, 7168:8192]
            wscr = y_sb[0:W, 6144:7168]
            a1 = op("V", lambda v: v.tensor_reduce(wstat[0:W, 0:1], wxs, axis=AX.X, op=ALU.add),
                    [wx_ev], "V")
            yield
            a3 = op("S", lambda s: s.activation(wscr, wxs, AF.Square, accum_out=wstat[0:W, 1:2]),
                    [wx_ev], "S")
            yield
            a4 = op("P", lambda g: g.tensor_scalar_mul(wstat[0:W, 2:3], wstat[0:W, 0:1], -1.0 / D),
                    [a1], "Pc")
            yield
            a5 = op("P", lambda g: g.tensor_scalar_mul(wstat[0:W, 3:4], wstat[0:W, 1:2], 1.0 / D),
                    [a3, a4], "Pc")
            yield
            a6 = op("P", lambda g: g.tensor_tensor(wstat[0:W, 4:5], wstat[0:W, 2:3],
                                                   wstat[0:W, 2:3], op=ALU.mult), [a5], "Pc")
            yield
            a7 = op("P", lambda g: g.tensor_tensor(wstat[0:W, 4:5], wstat[0:W, 4:5],
                                                   wstat[0:W, 3:4], op=ALU.subtract), [a6], "Pc")
            yield
            a8 = op("S", lambda s: s.activation(wstat[0:W, 5:6], wstat[0:W, 4:5], AF.Sqrt,
                                                bias=epsc[0:W, 0:1], scale=-1.0), [a7, ev_eps], "S")
            yield
            a9 = op("V", lambda v: v.reciprocal(wstat[0:W, 6:7], wstat[0:W, 5:6]), [a8], "V")
            yield
            a10 = op("P", lambda g: g.tensor_tensor(wstat[0:W, 7:8], wstat[0:W, 2:3],
                                                    wstat[0:W, 6:7], op=ALU.mult), [a9], "Pc")
            yield
            wxn_ev = op("S", lambda s: s.activation(wxn[0:W, :], wxs, AF.Identity,
                                                    bias=wstat[0:W, 7:8], scale=wstat[0:W, 6:7]),
                        [a10], "S")
            yield
            b1 = op("V", lambda v: v.tensor_reduce(wstat[0:W, 8:9], wxn[0:W, :], axis=AX.X,
                                                   op=ALU.add), [wxn_ev], "V")
            yield
            b3 = op("S", lambda s: s.activation(wscr, wxn[0:W, :], AF.Square,
                                                accum_out=wstat[0:W, 9:10]), [wxn_ev], "S")
            yield
            b4 = op("P", lambda g: g.tensor_scalar_mul(wstat[0:W, 10:11], wstat[0:W, 8:9],
                                                       -1.0 / D), [b1], "Pc")
            yield
            b5 = op("P", lambda g: g.tensor_scalar_mul(wstat[0:W, 11:12], wstat[0:W, 9:10],
                                                       1.0 / D), [b3, b4], "Pc")
            yield
            b6 = op("P", lambda g: g.tensor_tensor(wstat[0:W, 12:13], wstat[0:W, 10:11],
                                                   wstat[0:W, 10:11], op=ALU.mult), [b5], "Pc")
            yield
            b7 = op("P", lambda g: g.tensor_tensor(wstat[0:W, 12:13], wstat[0:W, 12:13],
                                                   wstat[0:W, 11:12], op=ALU.subtract), [b6], "Pc")
            yield
            b8 = op("S", lambda s: s.activation(wstat[0:W, 13:14], wstat[0:W, 12:13], AF.Sqrt,
                                                bias=epsc[0:W, 0:1], scale=-1.0), [b7, ev_eps], "S")
            yield
            b9 = op("V", lambda v: v.reciprocal(wstat[0:W, 14:15], wstat[0:W, 13:14]), [b8], "V")
            yield
            b10 = op("P", lambda g: g.tensor_tensor(wstat[0:W, 15:16], wstat[0:W, 10:11],
                                                    wstat[0:W, 14:15], op=ALU.mult), [b9], "Pc")
            yield
            wt1_ev = op("S", lambda s: s.activation(wxn2[0:W, :], wxn[0:W, :], AF.Identity,
                                                    bias=wstat[0:W, 15:16], scale=wstat[0:W, 14:15]),
                        [b10], "S")
            yield
            wt2_ev = op("V", lambda v: v.tensor_tensor(wt2[0:W, :], wxn2[0:W, :], lnw_bt[0:W, :],
                                                       op=ALU.mult), [wt1_ev, ev_lnw], "V")
            yield
            wxn2_ev = op("P", lambda g: g.tensor_tensor(wxn2[0:W, :], wt2[0:W, :], lnb_bt[0:W, :],
                                                        op=ALU.add), [wt2_ev, ev_lnb], "Pc")
            yield
            wstate["wxn2"] = wxn2_ev
            yield

        wg_gen = gen_warm() if warm else iter(())

        def pump(gen, n):
            for _ in range(n):
                if next(gen, "END") == "END":
                    break

        sum_ev = [None] * TCH
        sqs_ev = [None] * TCH
        xn_ev = [None] * TCH
        for g in range(4):
            for tch in (2 * g, 2 * g + 1):
                xs = xarena[:, (tch % 3) * 1024:((tch % 3) + 1) * 1024]
                scr = xn_tm[:, tch * 1024:(tch + 1) * 1024]
                sum_ev[tch] = op("V", (lambda xs, tch: lambda v: v.tensor_reduce(
                    stc(SUMS, tch), xs, axis=AX.X, op=ALU.add))(xs, tch),
                    [x_ev[tch], st_ev["stats_free"], st_ev["stats_free2"]], "V")
                sqs_ev[tch] = op("S", (lambda xs, scr, tch: lambda s: s.activation(
                    scr, xs, AF.Square, accum_out=stc(SQS, tch)))(xs, scr, tch),
                    [x_ev[tch], st_ev["xn_tm_rd"], st_ev["stats_free"],
                     st_ev["stats_free2"]], "S")
            rs1_ev, nm1_ev = stats_chain(MEANB, MSQB, NVAR, STDB, RSTD, NMR, SUMS, SQS,
                                         2 * g, 2 * g + 2,
                                         sum_ev[2 * g + 1], sqs_ev[2 * g + 1])
            for tch in (2 * g, 2 * g + 1):
                xs = xarena[:, (tch % 3) * 1024:((tch % 3) + 1) * 1024]
                xn_ev[tch] = op("S", (lambda xs, tch: lambda s: s.activation(
                    xn_tm[:, tch * 1024:(tch + 1) * 1024], xs, AF.Identity,
                    bias=stc(NMR, tch), scale=stc(RSTD, tch)))(xs, tch),
                    [nm1_ev, rs1_ev], "S")
                st_ev["xslot"][tch % 3] = xn_ev[tch]
            for t in (2 * g + 3, 2 * g + 4):
                if t < TCH and x_ev[t] is None:
                    issue_x(t, [xn_ev[t - 3]])
            pump(wg_gen, 7)
            yield

        sum2_ev = [None] * TCH
        sqs2_ev = [None] * TCH
        for tch in range(TCH):
            xs = xn_tm[:, tch * 1024:(tch + 1) * 1024]
            scr = xn2c[tch % 2][:]
            scr_war = sqs2_ev[tch - 2] if tch >= 2 else st_ev["xn2c_free"][tch % 2]
            sum2_ev[tch] = op("V", (lambda xs, tch: lambda v: v.tensor_reduce(
                stc(SUMS2, tch), xs, axis=AX.X, op=ALU.add))(xs, tch), [xn_ev[tch]], "V")
            sqs2_ev[tch] = op("S", (lambda xs, scr, tch: lambda s: s.activation(
                scr, xs, AF.Square, accum_out=stc(SQS2, tch)))(xs, scr, tch),
                [xn_ev[tch], scr_war], "S")
            if tch % 2 == 1:
                pump(wg_gen, 6)
                yield
        rs2_ev, nm2_ev = stats_chain(MEAN2, MSQ2, NVAR2, STD2, RSTD2, NMR2, SUMS2, SQS2,
                                     0, 8, sum2_ev[-1], sqs2_ev[-1])
        pump(wg_gen, 99)
        uw_src_ev = wstate.get("wxn2")

        C.update(x_ev=x_ev, xn_ev=xn_ev, uw_src_ev=uw_src_ev,
                 nm2_ev=nm2_ev, rs2_ev=rs2_ev)
        yield

    def gen_B1(mt, C):
        xn_ev = C["xn_ev"]
        nm2_ev = C["nm2_ev"]
        rs2_ev = C["rs2_ev"]
        xn2_fm_ev = [None] * TCH
        t1_last = None
        for tch in range(TCH):
            c = tch % 2
            t1_ev = op("S", (lambda tch, c: lambda s: s.activation(
                xn2c[c][:], xn_tm[:, tch * 1024:(tch + 1) * 1024], AF.Identity,
                bias=stc(NMR2, tch), scale=stc(RSTD2, tch)))(tch, c),
                [nm2_ev, rs2_ev, st_ev["xn2c_free"][c]], "S")
            t1_last = t1_ev
            t2_ev = op("V", (lambda c: lambda v: v.tensor_tensor(
                t2c[c][:], xn2c[c][:], lnw_bt[:], op=ALU.mult))(c),
                [t1_ev, ev_lnw, st_ev["t2c_free"][c]], "V")
            xn2c_ev = op("P", (lambda c: lambda g: g.tensor_tensor(
                xn2c[c][:], t2c[c][:], lnb_bt[:], op=ALU.add))(c),
                [t2_ev, ev_lnb], "Pc")
            st_ev["t2c_free"][0] = xn2c_ev
            st_ev["t2c_free"][1] = xn2c_ev
            mm = None
            for dc in range(DC):
                mm = op("T", (lambda c, dc: lambda t: t.transpose(
                    TRp[:, dc * 128:(dc + 1) * 128],
                    xn2c[c][:, dc * 128:(dc + 1) * 128], eye_b[:]))(c, dc),
                    [xn2c_ev, st_ev["trp_free"], ev_initP] if dc == 0 else [],
                    "T" if dc == DC - 1 else None)
            ev2 = op("V", (lambda tch: lambda v: v.tensor_copy(
                xn2_fm[:].rearrange("p (dc t) -> p dc t", t=1024)[:, :, tch * 128:(tch + 1) * 128],
                TRp[:].rearrange("p (dc t) -> p dc t", t=128)))(tch),
                [mm, st_ev["xn2_rd"]], "V")
            xn2_fm_ev[tch] = ev2
            st_ev["trp_free"] = ev2
            st_ev["xn2c_free"][c] = ev2
            yield
        st_ev["stats_free"] = nm2_ev
        st_ev["stats_free2"] = t1_last
        C.update(xn2_fm_ev=xn2_fm_ev)
        yield

    def gen_B2(mt, C):
        xn_ev = C["xn_ev"]
        xn_fm_ev = [None] * TCH
        scores_mm_ev = None
        for tch in range(TCH):
            mm = None
            for dc in range(DC):
                mm = op("T", (lambda tch, dc: lambda t: t.transpose(
                    TRp[:, dc * 128:(dc + 1) * 128],
                    xn_tm[:, tch * 1024 + dc * 128: tch * 1024 + (dc + 1) * 128],
                    eye_b[:]))(tch, dc),
                    [xn_ev[tch], st_ev["trp_free"], ev_initP] if dc == 0 else [],
                    "T" if dc == DC - 1 else None)
            ev1 = op("V", (lambda tch: lambda v: v.tensor_copy(
                xn_fm[:].rearrange("p (dc t) -> p dc t", t=1024)[:, :, tch * 128:(tch + 1) * 128],
                TRp[:].rearrange("p (dc t) -> p dc t", t=128)))(tch),
                [mm, st_ev["xn_fm_rd"]], "V")
            xn_fm_ev[tch] = ev1
            st_ev["trp_free"] = ev1
            yield
        C.update(xn_fm_ev=xn_fm_ev)
        for tch in range(TCH):
            smm = None
            for kc in range(DC):
                smm = op("T", (lambda tch, kc: lambda t: t.matmul(
                    CBp[:, tch * 16:(tch + 1) * 16],
                    xn_fm[:, kc * 1024 + tch * 128: kc * 1024 + (tch + 1) * 128],
                    gwTs[:, kc * 16:(kc + 1) * 16],
                    start=(kc == 0), stop=(kc == DC - 1)))(tch, kc),
                    [xn_fm_ev[tch], ev_initP] + ([st_ev["cbp_free"]] if tch == 0 else []),
                    "T" if kc == DC - 1 else None)
            scores_mm_ev = smm
        C.update(scores_mm_ev=scores_mm_ev)
        yield

    def emit_R(mt, C):
        warm = (mt == 0)
        Lw = (W if warm else 0) + MT
        u_chunks = ([("w", 0, W)] if warm else []) + \
                   [("m", (W if warm else 0) + 0, 512), ("m", (W if warm else 0) + 512, 512)]
        sc_chunks = []
        c0 = 0
        while c0 < Lw:
            cw = min(512, Lw - c0)
            sc_chunks.append((c0, cw))
            c0 += cw
        x_ev = C["x_ev"]
        xn_ev = C["xn_ev"]
        xn2_fm_ev = C["xn2_fm_ev"]
        xn_fm_ev = C["xn_fm_ev"]
        scores_mm_ev = C["scores_mm_ev"]
        uw_src_ev = C["uw_src_ev"]
        if warm:
            trw_mm = None
            for dc in range(DC):
                trw_mm = op("T", (lambda dc: lambda t: t.transpose(
                    TRp[:, dc * W:(dc + 1) * W],
                    wxn2[0:W, dc * 128:(dc + 1) * 128], eye_b[0:W, 0:W]))(dc),
                    [uw_src_ev, st_ev["trp_free"], ev_initP] if dc == 0 else [],
                    "T" if dc == DC - 1 else None)
            uw_src_ev = op("V", lambda v: v.tensor_copy(xnw2_fm[:], TRp[:, 0:DC * W]),
                           [trw_mm], "V")
            st_ev["trp_free"] = uw_src_ev
        u_evs = []
        ping = 0
        m_write_evs = []
        for kind, c0, cw in u_chunks:
            if kind == "w":
                rhs = xnw2_fm[:, 0:DC * W].rearrange("p (kc t) -> p kc t", t=W)
                src_ev = uw_src_ev
            else:
                t0 = c0 - (W if warm else 0)
                rhs = xn2_fm[:].rearrange("p (kc t) -> p kc t", t=1024)[:, :, t0:t0 + cw]
                src_ev = xn2_fm_ev[-1]
            mm = None
            for kc in range(DC):
                mm = op("T", (lambda rhs, kc, cw, ping: lambda t: t.matmul(
                    Y2[ping][0:16, 0:cw], wsumT[:, kc * 16:(kc + 1) * 16], rhs[:, kc, :],
                    start=(kc == 0), stop=(kc == DC - 1)))(rhs, kc, cw, ping),
                    [src_ev, ev_wsumT, st_ev["y2_free"][ping]] if kc == 0 else [],
                    "T" if kc == DC - 1 else None)
            ev_e = op("S", (lambda c0, cw, ping: lambda s: s.activation(
                M_t[0][32:48, c0:c0 + cw], Y2[ping][0:16, 0:cw], AF.Identity,
                bias=ub_sb[:]))(c0, cw, ping), [mm, ev_ub, st_ev["m_rd"]], "S")
            st_ev["y2_free"][ping] = ev_e
            u_evs.append(ev_e)
            m_write_evs.append(ev_e)
            ping ^= 1
        if warm:
            mask_ev = op("V", lambda v: v.tensor_tensor(
                M_t[0][32:48, 0:W], M_t[0][32:48, 0:W], umask_sb[32:48, :], op=ALU.mult),
                [u_evs[0], ev_initP], "V")
            m_write_evs.append(mask_ev)
        ucopy_ev = op("P", (lambda Lw: lambda g: g.tensor_copy(
            M_t[1][32:48, 0:Lw], M_t[0][32:48, 0:Lw]))(Lw), m_write_evs + [st_ev["m_rd"]], "Pc")

        if warm:
            pass
        else:
            hb = st_ev["hfin_prev"]
            z0 = op("V", lambda v: v.tensor_copy(M_t[0][0:16, 0:1], M_t[0][0:16, 1048:1049]),
                    [hb, st_ev["m_rd"]], "V")
            z1 = op("V", lambda v: v.tensor_copy(M_t[1][0:16, 0:1], M_t[0][0:16, 1048:1049]),
                    [z0], "V")
            m_write_evs += [z0, z1]

        sg_ev = op("S", lambda s: s.activation(scores_sb[:], CBp[:, 0:128], AF.Sigmoid),
                   [scores_mm_ev], "S")
        bi_ev = op("V", lambda v: v.tensor_tensor(biased, scores_sb[:], gb_bt[:], op=ALU.add),
                   [sg_ev, ev_gb, st_ev["xslot"][2], st_ev["xslot_w"][2]], "V")
        m8_ev = None
        for tch in range(TCH):
            m8_ev = op("V", (lambda tch: lambda v: v.max(
                out=m8s[:, tch * 8:(tch + 1) * 8], in_=biased[:, tch * 16:(tch + 1) * 16]))(tch),
                [bi_ev], "V")
        mk_ev = None
        for tch in range(TCH):
            mk_ev = op("V", (lambda tch: lambda v: v.tensor_scalar(
                masks[:, tch * 16:(tch + 1) * 16], biased[:, tch * 16:(tch + 1) * 16],
                m8s[:, tch * 8 + 3: tch * 8 + 4], None, op0=ALU.is_ge))(tch),
                [m8_ev], "V")
        wr_ev = op("V", lambda v: v.tensor_tensor(wraw, masks, scores_sb[:], op=ALU.mult),
                   [mk_ev], "V")
        sw_ev = op("V", lambda v: v.tensor_reduce(
            sumw, wraw.rearrange("p (a b) -> p a b", b=16), axis=AX.X, op=ALU.add),
            [wr_ev], "V")
        rs_ev = op("V", lambda v: v.reciprocal(rsw, sumw), [sw_ev], "V")
        cmb_ev = None
        for tch in range(TCH):
            cmb_ev = op("V", (lambda tch: lambda v: v.tensor_scalar(
                comb_tm[:, tch * 16:(tch + 1) * 16], wraw[:, tch * 16:(tch + 1) * 16],
                rsw[:, tch:tch + 1], 2.5, op0=ALU.mult, op1=ALU.mult))(tch),
                [rs_ev], "V")

        comb_done = [None]

        def emit_comb_tr():
            mm = None
            for tch in range(TCH):
                mm = op("T", (lambda tch: lambda t: t.transpose(
                    TRp[0:16, tch * 128:(tch + 1) * 128],
                    comb_tm[:, tch * 16:(tch + 1) * 16], eye_b[:]))(tch),
                    [cmb_ev, st_ev["trp_free"]] if tch == 0 else [],
                    "T" if tch == TCH - 1 else None)
            ev = op("V", lambda v: v.tensor_copy(comb_fm[:], TRp[0:16, 0:1024]), [mm], "V")
            st_ev["trp_free"] = ev
            comb_done[0] = ev
        st_ev["cbp_free"] = sg_ev
        st_ev["xslot"][2] = cmb_ev
        st_ev["xslot_w"][2] = rs_ev

        def emit_rext(ex):
            b = ex % 2
            ev = dma("Y", f"dCB{b}", cbrow[b][:], comb_fm[ex:ex + 1, :],
                     [comb_done[0], st_ev["cbrow_free"][b]])
            st_ev["rext_ev"][b] = ev

        def emit_cb_half(ex, h):
            b = ex % 2
            mmcb = op("T", (lambda b, h: lambda t: t.matmul(
                CBp[:], ones_b[:], cbrow[b][:, h * 512:(h + 1) * 512],
                start=True, stop=True))(b, h),
                [st_ev["rext_ev"][b], st_ev["cbp_free"]], "T")
            cbe = op("S", (lambda b, h: lambda s: s.activation(
                cb_sb[b][:, h * 512:(h + 1) * 512], CBp[:], AF.Identity))(b, h),
                [mmcb, st_ev["cb_lastread"][b]], "S")
            st_ev["cbp_free"] = cbe
            if h == 0:
                st_ev["cb_evs"][b] = [cbe, None]
            else:
                st_ev["cb_evs"][b][1] = cbe
                st_ev["cbrow_free"][b] = mmcb


        ghosts = [cb_sb[0][:], cb_sb[1][:], xn2c[0][:], xn2c[1][:], t2c[0][:]] + \
                 [hp[0][:, t * 1024:(t + 1) * 1024] for t in range(3)]
        ghost_free = [st_ev["cb_lastread"][0], st_ev["cb_lastread"][1],
                      st_ev["xn2c_free"][0], st_ev["xn2c_free"][1], st_ev["t2c_free"][0],
                      st_ev["hp_free"][0], st_ev["hp_free"][1], st_ev["hp_free"][0]]
        sg_evs = [[None, None] for _ in range(TCH)]
        gate_last_mm = None

        def emit_gate_group(tch, h):
            nonlocal gate_last_mm
            pair = [GA[0], GA[1]] if tch % 2 == 0 else [UP[0], UP[1]]
            pfree = st_ev["ga_free"] if tch % 2 == 0 else st_ev["up_free"]
            mm = None
            for kc in range(DC):
                mm = op("T", (lambda tch, h, kc, pair: lambda t: t.matmul(
                    pair[h][:], xn2_fm[:, kc * 1024 + tch * 128: kc * 1024 + (tch + 1) * 128],
                    w13sb[1][:, kc * 1024 + h * 512: kc * 1024 + (h + 1) * 512],
                    start=(kc == 0), stop=(kc == DC - 1)))(tch, h, kc, pair),
                    ([xn2_fm_ev[tch], st_ev["w13_ld"][1], pfree[h]] if kc == 0 else []),
                    "T" if kc == DC - 1 else None)
            gate_last_mm = mm
            sgm = op("S", (lambda tch, h, pair: lambda s: s.activation(
                ghosts[tch][:, h * 512:(h + 1) * 512], pair[h][:], AF.Sigmoid))(tch, h, pair),
                [mm, ghost_free[tch]], "S")
            pfree[h] = sgm
            sg_evs[tch][h] = sgm

        gate_seq = [(t, hh) for t in range(TCH) for hh in range(2)]
        gi = 0

        nch = len(sc_chunks)
        scan_tanh_ev = {}
        for j in range(1, K_J + 1):
            for _ in range(2):
                if gi < len(gate_seq):
                    emit_gate_group(*gate_seq[gi])
                    gi += 1
            src_t = M_t[(j + 1) % 2]
            dst = M_t[j % 2]
            for ci, (c0, cw) in enumerate(sc_chunks):
                pg = ci % 2
                if j == 1:
                    waits = list(m_write_evs)
                elif j == 2:
                    waits = [scan_tanh_ev[(j - 1, ci)], ucopy_ev]
                    if ci > 0:
                        waits.append(scan_tanh_ev[(j - 1, ci - 1)])
                else:
                    waits = [scan_tanh_ev[(j - 1, ci)]]
                    if ci > 0:
                        waits.append(scan_tanh_ev[(j - 1, ci - 1)])
                waits.append(st_ev["y2_free"][pg])
                mm = op("T", (lambda src_t, c0, cw, pg: lambda t: t.matmul(
                    Y2[pg][0:16, 0:cw], a_sb[:], src_t[0:64, c0:c0 + cw],
                    start=True, stop=True))(src_t, c0, cw, pg),
                    waits + ([ev_initP] if (j, ci) == (1, 0) else []), "T")
                th = op("S", (lambda dst, c0, cw, pg: lambda s: s.activation(
                    dst[0:16, c0 + 1:c0 + 1 + cw], Y2[pg][0:16, 0:cw], AF.Tanh))(dst, c0, cw, pg),
                    [mm], "S")
                st_ev["y2_free"][pg] = th
                scan_tanh_ev[(j, ci)] = th
            if j == 1:
                emit_comb_tr()
        while gi < len(gate_seq):
            emit_gate_group(*gate_seq[gi])
            gi += 1
        hfin = M_t[K_J % 2]
        hfin_ev = [scan_tanh_ev[(K_J, ci)] for ci in range(nch)]
        st_ev["hfin_prev"] = hfin_ev[-1]
        h_off = (W if warm else 0) + 1

        emit_rext(0)

        if mt == 1:
            ld1 = dma("P", "dW13b",
                      w13sb[1][:].rearrange("p (kc m) -> p kc m", m=1024),
                      w13T[0].rearrange("(kc p) m -> p kc m", p=P),
                      [gate_last_mm])
            ld2 = dma("P", "dW2b",
                      w2sb[1][:].rearrange("p (ic m) -> p ic m", m=1024),
                      w2T[0].rearrange("(ic p) m -> p ic m", p=P),
                      [st_ev["w2_rdT"][1]])
            st_ev["w13_ld"][1] = ld1
            st_ev["w2_ld"][1] = ld2
        st_ev["wg_rdT"] = gate_last_mm
        st_ev["xn2_rd"] = gate_last_mm
        C.update(hfin=hfin, hfin_ev=hfin_ev, h_off=h_off, sg_evs=sg_evs,
                 ghosts=ghosts, xn_ev=C["xn_ev"],
                 comb_ev=comb_done[0], emit_rext=emit_rext, emit_cb_half=emit_cb_half)

    def gen_R2(mt, C):
        hfin = C["hfin"]
        hfin_ev = C["hfin_ev"]
        h_off = C["h_off"]
        sg_evs = C["sg_evs"]
        ghosts = C["ghosts"]
        xn_ev = C["xn_ev"]
        s_ev = [None] * TCH
        t2_by_tch = [None] * TCH
        for tch in range(TCH):
            for h in range(2):
                pg = h
                mm = op("T", (lambda tch, h, pg: lambda t: t.matmul(
                    Y2[pg][:], hfin[0:17, h_off + tch * 128: h_off + (tch + 1) * 128],
                    woT[:, h * 512:(h + 1) * 512], start=True, stop=True))(tch, h, pg),
                    [hfin_ev[-1], ev_initP, st_ev["y2_free"][pg]], "T")
                tbuf = xarena[:, 2048 + h * 512: 2048 + (h + 1) * 512]
                t1e = op("S", (lambda tbuf, pg: lambda s: s.activation(
                    tbuf, Y2[pg][:], AF.Identity))(tbuf, pg),
                    [mm, st_ev["xslot_w"][2], st_ev["xslot"][2]], "S")
                st_ev["y2_free"][pg] = t1e
                t2e = op("P", (lambda tbuf, tch, h: lambda g: g.tensor_tensor(
                    tbuf, tbuf, ghosts[tch][:, h * 512:(h + 1) * 512], op=ALU.mult))(tbuf, tch, h),
                    [t1e, sg_evs[tch][h]], "Pc")
                s_e = op("V", (lambda tbuf, tch, h: lambda v: v.tensor_tensor(
                    s_tm[:, tch * 1024 + h * 512: tch * 1024 + (h + 1) * 512],
                    tbuf,
                    xn_tm[:, tch * 1024 + h * 512: tch * 1024 + (h + 1) * 512],
                    op=ALU.add))(tbuf, tch, h),
                    [t2e, xn_ev[tch], st_ev["s_rd"]], "V")
                st_ev["xslot"][2] = s_e
                st_ev["xslot_w"][2] = t2e
            s_ev[tch] = s_e
            t2_by_tch[tch] = t2e
            yield
        st_ev["cb_lastread"][0] = t2_by_tch[0]
        st_ev["cb_lastread"][1] = t2_by_tch[1]
        st_ev["xn2c_free"][0] = t2_by_tch[2]
        st_ev["xn2c_free"][1] = t2_by_tch[3]
        st_ev["t2c_free"][0] = t2_by_tch[4]
        st_ev["t2c_free"][1] = t2_by_tch[4]
        st_ev["hp_free"][0] = t2_by_tch[7]
        st_ev["hp_free"][1] = t2_by_tch[7]
        st_ev["xn_tm_rd"] = s_ev[-1]
        C.update(s_ev=s_ev)
        yield


    def emit_E(mt, C, gen1=None, gen2=None):
        xn_fm_ev = C["xn_fm_ev"]
        comb_ev = C["comb_ev"]
        emit_rext = C["emit_rext"]
        emit_cb_half = C["emit_cb_half"]

        def pump(gen, n):
            if gen is None:
                return
            for _ in range(n):
                if next(gen, "END") == "END":
                    break
        y_acc_ev = [[None, None] for _ in range(TCH)]
        h2_last_by_buf = [None, None]
        st1_lastmm_by_buf = [None, None]
        st2_lastmm_by_buf = [None, None]

        for ex in range(NEXP):
            if ex + 1 < 16:
                emit_rext(ex + 1)
            pump(gen1, 1)
            buf = (ex + mt) % 2
            if ex + 1 < NEXP:
                nbuf = buf ^ 1
                free_w13 = st_ev["wg_rdT"] if (mt == 0 and ex == 0) else st1_lastmm_by_buf[nbuf]
                if free_w13 is None:
                    free_w13 = st_ev["w13_rdT"][nbuf]
                ld1 = dma("P", "dW13a" if nbuf == 0 else "dW13b",
                          w13sb[nbuf][:].rearrange("p (kc m) -> p kc m", m=1024),
                          w13T[ex + 1].rearrange("(kc p) m -> p kc m", p=P),
                          [free_w13])
                ld2 = dma("P", "dW2a" if nbuf == 0 else "dW2b",
                          w2sb[nbuf][:].rearrange("p (ic m) -> p ic m", m=1024),
                          w2T[ex + 1].rearrange("(ic p) m -> p ic m", p=P),
                          [st2_lastmm_by_buf[nbuf] or st_ev["w2_rdT"][nbuf]])
                st_ev["w13_ld"][nbuf] = ld1
                st_ev["w2_ld"][nbuf] = ld2
            elif mt == 0:
                ldwg = dma("P", "dW13b",
                           w13sb[1][:].rearrange("p (kc m) -> p kc m", m=1024),
                           wgT.rearrange("(kc p) m -> p kc m", p=P),
                           [st1_lastmm_by_buf[1]])
                st_ev["w13_ld"][1] = ldwg

            h2_evs = []
            h2_last_half = [None, None]
            gidx = 0
            for h in range(2):
                for i in range(4):
                    if (h, i) == (1, 0):
                        pump(gen1, 1)
                    if (h, i) == (1, 2) and ex < 8:
                        pump(gen1, 1)
                    if ex == 0 and (h, i) == (0, 0):
                        emit_cb_half(0, 0)
                    if ex == 0 and (h, i) == (0, 1):
                        emit_cb_half(0, 1)
                    if ex + 1 < 16 and (h, i) == ((1, 0) if ex == 0 else (0, 1)):
                        emit_cb_half(ex + 1, 0)
                    if ex + 1 < 16 and (h, i) == ((1, 2) if ex == 0 else (0, 2)):
                        emit_cb_half(ex + 1, 1)
                    s1 = gidx % 2
                    gidx += 1
                    mm = None
                    for kc in range(DC):
                        mm = op("T", (lambda buf, i, h, kc, s1: lambda t: t.matmul(
                            GA[s1][:],
                            w13sb[buf][:, kc * 1024 + i * 128: kc * 1024 + (i + 1) * 128],
                            xn_fm[:, kc * 1024 + h * 512: kc * 1024 + (h + 1) * 512],
                            start=(kc == 0), stop=(kc == DC - 1)))(buf, i, h, kc, s1),
                            ([st_ev["w13_ld"][buf], xn_fm_ev[-1], st_ev["ga_free"][s1]]
                             if kc == 0 else []),
                            "T" if kc == DC - 1 else None)
                    gate_mm = mm
                    mm = None
                    for kc in range(DC):
                        mm = op("T", (lambda buf, i, h, kc, s1: lambda t: t.matmul(
                            UP[s1][:],
                            w13sb[buf][:, kc * 1024 + (i + 4) * 128: kc * 1024 + (i + 5) * 128],
                            xn_fm[:, kc * 1024 + h * 512: kc * 1024 + (h + 1) * 512],
                            start=(kc == 0), stop=(kc == DC - 1)))(buf, i, h, kc, s1),
                            ([st_ev["up_free"][s1]] if kc == 0 else []),
                            "T" if kc == DC - 1 else None)
                    up_mm = mm
                    st1_lastmm_by_buf[buf] = up_mm
                    sil = op("S", (lambda s1: lambda s: s.activation(
                        silu_sb[s1][:], GA[s1][:], AF.Silu))(s1),
                        [gate_mm, st_ev["silu_free"][s1]], "S")
                    st_ev["ga_free"][s1] = sil
                    h1e = op("V", (lambda s1: lambda v: v.tensor_tensor(
                        h1b[s1][:], silu_sb[s1][:], UP[s1][:], op=ALU.mult))(s1),
                        [sil, up_mm, st_ev["h1_free"][s1]], "V")
                    st_ev["up_free"][s1] = h1e
                    st_ev["silu_free"][s1] = h1e
                    if ex < 16:
                        h2e = op("P", (lambda ex, buf, i, h, s1: lambda g: g.tensor_tensor(
                            hp[buf][:, i * 1024 + h * 512: i * 1024 + (h + 1) * 512],
                            h1b[s1][:], cb_sb[ex % 2][:, h * 512:(h + 1) * 512],
                            op=ALU.mult))(ex, buf, i, h, s1),
                            [h1e, st_ev["cb_evs"][ex % 2][h], st_ev["hp_free"][0],
                             st_ev["hp_free"][1]], "Pc")
                        st_ev["cb_lastread"][ex % 2] = h2e
                    else:
                        h2e = op("P", (lambda buf, i, h, s1: lambda g: g.tensor_copy(
                            hp[buf][:, i * 1024 + h * 512: i * 1024 + (h + 1) * 512],
                            h1b[s1][:]))(buf, i, h, s1),
                            [h1e, st_ev["hp_free"][0], st_ev["hp_free"][1]], "Pc")
                    st_ev["h1_free"][s1] = h2e
                    h2_evs.append(h2e)
                    h2_last_half[h] = h2e
            h2_last_by_buf[buf] = h2_evs[-1]

            if ex == NEXP - 1:
                pump(gen1, 99)
                st_ev["xn_fm_rd"] = st1_lastmm_by_buf[buf]
            st2_slots = [Y2[0], Y2[1], CBp]
            nslot = 2 if ex == NEXP - 1 else 3
            for tch in range(TCH):
                if ex == NEXP - 1:
                    pump(gen2, 1)
                for dh in range(2):
                    s2 = (tch * 2 + dh) % nslot
                    ps = st2_slots[s2]
                    free_ev = st_ev["y2_free"][s2] if s2 < 2 else st_ev["cbp_free"]
                    mm = None
                    for ic in range(4):
                        mm = op("T", (lambda buf, tch, dh, ic, ps: lambda t: t.matmul(
                            ps[:],
                            hp[buf][:, ic * 1024 + tch * 128: ic * 1024 + (tch + 1) * 128],
                            w2sb[buf][:, ic * 1024 + dh * 512: ic * 1024 + (dh + 1) * 512],
                            start=(ic == 0), stop=(ic == 3)))(buf, tch, dh, ic, ps),
                            ([h2_last_half[tch // 4], st_ev["w2_ld"][buf],
                              free_ev] if ic == 0 else []),
                            "T" if ic == 3 else None)
                    st2_lastmm_by_buf[buf] = mm
                    if ex == 0:
                        acc = op("V", (lambda tch, dh, ps: lambda v: v.tensor_copy(
                            y_sb[:, tch * 1024 + dh * 512: tch * 1024 + (dh + 1) * 512],
                            ps[:]))(tch, dh, ps),
                            [mm, st_ev["y_free"]], "V")
                    else:
                        acc = op("V", (lambda tch, dh, ps: lambda v: v.tensor_tensor(
                            y_sb[:, tch * 1024 + dh * 512: tch * 1024 + (dh + 1) * 512],
                            y_sb[:, tch * 1024 + dh * 512: tch * 1024 + (dh + 1) * 512],
                            ps[:], op=ALU.add))(tch, dh, ps),
                            [mm, y_acc_ev[tch][dh]], "V")
                    y_acc_ev[tch][dh] = acc
                    if s2 < 2:
                        st_ev["y2_free"][s2] = acc
                    else:
                        st_ev["cbp_free"] = acc
            st_ev["hp_free"][0] = st2_lastmm_by_buf[buf]
            st_ev["hp_free"][1] = st2_lastmm_by_buf[buf]
            st_ev["w13_rdT"][buf] = st1_lastmm_by_buf[buf]
            st_ev["w2_rdT"][buf] = st2_lastmm_by_buf[buf]

        pump(gen2, 99)
        st_ev["xn_fm_rd"] = st1_lastmm_by_buf[(16 + mt) % 2]
        st_ev["m_rd"] = st2_lastmm_by_buf[(16 + mt) % 2]
        C.update(y_acc_ev=y_acc_ev)

    def emit_tail(mt, C):
        y_acc_ev = C["y_acc_ev"]
        s_ev = C["s_ev"]
        out_dma_evs = []
        gx = [hp[0][:, 3072:4096], fbuf[0][:]]
        gx_free = [st_ev["hp_free"][0], st_ev["fbuf_free"][0]]
        f1_prev = None
        f2m2 = [None, None]
        for tch in range(TCH):
            b = tch % 2
            ge = op("S", (lambda tch, b: lambda s: s.activation(
                gx[b], y_sb[:, tch * 1024:(tch + 1) * 1024], AF.Silu))(tch, b),
                [y_acc_ev[tch][0], y_acc_ev[tch][1], gx_free[b], f2m2[b]], "S")
            slot = tch % 3
            xe = dma("Y", f"dX{slot}",
                     xarena[:, slot * 1024:(slot + 1) * 1024],
                     xh[xrow0(mt) + tch * P: xrow0(mt) + (tch + 1) * P, :],
                     [st_ev["xslot"][slot], st_ev["xslot_w"][slot]])
            st_ev["xslot_w"][slot] = xe
            f1 = op("V", (lambda tch, b: lambda v: v.tensor_tensor(
                gx[b], s_tm[:, tch * 1024:(tch + 1) * 1024], gx[b], op=ALU.mult))(tch, b),
                [ge, s_ev[tch], st_ev["s_rd"]], "V")
            f1_prev = f1
            f2 = op("P", (lambda tch, slot, b: lambda g: g.tensor_tensor(
                y_sb[:, tch * 1024:(tch + 1) * 1024],
                gx[b], xarena[:, slot * 1024:(slot + 1) * 1024],
                op=ALU.add))(tch, slot, b),
                [f1, xe, ge], "Pc")
            f2m2[b] = f2
            st_ev["xslot"][slot] = f2
            ode = dma("P", "dO", out_d[mt * MT + tch * P: mt * MT + (tch + 1) * P, :],
                      y_sb[:, tch * 1024:(tch + 1) * 1024], [f2])
            out_dma_evs.append(ode)
        st_ev["y_free"] = out_dma_evs[-1]
        st_ev["hp_free"][0] = f2m2[0]
        st_ev["hp_free"][1] = f2m2[0]
        st_ev["fbuf_free"][0] = f2m2[1]
        st_ev["fbuf_free"][1] = f2m2[1]
        st_ev["s_rd"] = f1_prev

    import itertools
    C0 = {}
    for _ in gen_A(0, C0):
        pass
    for _ in gen_B1(0, C0):
        pass
    for _ in gen_B2(0, C0):
        pass
    emit_R(0, C0)
    for _ in gen_R2(0, C0):
        pass
    if n_mts > 1:
        C1 = {}
        g1 = itertools.chain(gen_A(1, C1), gen_B1(1, C1))
        g2 = gen_B2(1, C1)
        emit_E(0, C0, gen1=g1, gen2=g2)
        emit_R(1, C1)
        emit_tail(0, C0)
        for _ in gen_R2(1, C1):
            pass
        emit_E(1, C1)
        emit_tail(1, C1)
    else:
        for _ in gen_R2(0, C0):
            pass
        emit_E(0, C0)
        emit_tail(0, C0)

    if dbg_taps:
        allev = [("T", cnt["T"]), ("S", cnt["S"]), ("V", cnt["V"]), ("Pc", cnt["Pc"])]
        tensors = dict(xn_tm=xn_tm, xn_fm=xn_fm, xn2_fm=xn2_fm, s_tm=s_tm, y_sb=y_sb,
                       comb_tm=comb_tm, comb_fm=comb_fm, scores_sb=scores_sb, M0=M_t[0],
                       M1=M_t[1], stats=stats, biased=biased, m8s=m8s, wraw=wraw,
                       xnw2_fm=xnw2_fm, hp0=hp[0], hp1=hp[1], cb_sb=cb_sb)
        for name, shape, dt in dbg_taps:
            ap = tensors[name]
            dma("Y", "dDBG", dbg_drams[name], ap[0:shape[0], 0:shape[1]], allev)

    n_out = cnt["dO"]
    ops["Y"].append(([("dO", n_out), ("dDBG", cnt["dDBG"])], None, None, 0))
    ops["V"].append(([("dO", n_out), ("dDBG", cnt["dDBG"])], None, None, 0))

    with nc.Block() as block:
        def emit_stream(eng, stream):
            waited = {}
            for waits, fn, inc, inck in ops[stream]:
                for wv in waits:
                    if wv is None:
                        continue
                    sname, val = wv
                    if val <= 0 or waited.get(sname, 0) >= val:
                        continue
                    eng.wait_ge(sems[sname], val)
                    waited[sname] = val
                if fn is None:
                    continue
                inst = fn(eng)
                if inc is not None:
                    inst.then_inc(sems[inc], inck)

        block.tensor(lambda t: emit_stream(t, "T"))
        block.scalar(lambda s: emit_stream(s, "S"))
        block.vector(lambda v: emit_stream(v, "V"))
        block.gpsimd(lambda g: emit_stream(g, "P"))
        block.sync(lambda s: emit_stream(s, "Y"))

    ctx.close()
    return nc


_NC_CACHE = {}


def _get_nc():
    if "nc" not in _NC_CACHE:
        _NC_CACHE["nc"] = build_program()
    return _NC_CACHE["nc"]


def make_in_maps(L0_attn_x, L0_attn_Ws, L0_attn_bs, L0_attn_Wi, L0_attn_bi,
                 L0_attn_Wo, L0_attn_bo, L0_attn_Wg, L0_attn_bg, L0_attn_ln_w,
                 L0_attn_ln_b, L0_attn_A, L0_ffn_gate_w, L0_ffn_gate_b,
                 L0_ffn_w13, L0_ffn_w2, L0_ffn_w13_s, L0_ffn_w2_s):
    f = np.float32
    x = np.asarray(L0_attn_x, f)
    w13T = np.concatenate([np.transpose(np.asarray(L0_ffn_w13, f), (0, 2, 1)),
                           np.asarray(L0_ffn_w13_s, f).T[None]], 0)
    w13T = np.ascontiguousarray(w13T)
    w2T = np.concatenate([np.transpose(np.asarray(L0_ffn_w2, f), (0, 2, 1)),
                          np.asarray(L0_ffn_w2_s, f).T[None]], 0)
    w2T = np.ascontiguousarray(w2T)
    wgT = np.ascontiguousarray(np.asarray(L0_attn_Wg, f).T)
    woT = np.ascontiguousarray(np.concatenate(
        [np.asarray(L0_attn_Wo, f).T,
         np.asarray(L0_attn_bo, f)[None]], 0))
    wsT = np.ascontiguousarray(np.asarray(L0_attn_Ws, f).T)
    wiT = np.ascontiguousarray(np.asarray(L0_attn_Wi, f).T)
    gwT = np.ascontiguousarray(np.asarray(L0_ffn_gate_w, f).T)
    a_st = np.zeros((64, 16), f)
    a_st[0:16, :] = np.asarray(L0_attn_A, f).T
    a_st[32:48, :] = np.eye(16, dtype=f)
    bsbi = np.stack([np.asarray(L0_attn_bs, f), np.asarray(L0_attn_bi, f)], 1)
    gb8 = np.tile(np.asarray(L0_ffn_gate_b, f), 8)[None]
    lnw_r = np.asarray(L0_attn_ln_w, f)[None]
    lnb_r = np.asarray(L0_attn_ln_b, f)[None]
    eye_f = np.eye(P, dtype=f)
    ones_f = np.ones((1, P), f)

    in_maps = []
    for core in range(8):
        b = core // 4
        q = core % 4
        t0 = q * NTOK
        xs = np.empty((W + NTOK, D), f)
        if q == 0:
            xs[0:W] = 0.0
            um = np.zeros((16, W), f)
        else:
            xs[0:W] = x[b, t0 - W:t0]
            um = np.ones((16, W), f)
        xs[W:] = x[b, t0:t0 + NTOK]
        in_maps.append({
            "xh": xs, "w13T": w13T, "w2T": w2T, "wgT": wgT, "woT": woT,
            "wsT": wsT, "wiT": wiT, "gwT": gwT, "a_st": a_st, "bsbi": bsbi,
            "gb8": gb8, "lnw_r": lnw_r, "lnb_r": lnb_r,
            "m1r": np.ones((1, 1056), f),
            "eye_f": eye_f, "ones_f": ones_f, "umask": um,
        })
    return in_maps


def kernel(L0_attn_x, L0_attn_Ws, L0_attn_bs, L0_attn_Wi, L0_attn_bi,
           L0_attn_Wo, L0_attn_bo, L0_attn_Wg, L0_attn_bg, L0_attn_ln_w,
           L0_attn_ln_b, L0_attn_A, L0_ffn_input_ids, L0_ffn_gate_w,
           L0_ffn_gate_b, L0_ffn_w13, L0_ffn_w2, L0_ffn_w13_s, L0_ffn_w2_s):
    in_maps = make_in_maps(L0_attn_x, L0_attn_Ws, L0_attn_bs, L0_attn_Wi,
                           L0_attn_bi, L0_attn_Wo, L0_attn_bo, L0_attn_Wg,
                           L0_attn_bg, L0_attn_ln_w, L0_attn_ln_b, L0_attn_A,
                           L0_ffn_gate_w, L0_ffn_gate_b, L0_ffn_w13,
                           L0_ffn_w2, L0_ffn_w13_s, L0_ffn_w2_s)
    nc = _get_nc()
    res = run_bass_kernel_spmd(nc, in_maps, core_ids=list(range(8)))
    out = np.empty((2, 8192, 1024), np.float32)
    for core in range(8):
        b = core // 4
        q = core % 4
        out[b, q * NTOK:(q + 1) * NTOK] = res.results[core]["out"]
    return out


# version 21
# speedup vs baseline: 1.0412x; 1.0412x over previous
import numpy as np
import ml_dtypes
from contextlib import ExitStack

import concourse.bass as bass
import concourse.mybir as mybir
from concourse.bass_utils import run_bass_kernel_spmd

F32 = mybir.dt.float32
BF16 = mybir.dt.bfloat16
ALU = mybir.AluOpType
AF = mybir.ActivationFunctionType
AX = mybir.AxisListType

P = 128
D = 1024
DC = 8
NTOK = 2048
MT = 1024
TCH = 8
NEXP = 17
W = 24
K_J = 8
EPS = 1e-5


def build_program(n_mts=2, dbg_taps=None):
    nc = bass.Bass("TRN2", target_bir_lowering=False)

    xh = nc.dram_tensor("xh", [W + NTOK, D], F32, kind="ExternalInput").ap()
    w13T = nc.dram_tensor("w13T", [NEXP, D, D], F32, kind="ExternalInput").ap()
    w2T = nc.dram_tensor("w2T", [NEXP, 512, D], F32, kind="ExternalInput").ap()
    wgT = nc.dram_tensor("wgT", [D, D], F32, kind="ExternalInput").ap()
    woT_d = nc.dram_tensor("woT", [17, D], F32, kind="ExternalInput").ap()
    wsT = nc.dram_tensor("wsT", [D, 16], F32, kind="ExternalInput").ap()
    wiT = nc.dram_tensor("wiT", [D, 16], F32, kind="ExternalInput").ap()
    gwT = nc.dram_tensor("gwT", [D, 16], F32, kind="ExternalInput").ap()
    a_st = nc.dram_tensor("a_st", [64, 16], F32, kind="ExternalInput").ap()
    bsbi = nc.dram_tensor("bsbi", [16, 2], F32, kind="ExternalInput").ap()
    gb8 = nc.dram_tensor("gb8", [1, 128], F32, kind="ExternalInput").ap()
    lnwf_d = nc.dram_tensor("lnwf", [P, D], F32, kind="ExternalInput").ap()
    lnbf_d = nc.dram_tensor("lnbf", [P, D], F32, kind="ExternalInput").ap()
    m1r = nc.dram_tensor("m1r", [1, 1056], F32, kind="ExternalInput").ap()
    eye_f = nc.dram_tensor("eye_f", [P, P], F32, kind="ExternalInput").ap()
    ones_f = nc.dram_tensor("ones_f", [1, P], F32, kind="ExternalInput").ap()
    umask_d = nc.dram_tensor("umask", [16, W], F32, kind="ExternalInput").ap()
    out_d = nc.dram_tensor("out", [NTOK, D], F32, kind="ExternalOutput").ap()

    dbg_drams = {}
    if dbg_taps:
        for name, shape, dt in dbg_taps:
            dbg_drams[name] = nc.dram_tensor("dbg_" + name, shape, dt, kind="ExternalOutput").ap()

    ctx = ExitStack()
    e = ctx.enter_context

    xarena = e(nc.sbuf_tensor("xarena", [P, 3072], F32))
    xn_tm = e(nc.sbuf_tensor("xn_tm", [P, 8192], BF16))
    xn_fm = e(nc.sbuf_tensor("xn_fm", [P, 8192], BF16))
    xn2_fm = e(nc.sbuf_tensor("xn2_fm", [P, 8192], BF16))
    s_tm = e(nc.sbuf_tensor("s_tm", [P, 8192], BF16))
    y_sb = e(nc.sbuf_tensor("y_sb", [P, 8192], F32))
    w13sb = [e(nc.sbuf_tensor(f"w13sb{i}", [P, 8192], BF16)) for i in range(2)]
    w2sb = [e(nc.sbuf_tensor(f"w2sb{i}", [P, 4096], BF16)) for i in range(2)]
    hp0 = e(nc.sbuf_tensor("hp0", [P, 4096], BF16))
    hp = [hp0, hp0]
    gbuf0 = e(nc.sbuf_tensor("gbuf0", [P, 1024], BF16))
    gbuf = [gbuf0, gbuf0]
    fbuf0 = e(nc.sbuf_tensor("fbuf0", [P, 1024], BF16))
    fbuf = [fbuf0, fbuf0]
    silu_sb = [e(nc.sbuf_tensor(f"silu{i}", [P, 512], BF16)) for i in range(2)]
    h1b = [e(nc.sbuf_tensor(f"h1b{i}", [P, 512], BF16)) for i in range(2)]
    cb_sb = [e(nc.sbuf_tensor(f"cb_sb{i}", [P, 1024], BF16)) for i in range(2)]
    xn2c = [e(nc.sbuf_tensor(f"xn2c{i}", [P, 1024], BF16)) for i in range(2)]
    t2c0 = e(nc.sbuf_tensor("t2c0", [P, 1024], BF16))
    t2c = [t2c0, t2c0]
    M_t = [e(nc.sbuf_tensor(f"M{i}", [64, 1056], BF16)) for i in range(2)]
    woT = e(nc.sbuf_tensor("woTs", [17, 1024], BF16))
    gwTs = e(nc.sbuf_tensor("gwTs", [P, 128], BF16))
    wsumT = e(nc.sbuf_tensor("wsumT", [P, 128], BF16))
    wsf = xarena[:, 0:128]
    wif = xarena[:, 128:256]
    wsumf = xarena[:, 256:384]
    a_sb = e(nc.sbuf_tensor("a_sb", [64, 16], BF16))
    eye_b = e(nc.sbuf_tensor("eye_b", [P, P], BF16))
    ones_b = e(nc.sbuf_tensor("ones_b", [1, P], BF16))
    umask_sb = e(nc.sbuf_tensor("umask_sb", [48, W], BF16))
    lnwf_bt = e(nc.sbuf_tensor("lnwf_bt", [P, D], BF16))
    lnbf_bt = e(nc.sbuf_tensor("lnbf_bt", [P, D], BF16))
    gb_bt = e(nc.sbuf_tensor("gb_bt", [P, 128], F32))
    lnw_row = e(nc.sbuf_tensor("lnw_row", [1, D], BF16))
    lnb_row = e(nc.sbuf_tensor("lnb_row", [1, D], BF16))
    cbrow = [lnw_row, lnb_row]
    gb_row = e(nc.sbuf_tensor("gb_row", [1, 128], BF16))
    bsbi_sb = e(nc.sbuf_tensor("bsbi_sb", [16, 2], F32))
    epsc = e(nc.sbuf_tensor("epsc", [P, 1], F32))
    ub_sb = e(nc.sbuf_tensor("ub_sb", [16, 1], F32))
    stats = e(nc.sbuf_tensor("stats", [P, 128], F32))
    wstat = e(nc.sbuf_tensor("wstat", [32, 32], F32))
    wxn = s_tm[0:32, 0:1024]
    wxn2 = s_tm[0:32, 1024:2048]
    wt2 = s_tm[0:32, 2048:3072]
    xnw2_fm = e(nc.sbuf_tensor("xnw2_fm", [P, DC * W], BF16))
    scores_sb = e(nc.sbuf_tensor("scores_sb", [P, 128], F32))
    biased = xarena[:, 2048:2176]
    m8s = xarena[:, 2432:2496]
    masks = xarena[:, 2176:2304]
    wraw = xarena[:, 2304:2432]
    sumw = xarena[:, 2496:2504]
    rsw = xarena[:, 2504:2512]
    comb_tm = e(nc.sbuf_tensor("comb_tm", [P, 128], BF16))
    comb_fm = e(nc.sbuf_tensor("comb_fm", [16, 1024], BF16))

    SUMS, SQS, MEANB, MSQB, NVAR, STDB, RSTD, NMR = range(8)
    SUMS2, SQS2, MEAN2, MSQ2, NVAR2, STD2, RSTD2, NMR2 = range(8, 16)

    def st(i):
        return stats[:, i * 8:(i + 1) * 8]

    def stc(i, tch):
        return stats[:, i * 8 + tch: i * 8 + tch + 1]

    GA = [e(nc.psum_tensor(f"GA{i}", [P, 512], F32)) for i in range(2)]
    UP = [e(nc.psum_tensor(f"UP{i}", [P, 512], F32)) for i in range(2)]
    Y2 = [e(nc.psum_tensor(f"Y2{i}", [P, 512], F32)) for i in range(2)]
    CBp = e(nc.psum_tensor("CBp", [P, 512], F32))
    TRp = e(nc.psum_tensor("TRp", [P, 1024], BF16))

    sems = {}
    cnt = {}
    for name in ("T", "S", "V", "Pc", "dIY", "dIP", "dWX", "dX0", "dX1", "dX2",
                 "dW13a", "dW13b", "dW2a", "dW2b", "dCB0", "dCB1", "dO", "dDBG",
                 "dTR0", "dTR1", "dTR2", "dTR3", "dTR4", "dTR5", "dTR6", "dTR7"):
        sems[name] = nc.alloc_semaphore(name="sem_" + name)
        cnt[name] = 0
    ops = {"T": [], "S": [], "V": [], "P": [], "Y": []}
    ops["C"] = ops["S"]

    def op(stream, emitfn, waits=(), inc=None, inck=1):
        evt = None
        if inc is not None:
            cnt[inc] += inck
            evt = (inc, cnt[inc])
        ops[stream].append((list(waits), emitfn, inc, inck))
        return evt

    def dma(stream, sem_name, out_ap, in_ap, waits=(), transpose=False):
        return op(stream, lambda g: g.dma_start(out_ap, in_ap, transpose=transpose),
                  waits, sem_name, 16)

    def tr_batch(src_tm, dst_fm, tch, waits, rot):
        sname = f"dTR{rot[0] % 8}"
        rot[0] += 1
        ops["C"].append(([(sname, cnt[sname])], None, None, 0))
        ev = None
        for dc in range(DC):
            ev = dma("C", sname,
                     dst_fm[:, dc * 1024 + tch * 128: dc * 1024 + (tch + 1) * 128],
                     src_tm[:, dc * 128:(dc + 1) * 128],
                     waits if dc == 0 else [], transpose=True)
        return ev

    ld_wsf = dma("Y", "dIY", wsf.rearrange("p (kc n) -> p kc n", n=16),
                 wsT.rearrange("(kc p) n -> p kc n", p=P))
    ld_wif = dma("Y", "dIY", wif.rearrange("p (kc n) -> p kc n", n=16),
                 wiT.rearrange("(kc p) n -> p kc n", p=P))
    ld_bsbi = dma("Y", "dIY", bsbi_sb[:], bsbi)

    ld_a = dma("P", "dIP", a_sb[:], a_st)
    ld_gbr = dma("P", "dIP", gb_row[:], gb8)
    ld_eye = dma("P", "dIP", eye_b[:], eye_f)
    ld_ones = dma("P", "dIP", ones_b[:], ones_f)
    ld_um = dma("P", "dIP", umask_sb[32:48, :], umask_d)
    ld_lnwf = dma("P", "dIP", lnwf_bt[:], lnwf_d)
    ld_lnbf = dma("P", "dIP", lnbf_bt[:], lnbf_d)
    ld_gw = dma("P", "dIP", gwTs[:].rearrange("p (kc n) -> p kc n", n=16),
                gwT.rearrange("(kc p) n -> p kc n", p=P))
    ld_wo = dma("P", "dIP", woT[:], woT_d)
    ld_wg = dma("P", "dW13b", w13sb[1][:].rearrange("p (kc m) -> p kc m", m=1024),
                wgT.rearrange("(kc p) m -> p kc m", p=P))
    ld_w13_0 = dma("P", "dW13a", w13sb[0][:].rearrange("p (kc m) -> p kc m", m=1024),
                   w13T[0].rearrange("(kc p) m -> p kc m", p=P))
    ld_w2_0 = dma("P", "dW2a", w2sb[0][:].rearrange("p (ic m) -> p ic m", m=1024),
                  w2T[0].rearrange("(ic p) m -> p ic m", p=P))

    ev_initY = ("dIY", cnt["dIY"])
    ev_initP = ("dIP", cnt["dIP"])
    ev_eps = op("V", lambda v: v.memset(epsc[:], float(EPS)), [], "V")
    ev_ub = op("V", lambda v: v.tensor_tensor(ub_sb[:], bsbi_sb[:, 0:1], bsbi_sb[:, 1:2], op=ALU.add),
               [ev_initY], "V")
    ev_wsumf = op("V", lambda v: v.tensor_tensor(wsumf, wsf, wif, op=ALU.add),
                  [ev_initY], "V")
    ev_wsumT = op("S", lambda s: s.copy(wsumT[:], wsumf), [ev_wsumf], "S")

    def bc_half(dst_ap, row_ap, waits, prev_free):
        mm = op("T", (lambda: (lambda t: t.matmul(
            CBp[:, 0:row_ap.shape[-1]], ones_b[:], row_ap, start=True, stop=True)))(),
            list(waits) + ([prev_free] if prev_free else []), "T")
        return op("V", (lambda: (lambda v: v.tensor_copy(dst_ap, CBp[:, 0:row_ap.shape[-1]])))(),
                  [mm], "V")

    ev_gb = bc_half(gb_bt[:], gb_row[:], [ev_initP], None)

    ev_mz0 = op("S", lambda s: s.memzero(M_t[0][:]), [], "S")
    ev_mz1 = op("S", lambda s: s.memzero(M_t[1][:]), [], "S")
    dma("P", "dTR0", M_t[0][16:17, :], m1r, [ev_mz0])
    ev_mz1 = dma("P", "dTR0", M_t[1][16:17, :], m1r, [ev_mz1])
    ev_mz1 = ("dTR0", 32)

    st_ev = {
        "xslot": [ev_wsumT, None, None],
        "xslot_w": [None] * 3,
        "trp_free": None,
        "cbp_free": ev_gb,
        "ga_free": [None, None], "up_free": [None, None],
        "y2_free": [None, None],
        "w13_ld": [ld_w13_0, ld_wg],
        "w2_ld": [ld_w2_0, None],
        "w13_rdT": [None, None], "w2_rdT": [None, None],
        "wg_rdT": None,
        "xn_tm_rd": None,
        "xn_fm_rd": None,
        "xn2_rd": None,
        "s_rd": None,
        "y_free": None,
        "silu_free": [None, None],
        "h1_free": [None, None],
        "hp_free": [None, None],
        "cbrow_free": [None, None],
        "cb_evs": [None, None],
        "cb_lastread": [None, None],
        "rext_ev": [None, None],
        "tr_rot": [0],
        "gbuf_free": [None, None],
        "fbuf_free": [None, None],
        "xn2c_free": [None, None],
        "t2c_free": [None, None],
        "m_rd": ev_mz1,
        "stats_free": None,
        "stats_free2": None,
        "hfin_prev": None,
        "out_dma": [],
    }

    def xrow0(mt):
        return W + mt * MT

    def stats_chain(ME, MS, NV, SD, RS, NM, SU, SQ, c0, c1, last_sum, last_sq):
        def sl(i):
            return stats[:, i * 8 + c0: i * 8 + c1]
        m_ev = op("P", lambda g: g.tensor_scalar_mul(sl(ME), sl(SU), -1.0 / D), [last_sum], "Pc")
        q_ev = op("P", lambda g: g.tensor_scalar_mul(sl(MS), sl(SQ), 1.0 / D), [last_sq, m_ev], "Pc")
        m2_ev = op("P", lambda g: g.tensor_tensor(sl(NV), sl(ME), sl(ME), op=ALU.mult),
                   [m_ev, q_ev], "Pc")
        nv_ev = op("P", lambda g: g.tensor_tensor(sl(NV), sl(NV), sl(MS), op=ALU.subtract),
                   [m2_ev], "Pc")
        sd_ev = op("S", lambda s: s.activation(sl(SD), sl(NV), AF.Sqrt,
                                               bias=epsc[:, 0:1], scale=-1.0),
                   [nv_ev, ev_eps], "S")
        rs_ev = op("V", lambda v: v.reciprocal(sl(RS), sl(SD)), [sd_ev], "V")
        nm_ev = op("P", lambda g: g.tensor_tensor(sl(NM), sl(ME), sl(RS), op=ALU.mult),
                   [rs_ev], "Pc")
        return rs_ev, nm_ev

    def gen_A(mt, C):
        warm = (mt == 0)
        x_ev = [None] * TCH

        def issue_x(tch, extra):
            slot = tch % 3
            x_ev[tch] = dma("Y", f"dX{slot}",
                            xarena[:, slot * 1024:(slot + 1) * 1024],
                            xh[xrow0(mt) + tch * P: xrow0(mt) + (tch + 1) * P, :],
                            [st_ev["xslot"][slot], st_ev["xslot_w"][slot]] + extra)
            st_ev["xslot_w"][slot] = x_ev[tch]
            st_ev["xslot"][slot] = x_ev[tch]

        for t0 in range(3):
            issue_x(t0, [])
        if warm:
            wx_ev = dma("Y", "dWX", y_sb[0:W, 7168:8192], xh[0:W, :], [st_ev["y_free"]])
        yield
        wstate = {"uw": None}

        def gen_warm():
            wxs = y_sb[0:W, 7168:8192]
            wscr = y_sb[0:W, 6144:7168]
            a1 = op("V", lambda v: v.tensor_reduce(wstat[0:W, 0:1], wxs, axis=AX.X, op=ALU.add),
                    [wx_ev], "V")
            yield
            a3 = op("S", lambda s: s.activation(wscr, wxs, AF.Square, accum_out=wstat[0:W, 1:2]),
                    [wx_ev], "S")
            yield
            a4 = op("P", lambda g: g.tensor_scalar_mul(wstat[0:W, 2:3], wstat[0:W, 0:1], -1.0 / D),
                    [a1], "Pc")
            yield
            a5 = op("P", lambda g: g.tensor_scalar_mul(wstat[0:W, 3:4], wstat[0:W, 1:2], 1.0 / D),
                    [a3, a4], "Pc")
            yield
            a6 = op("P", lambda g: g.tensor_tensor(wstat[0:W, 4:5], wstat[0:W, 2:3],
                                                   wstat[0:W, 2:3], op=ALU.mult), [a5], "Pc")
            yield
            a7 = op("P", lambda g: g.tensor_tensor(wstat[0:W, 4:5], wstat[0:W, 4:5],
                                                   wstat[0:W, 3:4], op=ALU.subtract), [a6], "Pc")
            yield
            a8 = op("S", lambda s: s.activation(wstat[0:W, 5:6], wstat[0:W, 4:5], AF.Sqrt,
                                                bias=epsc[0:W, 0:1], scale=-1.0), [a7, ev_eps], "S")
            yield
            a9 = op("V", lambda v: v.reciprocal(wstat[0:W, 6:7], wstat[0:W, 5:6]), [a8], "V")
            yield
            a10 = op("P", lambda g: g.tensor_tensor(wstat[0:W, 7:8], wstat[0:W, 2:3],
                                                    wstat[0:W, 6:7], op=ALU.mult), [a9], "Pc")
            yield
            wxn_ev = op("S", lambda s: s.activation(wxn[0:W, :], wxs, AF.Identity,
                                                    bias=wstat[0:W, 7:8], scale=wstat[0:W, 6:7]),
                        [a10], "S")
            yield
            b1 = op("V", lambda v: v.tensor_reduce(wstat[0:W, 8:9], wxn[0:W, :], axis=AX.X,
                                                   op=ALU.add), [wxn_ev], "V")
            yield
            b3 = op("S", lambda s: s.activation(wscr, wxn[0:W, :], AF.Square,
                                                accum_out=wstat[0:W, 9:10]), [wxn_ev], "S")
            yield
            b4 = op("P", lambda g: g.tensor_scalar_mul(wstat[0:W, 10:11], wstat[0:W, 8:9],
                                                       -1.0 / D), [b1], "Pc")
            yield
            b5 = op("P", lambda g: g.tensor_scalar_mul(wstat[0:W, 11:12], wstat[0:W, 9:10],
                                                       1.0 / D), [b3, b4], "Pc")
            yield
            b6 = op("P", lambda g: g.tensor_tensor(wstat[0:W, 12:13], wstat[0:W, 10:11],
                                                   wstat[0:W, 10:11], op=ALU.mult), [b5], "Pc")
            yield
            b7 = op("P", lambda g: g.tensor_tensor(wstat[0:W, 12:13], wstat[0:W, 12:13],
                                                   wstat[0:W, 11:12], op=ALU.subtract), [b6], "Pc")
            yield
            b8 = op("S", lambda s: s.activation(wstat[0:W, 13:14], wstat[0:W, 12:13], AF.Sqrt,
                                                bias=epsc[0:W, 0:1], scale=-1.0), [b7, ev_eps], "S")
            yield
            b9 = op("V", lambda v: v.reciprocal(wstat[0:W, 14:15], wstat[0:W, 13:14]), [b8], "V")
            yield
            b10 = op("P", lambda g: g.tensor_tensor(wstat[0:W, 15:16], wstat[0:W, 10:11],
                                                    wstat[0:W, 14:15], op=ALU.mult), [b9], "Pc")
            yield
            wt1_ev = op("S", lambda s: s.activation(wxn2[0:W, :], wxn[0:W, :], AF.Identity,
                                                    bias=wstat[0:W, 15:16], scale=wstat[0:W, 14:15]),
                        [b10], "S")
            yield
            wstate["wxn2"] = wt1_ev
            yield

        wg_gen = gen_warm() if warm else iter(())

        def pump(gen, n):
            for _ in range(n):
                if next(gen, "END") == "END":
                    break

        sum_ev = [None] * TCH
        sqs_ev = [None] * TCH
        xn_ev = [None] * TCH
        for g in range(4):
            for tch in (2 * g, 2 * g + 1):
                xs = xarena[:, (tch % 3) * 1024:((tch % 3) + 1) * 1024]
                scr = xn_tm[:, tch * 1024:(tch + 1) * 1024]
                sum_ev[tch] = op("V", (lambda xs, tch: lambda v: v.tensor_reduce(
                    stc(SUMS, tch), xs, axis=AX.X, op=ALU.add))(xs, tch),
                    [x_ev[tch], st_ev["stats_free"], st_ev["stats_free2"]], "V")
                sqs_ev[tch] = op("S", (lambda xs, scr, tch: lambda s: s.activation(
                    scr, xs, AF.Square, accum_out=stc(SQS, tch)))(xs, scr, tch),
                    [x_ev[tch], st_ev["xn_tm_rd"], st_ev["stats_free"],
                     st_ev["stats_free2"]], "S")
            rs1_ev, nm1_ev = stats_chain(MEANB, MSQB, NVAR, STDB, RSTD, NMR, SUMS, SQS,
                                         2 * g, 2 * g + 2,
                                         sum_ev[2 * g + 1], sqs_ev[2 * g + 1])
            for tch in (2 * g, 2 * g + 1):
                xs = xarena[:, (tch % 3) * 1024:((tch % 3) + 1) * 1024]
                xn_ev[tch] = op("S", (lambda xs, tch: lambda s: s.activation(
                    xn_tm[:, tch * 1024:(tch + 1) * 1024], xs, AF.Identity,
                    bias=stc(NMR, tch), scale=stc(RSTD, tch)))(xs, tch),
                    [nm1_ev, rs1_ev], "S")
                st_ev["xslot"][tch % 3] = xn_ev[tch]
            for t in (2 * g + 3, 2 * g + 4):
                if t < TCH and x_ev[t] is None:
                    issue_x(t, [xn_ev[t - 3]])
            pump(wg_gen, 7)
            yield

        sum2_ev = [None] * TCH
        sqs2_ev = [None] * TCH
        for tch in range(TCH):
            xs = xn_tm[:, tch * 1024:(tch + 1) * 1024]
            scr = xn2c[tch % 2][:]
            scr_war = sqs2_ev[tch - 2] if tch >= 2 else st_ev["xn2c_free"][tch % 2]
            sum2_ev[tch] = op("V", (lambda xs, tch: lambda v: v.tensor_reduce(
                stc(SUMS2, tch), xs, axis=AX.X, op=ALU.add))(xs, tch), [xn_ev[tch]], "V")
            sqs2_ev[tch] = op("S", (lambda xs, scr, tch: lambda s: s.activation(
                scr, xs, AF.Square, accum_out=stc(SQS2, tch)))(xs, scr, tch),
                [xn_ev[tch], scr_war], "S")
            if tch % 2 == 1:
                pump(wg_gen, 6)
                yield
        rs2_ev, nm2_ev = stats_chain(MEAN2, MSQ2, NVAR2, STD2, RSTD2, NMR2, SUMS2, SQS2,
                                     0, 8, sum2_ev[-1], sqs2_ev[-1])
        pump(wg_gen, 99)
        uw_src_ev = wstate.get("wxn2")

        C.update(x_ev=x_ev, xn_ev=xn_ev, uw_src_ev=uw_src_ev,
                 nm2_ev=nm2_ev, rs2_ev=rs2_ev)
        yield

    def gen_B1(mt, C):
        xn_ev = C["xn_ev"]
        nm2_ev = C["nm2_ev"]
        rs2_ev = C["rs2_ev"]
        xn2_fm_ev = [None] * TCH
        t1_last = None
        for tch in range(TCH):
            c = tch % 2
            t1_ev = op("S", (lambda tch, c: lambda s: s.activation(
                xn2c[c][:], xn_tm[:, tch * 1024:(tch + 1) * 1024], AF.Identity,
                bias=stc(NMR2, tch), scale=stc(RSTD2, tch)))(tch, c),
                [nm2_ev, rs2_ev, st_ev["xn2c_free"][c]], "S")
            t1_last = t1_ev
            mm = None
            for dc in range(DC):
                mm = op("T", (lambda c, dc: lambda t: t.transpose(
                    TRp[:, dc * 128:(dc + 1) * 128],
                    xn2c[c][:, dc * 128:(dc + 1) * 128], eye_b[:]))(c, dc),
                    [t1_ev, st_ev["trp_free"], ev_initP] if dc == 0 else [],
                    "T" if dc == DC - 1 else None)
            ev2 = op("V", (lambda tch: lambda v: v.tensor_tensor(
                xn2_fm[:].rearrange("p (dc t) -> p dc t", t=1024)[:, :, tch * 128:(tch + 1) * 128],
                TRp[:].rearrange("p (dc t) -> p dc t", t=128),
                lnwf_bt[:].rearrange("p (dc t) -> p dc t", t=128),
                op=ALU.mult))(tch),
                [mm, st_ev["xn2_rd"]], "V")
            ev2b = op("P", (lambda tch: lambda g: g.tensor_tensor(
                xn2_fm[:].rearrange("p (dc t) -> p dc t", t=1024)[:, :, tch * 128:(tch + 1) * 128],
                xn2_fm[:].rearrange("p (dc t) -> p dc t", t=1024)[:, :, tch * 128:(tch + 1) * 128],
                lnbf_bt[:].rearrange("p (dc t) -> p dc t", t=128),
                op=ALU.add))(tch),
                [ev2], "Pc")
            xn2_fm_ev[tch] = ev2b
            st_ev["trp_free"] = ev2
            st_ev["xn2c_free"][c] = mm
            yield
        st_ev["stats_free"] = nm2_ev
        st_ev["stats_free2"] = t1_last
        C.update(xn2_fm_ev=xn2_fm_ev)
        yield

    def gen_B2(mt, C):
        xn_ev = C["xn_ev"]
        xn_fm_ev = [None] * TCH
        scores_mm_ev = None
        for tch in range(TCH):
            mm = None
            for dc in range(DC):
                mm = op("T", (lambda tch, dc: lambda t: t.transpose(
                    TRp[:, dc * 128:(dc + 1) * 128],
                    xn_tm[:, tch * 1024 + dc * 128: tch * 1024 + (dc + 1) * 128],
                    eye_b[:]))(tch, dc),
                    [xn_ev[tch], st_ev["trp_free"], ev_initP] if dc == 0 else [],
                    "T" if dc == DC - 1 else None)
            ev1 = op("V", (lambda tch: lambda v: v.tensor_copy(
                xn_fm[:].rearrange("p (dc t) -> p dc t", t=1024)[:, :, tch * 128:(tch + 1) * 128],
                TRp[:].rearrange("p (dc t) -> p dc t", t=128)))(tch),
                [mm, st_ev["xn_fm_rd"]], "V")
            xn_fm_ev[tch] = ev1
            st_ev["trp_free"] = ev1
            yield
        C.update(xn_fm_ev=xn_fm_ev)
        for tch in range(TCH):
            smm = None
            for kc in range(DC):
                smm = op("T", (lambda tch, kc: lambda t: t.matmul(
                    CBp[:, tch * 16:(tch + 1) * 16],
                    xn_fm[:, kc * 1024 + tch * 128: kc * 1024 + (tch + 1) * 128],
                    gwTs[:, kc * 16:(kc + 1) * 16],
                    start=(kc == 0), stop=(kc == DC - 1)))(tch, kc),
                    [xn_fm_ev[tch], ev_initP] + ([st_ev["cbp_free"]] if tch == 0 else []),
                    "T" if kc == DC - 1 else None)
            scores_mm_ev = smm
        C.update(scores_mm_ev=scores_mm_ev)
        yield

    def emit_R(mt, C):
        warm = (mt == 0)
        Lw = (W if warm else 0) + MT
        u_chunks = ([("w", 0, W)] if warm else []) + \
                   [("m", (W if warm else 0) + 0, 512), ("m", (W if warm else 0) + 512, 512)]
        sc_chunks = []
        c0 = 0
        while c0 < Lw:
            cw = min(512, Lw - c0)
            sc_chunks.append((c0, cw))
            c0 += cw
        x_ev = C["x_ev"]
        xn_ev = C["xn_ev"]
        xn2_fm_ev = C["xn2_fm_ev"]
        xn_fm_ev = C["xn_fm_ev"]
        scores_mm_ev = C["scores_mm_ev"]
        uw_src_ev = C["uw_src_ev"]
        if warm:
            trw_mm = None
            for dc in range(DC):
                trw_mm = op("T", (lambda dc: lambda t: t.transpose(
                    TRp[:, dc * W:(dc + 1) * W],
                    wxn2[0:W, dc * 128:(dc + 1) * 128], eye_b[0:W, 0:W]))(dc),
                    [uw_src_ev, st_ev["trp_free"], ev_initP] if dc == 0 else [],
                    "T" if dc == DC - 1 else None)
            uw_m = op("V", lambda v: v.tensor_tensor(
                xnw2_fm[:].rearrange("p (dc w) -> p dc w", w=W),
                TRp[:, 0:DC * W].rearrange("p (dc w) -> p dc w", w=W),
                lnwf_bt[:].rearrange("p (dc t) -> p dc t", t=128)[:, :, 0:W],
                op=ALU.mult), [trw_mm], "V")
            uw_src_ev = op("P", lambda g: g.tensor_tensor(
                xnw2_fm[:].rearrange("p (dc w) -> p dc w", w=W),
                xnw2_fm[:].rearrange("p (dc w) -> p dc w", w=W),
                lnbf_bt[:].rearrange("p (dc t) -> p dc t", t=128)[:, :, 0:W],
                op=ALU.add), [uw_m], "Pc")
            st_ev["trp_free"] = uw_m
        u_evs = []
        ping = 0
        m_write_evs = []
        for kind, c0, cw in u_chunks:
            if kind == "w":
                rhs = xnw2_fm[:, 0:DC * W].rearrange("p (kc t) -> p kc t", t=W)
                src_ev = uw_src_ev
            else:
                t0 = c0 - (W if warm else 0)
                rhs = xn2_fm[:].rearrange("p (kc t) -> p kc t", t=1024)[:, :, t0:t0 + cw]
                src_ev = xn2_fm_ev[-1]
            mm = None
            for kc in range(DC):
                mm = op("T", (lambda rhs, kc, cw, ping: lambda t: t.matmul(
                    Y2[ping][0:16, 0:cw], wsumT[:, kc * 16:(kc + 1) * 16], rhs[:, kc, :],
                    start=(kc == 0), stop=(kc == DC - 1)))(rhs, kc, cw, ping),
                    [src_ev, ev_wsumT, st_ev["y2_free"][ping]] if kc == 0 else [],
                    "T" if kc == DC - 1 else None)
            ev_e = op("S", (lambda c0, cw, ping: lambda s: s.activation(
                M_t[0][32:48, c0:c0 + cw], Y2[ping][0:16, 0:cw], AF.Identity,
                bias=ub_sb[:]))(c0, cw, ping), [mm, ev_ub, st_ev["m_rd"]], "S")
            st_ev["y2_free"][ping] = ev_e
            u_evs.append(ev_e)
            m_write_evs.append(ev_e)
            ping ^= 1
        if warm:
            mask_ev = op("V", lambda v: v.tensor_tensor(
                M_t[0][32:48, 0:W], M_t[0][32:48, 0:W], umask_sb[32:48, :], op=ALU.mult),
                [u_evs[0], ev_initP], "V")
            m_write_evs.append(mask_ev)
        ucopy_ev = op("P", (lambda Lw: lambda g: g.tensor_copy(
            M_t[1][32:48, 0:Lw], M_t[0][32:48, 0:Lw]))(Lw), m_write_evs + [st_ev["m_rd"]], "Pc")

        if warm:
            pass
        else:
            hb = st_ev["hfin_prev"]
            z0 = op("V", lambda v: v.tensor_copy(M_t[0][0:16, 0:1], M_t[0][0:16, 1048:1049]),
                    [hb, st_ev["m_rd"]], "V")
            z1 = op("V", lambda v: v.tensor_copy(M_t[1][0:16, 0:1], M_t[0][0:16, 1048:1049]),
                    [z0], "V")
            m_write_evs += [z0, z1]

        sg_ev = op("S", lambda s: s.activation(scores_sb[:], CBp[:, 0:128], AF.Sigmoid),
                   [scores_mm_ev], "S")
        bi_ev = op("V", lambda v: v.tensor_tensor(biased, scores_sb[:], gb_bt[:], op=ALU.add),
                   [sg_ev, ev_gb, st_ev["xslot"][2], st_ev["xslot_w"][2]], "V")
        m8_ev = None
        for tch in range(TCH):
            m8_ev = op("V", (lambda tch: lambda v: v.max(
                out=m8s[:, tch * 8:(tch + 1) * 8], in_=biased[:, tch * 16:(tch + 1) * 16]))(tch),
                [bi_ev], "V")
        mk_ev = None
        for tch in range(TCH):
            mk_ev = op("V", (lambda tch: lambda v: v.tensor_scalar(
                masks[:, tch * 16:(tch + 1) * 16], biased[:, tch * 16:(tch + 1) * 16],
                m8s[:, tch * 8 + 3: tch * 8 + 4], None, op0=ALU.is_ge))(tch),
                [m8_ev], "V")
        wr_ev = op("V", lambda v: v.tensor_tensor(wraw, masks, scores_sb[:], op=ALU.mult),
                   [mk_ev], "V")
        sw_ev = op("V", lambda v: v.tensor_reduce(
            sumw, wraw.rearrange("p (a b) -> p a b", b=16), axis=AX.X, op=ALU.add),
            [wr_ev], "V")
        rs_ev = op("V", lambda v: v.reciprocal(rsw, sumw), [sw_ev], "V")
        cmb_ev = None
        for tch in range(TCH):
            cmb_ev = op("V", (lambda tch: lambda v: v.tensor_scalar(
                comb_tm[:, tch * 16:(tch + 1) * 16], wraw[:, tch * 16:(tch + 1) * 16],
                rsw[:, tch:tch + 1], 2.5, op0=ALU.mult, op1=ALU.mult))(tch),
                [rs_ev], "V")

        comb_done = [None]

        def emit_comb_tr():
            mm = None
            for tch in range(TCH):
                mm = op("T", (lambda tch: lambda t: t.transpose(
                    TRp[0:16, tch * 128:(tch + 1) * 128],
                    comb_tm[:, tch * 16:(tch + 1) * 16], eye_b[:]))(tch),
                    [cmb_ev, st_ev["trp_free"]] if tch == 0 else [],
                    "T" if tch == TCH - 1 else None)
            ev = op("V", lambda v: v.tensor_copy(comb_fm[:], TRp[0:16, 0:1024]), [mm], "V")
            st_ev["trp_free"] = ev
            comb_done[0] = ev
        st_ev["cbp_free"] = sg_ev
        st_ev["xslot"][2] = cmb_ev
        st_ev["xslot_w"][2] = rs_ev

        def emit_rext(ex):
            b = ex % 2
            ev = dma("Y", f"dCB{b}", cbrow[b][:], comb_fm[ex:ex + 1, :],
                     [comb_done[0], st_ev["cbrow_free"][b]])
            st_ev["rext_ev"][b] = ev

        def emit_cb_half(ex, h):
            b = ex % 2
            mmcb = op("T", (lambda b, h: lambda t: t.matmul(
                CBp[:], ones_b[:], cbrow[b][:, h * 512:(h + 1) * 512],
                start=True, stop=True))(b, h),
                [st_ev["rext_ev"][b], st_ev["cbp_free"]], "T")
            cbe = op("S", (lambda b, h: lambda s: s.activation(
                cb_sb[b][:, h * 512:(h + 1) * 512], CBp[:], AF.Identity))(b, h),
                [mmcb, st_ev["cb_lastread"][b]], "S")
            st_ev["cbp_free"] = cbe
            if h == 0:
                st_ev["cb_evs"][b] = [cbe, None]
            else:
                st_ev["cb_evs"][b][1] = cbe
                st_ev["cbrow_free"][b] = mmcb


        ghosts = [cb_sb[0][:], cb_sb[1][:], xn2c[0][:], xn2c[1][:], t2c[0][:]] + \
                 [hp[0][:, t * 1024:(t + 1) * 1024] for t in range(3)]
        ghost_free = [st_ev["cb_lastread"][0], st_ev["cb_lastread"][1],
                      st_ev["xn2c_free"][0], st_ev["xn2c_free"][1], st_ev["t2c_free"][0],
                      st_ev["hp_free"][0], st_ev["hp_free"][1], st_ev["hp_free"][0]]
        sg_evs = [[None, None] for _ in range(TCH)]
        gate_last_mm = None

        def emit_gate_group(tch, h):
            nonlocal gate_last_mm
            pair = [GA[0], GA[1]] if tch % 2 == 0 else [UP[0], UP[1]]
            pfree = st_ev["ga_free"] if tch % 2 == 0 else st_ev["up_free"]
            mm = None
            for kc in range(DC):
                mm = op("T", (lambda tch, h, kc, pair: lambda t: t.matmul(
                    pair[h][:], xn2_fm[:, kc * 1024 + tch * 128: kc * 1024 + (tch + 1) * 128],
                    w13sb[1][:, kc * 1024 + h * 512: kc * 1024 + (h + 1) * 512],
                    start=(kc == 0), stop=(kc == DC - 1)))(tch, h, kc, pair),
                    ([xn2_fm_ev[tch], st_ev["w13_ld"][1], pfree[h]] if kc == 0 else []),
                    "T" if kc == DC - 1 else None)
            gate_last_mm = mm
            sgm = op("S", (lambda tch, h, pair: lambda s: s.activation(
                ghosts[tch][:, h * 512:(h + 1) * 512], pair[h][:], AF.Sigmoid))(tch, h, pair),
                [mm, ghost_free[tch]], "S")
            pfree[h] = sgm
            sg_evs[tch][h] = sgm

        gate_seq = [(t, hh) for t in range(TCH) for hh in range(2)]
        gi = 0

        nch = len(sc_chunks)
        scan_tanh_ev = {}
        for j in range(1, K_J + 1):
            for _ in range(2):
                if gi < len(gate_seq):
                    emit_gate_group(*gate_seq[gi])
                    gi += 1
            src_t = M_t[(j + 1) % 2]
            dst = M_t[j % 2]
            for ci, (c0, cw) in enumerate(sc_chunks):
                pg = ci % 2
                if j == 1:
                    waits = list(m_write_evs)
                elif j == 2:
                    waits = [scan_tanh_ev[(j - 1, ci)], ucopy_ev]
                    if ci > 0:
                        waits.append(scan_tanh_ev[(j - 1, ci - 1)])
                else:
                    waits = [scan_tanh_ev[(j - 1, ci)]]
                    if ci > 0:
                        waits.append(scan_tanh_ev[(j - 1, ci - 1)])
                waits.append(st_ev["y2_free"][pg])
                mm = op("T", (lambda src_t, c0, cw, pg: lambda t: t.matmul(
                    Y2[pg][0:16, 0:cw], a_sb[:], src_t[0:64, c0:c0 + cw],
                    start=True, stop=True))(src_t, c0, cw, pg),
                    waits + ([ev_initP] if (j, ci) == (1, 0) else []), "T")
                th = op("S", (lambda dst, c0, cw, pg: lambda s: s.activation(
                    dst[0:16, c0 + 1:c0 + 1 + cw], Y2[pg][0:16, 0:cw], AF.Tanh))(dst, c0, cw, pg),
                    [mm], "S")
                st_ev["y2_free"][pg] = th
                scan_tanh_ev[(j, ci)] = th
            if j == 1:
                emit_comb_tr()
        while gi < len(gate_seq):
            emit_gate_group(*gate_seq[gi])
            gi += 1
        hfin = M_t[K_J % 2]
        hfin_ev = [scan_tanh_ev[(K_J, ci)] for ci in range(nch)]
        st_ev["hfin_prev"] = hfin_ev[-1]
        h_off = (W if warm else 0) + 1

        emit_rext(0)

        if mt == 1:
            ld1 = dma("P", "dW13b",
                      w13sb[1][:].rearrange("p (kc m) -> p kc m", m=1024),
                      w13T[0].rearrange("(kc p) m -> p kc m", p=P),
                      [gate_last_mm])
            ld2 = dma("P", "dW2b",
                      w2sb[1][:].rearrange("p (ic m) -> p ic m", m=1024),
                      w2T[0].rearrange("(ic p) m -> p ic m", p=P),
                      [st_ev["w2_rdT"][1]])
            st_ev["w13_ld"][1] = ld1
            st_ev["w2_ld"][1] = ld2
        st_ev["wg_rdT"] = gate_last_mm
        st_ev["xn2_rd"] = gate_last_mm
        C.update(hfin=hfin, hfin_ev=hfin_ev, h_off=h_off, sg_evs=sg_evs,
                 ghosts=ghosts, xn_ev=C["xn_ev"],
                 comb_ev=comb_done[0], emit_rext=emit_rext, emit_cb_half=emit_cb_half)

    def gen_R2(mt, C):
        hfin = C["hfin"]
        hfin_ev = C["hfin_ev"]
        h_off = C["h_off"]
        sg_evs = C["sg_evs"]
        ghosts = C["ghosts"]
        xn_ev = C["xn_ev"]
        s_ev = [None] * TCH
        t2_by_tch = [None] * TCH
        for tch in range(TCH):
            for h in range(2):
                pg = h
                mm = op("T", (lambda tch, h, pg: lambda t: t.matmul(
                    Y2[pg][:], hfin[0:17, h_off + tch * 128: h_off + (tch + 1) * 128],
                    woT[:, h * 512:(h + 1) * 512], start=True, stop=True))(tch, h, pg),
                    [hfin_ev[-1], ev_initP, st_ev["y2_free"][pg]], "T")
                tbuf = xarena[:, 2048 + h * 512: 2048 + (h + 1) * 512]
                t1e = op("S", (lambda tbuf, pg: lambda s: s.activation(
                    tbuf, Y2[pg][:], AF.Identity))(tbuf, pg),
                    [mm, st_ev["xslot_w"][2], st_ev["xslot"][2]], "S")
                st_ev["y2_free"][pg] = t1e
                t2e = op("P", (lambda tbuf, tch, h: lambda g: g.tensor_tensor(
                    tbuf, tbuf, ghosts[tch][:, h * 512:(h + 1) * 512], op=ALU.mult))(tbuf, tch, h),
                    [t1e, sg_evs[tch][h]], "Pc")
                s_e = op("V", (lambda tbuf, tch, h: lambda v: v.tensor_tensor(
                    s_tm[:, tch * 1024 + h * 512: tch * 1024 + (h + 1) * 512],
                    tbuf,
                    xn_tm[:, tch * 1024 + h * 512: tch * 1024 + (h + 1) * 512],
                    op=ALU.add))(tbuf, tch, h),
                    [t2e, xn_ev[tch], st_ev["s_rd"]], "V")
                st_ev["xslot"][2] = s_e
                st_ev["xslot_w"][2] = t2e
            s_ev[tch] = s_e
            t2_by_tch[tch] = t2e
            yield
        st_ev["cb_lastread"][0] = t2_by_tch[0]
        st_ev["cb_lastread"][1] = t2_by_tch[1]
        st_ev["xn2c_free"][0] = t2_by_tch[2]
        st_ev["xn2c_free"][1] = t2_by_tch[3]
        st_ev["t2c_free"][0] = t2_by_tch[4]
        st_ev["t2c_free"][1] = t2_by_tch[4]
        st_ev["hp_free"][0] = t2_by_tch[7]
        st_ev["hp_free"][1] = t2_by_tch[7]
        st_ev["xn_tm_rd"] = s_ev[-1]
        C.update(s_ev=s_ev)
        yield


    def emit_E(mt, C, gen1=None, gen2=None):
        xn_fm_ev = C["xn_fm_ev"]
        comb_ev = C["comb_ev"]
        emit_rext = C["emit_rext"]
        emit_cb_half = C["emit_cb_half"]

        def pump(gen, n):
            if gen is None:
                return
            for _ in range(n):
                if next(gen, "END") == "END":
                    break
        y_acc_ev = [[None, None] for _ in range(TCH)]
        h2_last_by_buf = [None, None]
        st1_lastmm_by_buf = [None, None]
        st2_lastmm_by_buf = [None, None]

        for ex in range(NEXP):
            if ex + 1 < 16:
                emit_rext(ex + 1)
            pump(gen1, 1)
            buf = (ex + mt) % 2
            if ex + 1 < NEXP:
                nbuf = buf ^ 1
                free_w13 = st_ev["wg_rdT"] if (mt == 0 and ex == 0) else st1_lastmm_by_buf[nbuf]
                if free_w13 is None:
                    free_w13 = st_ev["w13_rdT"][nbuf]
                ld1 = dma("P", "dW13a" if nbuf == 0 else "dW13b",
                          w13sb[nbuf][:].rearrange("p (kc m) -> p kc m", m=1024),
                          w13T[ex + 1].rearrange("(kc p) m -> p kc m", p=P),
                          [free_w13])
                ld2 = dma("P", "dW2a" if nbuf == 0 else "dW2b",
                          w2sb[nbuf][:].rearrange("p (ic m) -> p ic m", m=1024),
                          w2T[ex + 1].rearrange("(ic p) m -> p ic m", p=P),
                          [st2_lastmm_by_buf[nbuf] or st_ev["w2_rdT"][nbuf]])
                st_ev["w13_ld"][nbuf] = ld1
                st_ev["w2_ld"][nbuf] = ld2
            elif mt == 0:
                ldwg = dma("P", "dW13b",
                           w13sb[1][:].rearrange("p (kc m) -> p kc m", m=1024),
                           wgT.rearrange("(kc p) m -> p kc m", p=P),
                           [st1_lastmm_by_buf[1]])
                st_ev["w13_ld"][1] = ldwg

            h2_evs = []
            h2_last_half = [None, None]
            gidx = 0
            for h in range(2):
                for i in range(4):
                    if (h, i) == (1, 0):
                        pump(gen1, 1)
                    if (h, i) == (1, 2) and ex < 8:
                        pump(gen1, 1)
                    if ex == 0 and (h, i) == (0, 0):
                        emit_cb_half(0, 0)
                    if ex == 0 and (h, i) == (0, 1):
                        emit_cb_half(0, 1)
                    if ex + 1 < 16 and (h, i) == ((1, 0) if ex == 0 else (0, 1)):
                        emit_cb_half(ex + 1, 0)
                    if ex + 1 < 16 and (h, i) == ((1, 2) if ex == 0 else (0, 2)):
                        emit_cb_half(ex + 1, 1)
                    s1 = gidx % 2
                    gidx += 1
                    mm = None
                    for kc in range(DC):
                        mm = op("T", (lambda buf, i, h, kc, s1: lambda t: t.matmul(
                            GA[s1][:],
                            w13sb[buf][:, kc * 1024 + i * 128: kc * 1024 + (i + 1) * 128],
                            xn_fm[:, kc * 1024 + h * 512: kc * 1024 + (h + 1) * 512],
                            start=(kc == 0), stop=(kc == DC - 1)))(buf, i, h, kc, s1),
                            ([st_ev["w13_ld"][buf], xn_fm_ev[-1], st_ev["ga_free"][s1]]
                             if kc == 0 else []),
                            "T" if kc == DC - 1 else None)
                    gate_mm = mm
                    mm = None
                    for kc in range(DC):
                        mm = op("T", (lambda buf, i, h, kc, s1: lambda t: t.matmul(
                            UP[s1][:],
                            w13sb[buf][:, kc * 1024 + (i + 4) * 128: kc * 1024 + (i + 5) * 128],
                            xn_fm[:, kc * 1024 + h * 512: kc * 1024 + (h + 1) * 512],
                            start=(kc == 0), stop=(kc == DC - 1)))(buf, i, h, kc, s1),
                            ([st_ev["up_free"][s1]] if kc == 0 else []),
                            "T" if kc == DC - 1 else None)
                    up_mm = mm
                    st1_lastmm_by_buf[buf] = up_mm
                    sil = op("S", (lambda s1: lambda s: s.activation(
                        silu_sb[s1][:], GA[s1][:], AF.Silu))(s1),
                        [gate_mm, st_ev["silu_free"][s1]], "S")
                    st_ev["ga_free"][s1] = sil
                    h1e = op("V", (lambda s1: lambda v: v.tensor_tensor(
                        h1b[s1][:], silu_sb[s1][:], UP[s1][:], op=ALU.mult))(s1),
                        [sil, up_mm, st_ev["h1_free"][s1]], "V")
                    st_ev["up_free"][s1] = h1e
                    st_ev["silu_free"][s1] = h1e
                    if ex < 16:
                        h2e = op("P", (lambda ex, buf, i, h, s1: lambda g: g.tensor_tensor(
                            hp[buf][:, i * 1024 + h * 512: i * 1024 + (h + 1) * 512],
                            h1b[s1][:], cb_sb[ex % 2][:, h * 512:(h + 1) * 512],
                            op=ALU.mult))(ex, buf, i, h, s1),
                            [h1e, st_ev["cb_evs"][ex % 2][h], st_ev["hp_free"][0],
                             st_ev["hp_free"][1]], "Pc")
                        st_ev["cb_lastread"][ex % 2] = h2e
                    else:
                        h2e = op("P", (lambda buf, i, h, s1: lambda g: g.tensor_copy(
                            hp[buf][:, i * 1024 + h * 512: i * 1024 + (h + 1) * 512],
                            h1b[s1][:]))(buf, i, h, s1),
                            [h1e, st_ev["hp_free"][0], st_ev["hp_free"][1]], "Pc")
                    st_ev["h1_free"][s1] = h2e
                    h2_evs.append(h2e)
                    h2_last_half[h] = h2e
            h2_last_by_buf[buf] = h2_evs[-1]

            if ex == NEXP - 1:
                pump(gen1, 99)
                st_ev["xn_fm_rd"] = st1_lastmm_by_buf[buf]
            st2_slots = [Y2[0], Y2[1], CBp]
            nslot = 2 if ex == NEXP - 1 else 3
            for tch in range(TCH):
                if ex == NEXP - 1:
                    pump(gen2, 1)
                for dh in range(2):
                    s2 = (tch * 2 + dh) % nslot
                    ps = st2_slots[s2]
                    free_ev = st_ev["y2_free"][s2] if s2 < 2 else st_ev["cbp_free"]
                    mm = None
                    for ic in range(4):
                        mm = op("T", (lambda buf, tch, dh, ic, ps: lambda t: t.matmul(
                            ps[:],
                            hp[buf][:, ic * 1024 + tch * 128: ic * 1024 + (tch + 1) * 128],
                            w2sb[buf][:, ic * 1024 + dh * 512: ic * 1024 + (dh + 1) * 512],
                            start=(ic == 0), stop=(ic == 3)))(buf, tch, dh, ic, ps),
                            ([h2_last_half[tch // 4], st_ev["w2_ld"][buf],
                              free_ev] if ic == 0 else []),
                            "T" if ic == 3 else None)
                    st2_lastmm_by_buf[buf] = mm
                    if ex == 0:
                        acc = op("V", (lambda tch, dh, ps: lambda v: v.tensor_copy(
                            y_sb[:, tch * 1024 + dh * 512: tch * 1024 + (dh + 1) * 512],
                            ps[:]))(tch, dh, ps),
                            [mm, st_ev["y_free"]], "V")
                    else:
                        acc = op("V", (lambda tch, dh, ps: lambda v: v.tensor_tensor(
                            y_sb[:, tch * 1024 + dh * 512: tch * 1024 + (dh + 1) * 512],
                            y_sb[:, tch * 1024 + dh * 512: tch * 1024 + (dh + 1) * 512],
                            ps[:], op=ALU.add))(tch, dh, ps),
                            [mm, y_acc_ev[tch][dh]], "V")
                    y_acc_ev[tch][dh] = acc
                    if s2 < 2:
                        st_ev["y2_free"][s2] = acc
                    else:
                        st_ev["cbp_free"] = acc
            st_ev["hp_free"][0] = st2_lastmm_by_buf[buf]
            st_ev["hp_free"][1] = st2_lastmm_by_buf[buf]
            st_ev["w13_rdT"][buf] = st1_lastmm_by_buf[buf]
            st_ev["w2_rdT"][buf] = st2_lastmm_by_buf[buf]

        pump(gen2, 99)
        st_ev["xn_fm_rd"] = st1_lastmm_by_buf[(16 + mt) % 2]
        st_ev["m_rd"] = st2_lastmm_by_buf[(16 + mt) % 2]
        C.update(y_acc_ev=y_acc_ev)

    def emit_tail(mt, C):
        y_acc_ev = C["y_acc_ev"]
        s_ev = C["s_ev"]
        out_dma_evs = []
        gx = [hp[0][:, 3072:4096], fbuf[0][:]]
        gx_free = [st_ev["hp_free"][0], st_ev["fbuf_free"][0]]
        f1_prev = None
        f2m2 = [None, None]
        for tch in range(TCH):
            b = tch % 2
            ge = op("S", (lambda tch, b: lambda s: s.activation(
                gx[b], y_sb[:, tch * 1024:(tch + 1) * 1024], AF.Silu))(tch, b),
                [y_acc_ev[tch][0], y_acc_ev[tch][1], gx_free[b], f2m2[b]], "S")
            slot = tch % 3
            xe = dma("Y", f"dX{slot}",
                     xarena[:, slot * 1024:(slot + 1) * 1024],
                     xh[xrow0(mt) + tch * P: xrow0(mt) + (tch + 1) * P, :],
                     [st_ev["xslot"][slot], st_ev["xslot_w"][slot]])
            st_ev["xslot_w"][slot] = xe
            f1 = op("V", (lambda tch, b: lambda v: v.tensor_tensor(
                gx[b], s_tm[:, tch * 1024:(tch + 1) * 1024], gx[b], op=ALU.mult))(tch, b),
                [ge, s_ev[tch], st_ev["s_rd"]], "V")
            f1_prev = f1
            f2 = op("P", (lambda tch, slot, b: lambda g: g.tensor_tensor(
                y_sb[:, tch * 1024:(tch + 1) * 1024],
                gx[b], xarena[:, slot * 1024:(slot + 1) * 1024],
                op=ALU.add))(tch, slot, b),
                [f1, xe, ge], "Pc")
            f2m2[b] = f2
            st_ev["xslot"][slot] = f2
            ode = dma("P", "dO", out_d[mt * MT + tch * P: mt * MT + (tch + 1) * P, :],
                      y_sb[:, tch * 1024:(tch + 1) * 1024], [f2])
            out_dma_evs.append(ode)
        st_ev["y_free"] = out_dma_evs[-1]
        st_ev["hp_free"][0] = f2m2[0]
        st_ev["hp_free"][1] = f2m2[0]
        st_ev["fbuf_free"][0] = f2m2[1]
        st_ev["fbuf_free"][1] = f2m2[1]
        st_ev["s_rd"] = f1_prev

    import itertools
    C0 = {}
    for _ in gen_A(0, C0):
        pass
    for _ in gen_B1(0, C0):
        pass
    for _ in gen_B2(0, C0):
        pass
    emit_R(0, C0)
    for _ in gen_R2(0, C0):
        pass
    if n_mts > 1:
        C1 = {}
        g1 = itertools.chain(gen_A(1, C1), gen_B1(1, C1))
        g2 = gen_B2(1, C1)
        emit_E(0, C0, gen1=g1, gen2=g2)
        emit_R(1, C1)
        emit_tail(0, C0)
        for _ in gen_R2(1, C1):
            pass
        emit_E(1, C1)
        emit_tail(1, C1)
    else:
        for _ in gen_R2(0, C0):
            pass
        emit_E(0, C0)
        emit_tail(0, C0)

    if dbg_taps:
        allev = [("T", cnt["T"]), ("S", cnt["S"]), ("V", cnt["V"]), ("Pc", cnt["Pc"])]
        tensors = dict(xn_tm=xn_tm, xn_fm=xn_fm, xn2_fm=xn2_fm, s_tm=s_tm, y_sb=y_sb,
                       comb_tm=comb_tm, comb_fm=comb_fm, scores_sb=scores_sb, M0=M_t[0],
                       M1=M_t[1], stats=stats, biased=biased, m8s=m8s, wraw=wraw,
                       xnw2_fm=xnw2_fm, hp0=hp[0], hp1=hp[1], cb_sb=cb_sb)
        for name, shape, dt in dbg_taps:
            ap = tensors[name]
            dma("Y", "dDBG", dbg_drams[name], ap[0:shape[0], 0:shape[1]], allev)

    n_out = cnt["dO"]
    ops["Y"].append(([("dO", n_out), ("dDBG", cnt["dDBG"])], None, None, 0))
    ops["V"].append(([("dO", n_out), ("dDBG", cnt["dDBG"])], None, None, 0))

    with nc.Block() as block:
        def emit_stream(eng, stream):
            waited = {}
            for waits, fn, inc, inck in ops[stream]:
                for wv in waits:
                    if wv is None:
                        continue
                    sname, val = wv
                    if val <= 0 or waited.get(sname, 0) >= val:
                        continue
                    eng.wait_ge(sems[sname], val)
                    waited[sname] = val
                if fn is None:
                    continue
                inst = fn(eng)
                if inc is not None:
                    inst.then_inc(sems[inc], inck)

        block.tensor(lambda t: emit_stream(t, "T"))
        block.scalar(lambda s: emit_stream(s, "S"))
        block.vector(lambda v: emit_stream(v, "V"))
        block.gpsimd(lambda g: emit_stream(g, "P"))
        block.sync(lambda s: emit_stream(s, "Y"))

    ctx.close()
    return nc


_NC_CACHE = {}


def _get_nc():
    if "nc" not in _NC_CACHE:
        _NC_CACHE["nc"] = build_program()
    return _NC_CACHE["nc"]


def make_in_maps(L0_attn_x, L0_attn_Ws, L0_attn_bs, L0_attn_Wi, L0_attn_bi,
                 L0_attn_Wo, L0_attn_bo, L0_attn_Wg, L0_attn_bg, L0_attn_ln_w,
                 L0_attn_ln_b, L0_attn_A, L0_ffn_gate_w, L0_ffn_gate_b,
                 L0_ffn_w13, L0_ffn_w2, L0_ffn_w13_s, L0_ffn_w2_s):
    f = np.float32
    x = np.asarray(L0_attn_x, f)
    w13T = np.concatenate([np.transpose(np.asarray(L0_ffn_w13, f), (0, 2, 1)),
                           np.asarray(L0_ffn_w13_s, f).T[None]], 0)
    w13T = np.ascontiguousarray(w13T)
    w2T = np.concatenate([np.transpose(np.asarray(L0_ffn_w2, f), (0, 2, 1)),
                          np.asarray(L0_ffn_w2_s, f).T[None]], 0)
    w2T = np.ascontiguousarray(w2T)
    wgT = np.ascontiguousarray(np.asarray(L0_attn_Wg, f).T)
    woT = np.ascontiguousarray(np.concatenate(
        [np.asarray(L0_attn_Wo, f).T,
         np.asarray(L0_attn_bo, f)[None]], 0))
    wsT = np.ascontiguousarray(np.asarray(L0_attn_Ws, f).T)
    wiT = np.ascontiguousarray(np.asarray(L0_attn_Wi, f).T)
    gwT = np.ascontiguousarray(np.asarray(L0_ffn_gate_w, f).T)
    a_st = np.zeros((64, 16), f)
    a_st[0:16, :] = np.asarray(L0_attn_A, f).T
    a_st[32:48, :] = np.eye(16, dtype=f)
    bsbi = np.stack([np.asarray(L0_attn_bs, f), np.asarray(L0_attn_bi, f)], 1)
    gb8 = np.tile(np.asarray(L0_ffn_gate_b, f), 8)[None]
    lnwf = np.repeat(np.asarray(L0_attn_ln_w, f).reshape(8, 128).T, 128, axis=1)
    lnbf = np.repeat(np.asarray(L0_attn_ln_b, f).reshape(8, 128).T, 128, axis=1)
    lnwf = np.ascontiguousarray(lnwf)
    lnbf = np.ascontiguousarray(lnbf)
    eye_f = np.eye(P, dtype=f)
    ones_f = np.ones((1, P), f)

    in_maps = []
    for core in range(8):
        b = core // 4
        q = core % 4
        t0 = q * NTOK
        xs = np.empty((W + NTOK, D), f)
        if q == 0:
            xs[0:W] = 0.0
            um = np.zeros((16, W), f)
        else:
            xs[0:W] = x[b, t0 - W:t0]
            um = np.ones((16, W), f)
        xs[W:] = x[b, t0:t0 + NTOK]
        in_maps.append({
            "xh": xs, "w13T": w13T, "w2T": w2T, "wgT": wgT, "woT": woT,
            "wsT": wsT, "wiT": wiT, "gwT": gwT, "a_st": a_st, "bsbi": bsbi,
            "gb8": gb8, "lnwf": lnwf, "lnbf": lnbf,
            "m1r": np.ones((1, 1056), f),
            "eye_f": eye_f, "ones_f": ones_f, "umask": um,
        })
    return in_maps


def kernel(L0_attn_x, L0_attn_Ws, L0_attn_bs, L0_attn_Wi, L0_attn_bi,
           L0_attn_Wo, L0_attn_bo, L0_attn_Wg, L0_attn_bg, L0_attn_ln_w,
           L0_attn_ln_b, L0_attn_A, L0_ffn_input_ids, L0_ffn_gate_w,
           L0_ffn_gate_b, L0_ffn_w13, L0_ffn_w2, L0_ffn_w13_s, L0_ffn_w2_s):
    in_maps = make_in_maps(L0_attn_x, L0_attn_Ws, L0_attn_bs, L0_attn_Wi,
                           L0_attn_bi, L0_attn_Wo, L0_attn_bo, L0_attn_Wg,
                           L0_attn_bg, L0_attn_ln_w, L0_attn_ln_b, L0_attn_A,
                           L0_ffn_gate_w, L0_ffn_gate_b, L0_ffn_w13,
                           L0_ffn_w2, L0_ffn_w13_s, L0_ffn_w2_s)
    nc = _get_nc()
    res = run_bass_kernel_spmd(nc, in_maps, core_ids=list(range(8)))
    out = np.empty((2, 8192, 1024), np.float32)
    for core in range(8):
        b = core // 4
        q = core % 4
        out[b, q * NTOK:(q + 1) * NTOK] = res.results[core]["out"]
    return out


# version 22
# speedup vs baseline: 1.0476x; 1.0062x over previous
import numpy as np
import ml_dtypes
from contextlib import ExitStack

import concourse.bass as bass
import concourse.mybir as mybir
from concourse.bass_utils import run_bass_kernel_spmd

F32 = mybir.dt.float32
BF16 = mybir.dt.bfloat16
ALU = mybir.AluOpType
AF = mybir.ActivationFunctionType
AX = mybir.AxisListType

P = 128
D = 1024
DC = 8
NTOK = 2048
MT = 1024
TCH = 8
NEXP = 17
W = 24
K_J = 8
EPS = 1e-5


def build_program(n_mts=2, dbg_taps=None):
    nc = bass.Bass("TRN2", target_bir_lowering=False)

    xh = nc.dram_tensor("xh", [W + NTOK, D], F32, kind="ExternalInput").ap()
    w13T = nc.dram_tensor("w13T", [NEXP, D, D], F32, kind="ExternalInput").ap()
    w2T = nc.dram_tensor("w2T", [NEXP, 512, D], F32, kind="ExternalInput").ap()
    wgT = nc.dram_tensor("wgT", [D, D], F32, kind="ExternalInput").ap()
    woT_d = nc.dram_tensor("woT", [17, D], F32, kind="ExternalInput").ap()
    wsT = nc.dram_tensor("wsT", [D, 16], F32, kind="ExternalInput").ap()
    wiT = nc.dram_tensor("wiT", [D, 16], F32, kind="ExternalInput").ap()
    gwT = nc.dram_tensor("gwT", [D, 16], F32, kind="ExternalInput").ap()
    a_st = nc.dram_tensor("a_st", [64, 16], F32, kind="ExternalInput").ap()
    bsbi = nc.dram_tensor("bsbi", [16, 2], F32, kind="ExternalInput").ap()
    gb8 = nc.dram_tensor("gb8", [1, 128], F32, kind="ExternalInput").ap()
    lnwf_d = nc.dram_tensor("lnwf", [P, D], F32, kind="ExternalInput").ap()
    lnbf_d = nc.dram_tensor("lnbf", [P, D], F32, kind="ExternalInput").ap()
    m1r = nc.dram_tensor("m1r", [1, 1056], F32, kind="ExternalInput").ap()
    eye_f = nc.dram_tensor("eye_f", [P, P], F32, kind="ExternalInput").ap()
    ones_f = nc.dram_tensor("ones_f", [1, P], F32, kind="ExternalInput").ap()
    umask_d = nc.dram_tensor("umask", [16, W], F32, kind="ExternalInput").ap()
    out_d = nc.dram_tensor("out", [NTOK, D], F32, kind="ExternalOutput").ap()

    dbg_drams = {}
    if dbg_taps:
        for name, shape, dt in dbg_taps:
            dbg_drams[name] = nc.dram_tensor("dbg_" + name, shape, dt, kind="ExternalOutput").ap()

    ctx = ExitStack()
    e = ctx.enter_context

    xarena = e(nc.sbuf_tensor("xarena", [P, 3072], F32))
    xn_tm = e(nc.sbuf_tensor("xn_tm", [P, 8192], BF16))
    xn_fm = e(nc.sbuf_tensor("xn_fm", [P, 8192], BF16))
    xn2_fm = e(nc.sbuf_tensor("xn2_fm", [P, 8192], BF16))
    s_tm = e(nc.sbuf_tensor("s_tm", [P, 8192], BF16))
    y_sb = e(nc.sbuf_tensor("y_sb", [P, 8192], F32))
    w13sb = [e(nc.sbuf_tensor(f"w13sb{i}", [P, 8192], BF16)) for i in range(2)]
    w2sb = [e(nc.sbuf_tensor(f"w2sb{i}", [P, 4096], BF16)) for i in range(2)]
    hp0 = e(nc.sbuf_tensor("hp0", [P, 4096], BF16))
    hp = [hp0, hp0]
    gbuf0 = e(nc.sbuf_tensor("gbuf0", [P, 1024], BF16))
    gbuf = [gbuf0, gbuf0]
    fbuf0 = e(nc.sbuf_tensor("fbuf0", [P, 1024], BF16))
    fbuf = [fbuf0, fbuf0]
    silu_sb = [e(nc.sbuf_tensor(f"silu{i}", [P, 512], BF16)) for i in range(2)]
    h1b = [e(nc.sbuf_tensor(f"h1b{i}", [P, 512], BF16)) for i in range(2)]
    cb_sb = [e(nc.sbuf_tensor(f"cb_sb{i}", [P, 1024], BF16)) for i in range(2)]
    xn2c = [e(nc.sbuf_tensor(f"xn2c{i}", [P, 1024], BF16)) for i in range(2)]
    t2c0 = e(nc.sbuf_tensor("t2c0", [P, 1024], BF16))
    t2c = [t2c0, t2c0]
    M_t = [e(nc.sbuf_tensor(f"M{i}", [64, 1056], BF16)) for i in range(2)]
    woT = e(nc.sbuf_tensor("woTs", [17, 1024], BF16))
    gwTs = e(nc.sbuf_tensor("gwTs", [P, 128], BF16))
    wsumT = e(nc.sbuf_tensor("wsumT", [P, 128], BF16))
    wsf = xarena[:, 0:128]
    wif = xarena[:, 128:256]
    wsumf = xarena[:, 256:384]
    a_sb = e(nc.sbuf_tensor("a_sb", [64, 16], BF16))
    eye_b = e(nc.sbuf_tensor("eye_b", [P, P], BF16))
    ones_b = e(nc.sbuf_tensor("ones_b", [1, P], BF16))
    umask_sb = e(nc.sbuf_tensor("umask_sb", [48, W], BF16))
    lnwf_bt = e(nc.sbuf_tensor("lnwf_bt", [P, D], BF16))
    lnbf_bt = e(nc.sbuf_tensor("lnbf_bt", [P, D], BF16))
    gb_bt = e(nc.sbuf_tensor("gb_bt", [P, 128], F32))
    lnw_row = e(nc.sbuf_tensor("lnw_row", [1, D], BF16))
    lnb_row = e(nc.sbuf_tensor("lnb_row", [1, D], BF16))
    cbrow = [lnw_row, lnb_row]
    gb_row = e(nc.sbuf_tensor("gb_row", [1, 128], BF16))
    bsbi_sb = e(nc.sbuf_tensor("bsbi_sb", [16, 2], F32))
    epsc = e(nc.sbuf_tensor("epsc", [P, 1], F32))
    ub_sb = e(nc.sbuf_tensor("ub_sb", [16, 1], F32))
    stats = e(nc.sbuf_tensor("stats", [P, 128], F32))
    wstat = e(nc.sbuf_tensor("wstat", [32, 32], F32))
    wxn = s_tm[0:32, 0:1024]
    wxn2 = s_tm[0:32, 1024:2048]
    wt2 = s_tm[0:32, 2048:3072]
    xnw2_fm = e(nc.sbuf_tensor("xnw2_fm", [P, DC * W], BF16))
    scores_sb = e(nc.sbuf_tensor("scores_sb", [P, 128], F32))
    biased = xarena[:, 2048:2176]
    m8s = xarena[:, 2432:2496]
    masks = xarena[:, 2176:2304]
    wraw = xarena[:, 2304:2432]
    sumw = xarena[:, 2496:2504]
    rsw = xarena[:, 2504:2512]
    comb_tm = e(nc.sbuf_tensor("comb_tm", [P, 128], BF16))
    comb_fm = e(nc.sbuf_tensor("comb_fm", [16, 1024], BF16))

    SUMS, SQS, MEANB, MSQB, NVAR, STDB, RSTD, NMR = range(8)
    SUMS2, SQS2, MEAN2, MSQ2, NVAR2, STD2, RSTD2, NMR2 = range(8, 16)

    def st(i):
        return stats[:, i * 8:(i + 1) * 8]

    def stc(i, tch):
        return stats[:, i * 8 + tch: i * 8 + tch + 1]

    GA = [e(nc.psum_tensor(f"GA{i}", [P, 512], F32)) for i in range(2)]
    UP = [e(nc.psum_tensor(f"UP{i}", [P, 512], F32)) for i in range(2)]
    Y2 = [e(nc.psum_tensor(f"Y2{i}", [P, 512], F32)) for i in range(2)]
    CBp = e(nc.psum_tensor("CBp", [P, 512], F32))
    TRp = e(nc.psum_tensor("TRp", [P, 1024], BF16))

    sems = {}
    cnt = {}
    for name in ("T", "S", "V", "Pc", "dIY", "dIP", "dWX", "dX0", "dX1", "dX2",
                 "dW13a", "dW13b", "dW2a", "dW2b", "dCB0", "dCB1", "dO", "dDBG",
                 "dTR0", "dTR1", "dTR2", "dTR3", "dTR4", "dTR5", "dTR6", "dTR7"):
        sems[name] = nc.alloc_semaphore(name="sem_" + name)
        cnt[name] = 0
    ops = {"T": [], "S": [], "V": [], "P": [], "Y": []}
    ops["C"] = ops["S"]

    def op(stream, emitfn, waits=(), inc=None, inck=1):
        evt = None
        if inc is not None:
            cnt[inc] += inck
            evt = (inc, cnt[inc])
        ops[stream].append((list(waits), emitfn, inc, inck))
        return evt

    def dma(stream, sem_name, out_ap, in_ap, waits=(), transpose=False):
        return op(stream, lambda g: g.dma_start(out_ap, in_ap, transpose=transpose),
                  waits, sem_name, 16)

    def tr_batch(src_tm, dst_fm, tch, waits, rot):
        sname = f"dTR{rot[0] % 8}"
        rot[0] += 1
        ops["C"].append(([(sname, cnt[sname])], None, None, 0))
        ev = None
        for dc in range(DC):
            ev = dma("C", sname,
                     dst_fm[:, dc * 1024 + tch * 128: dc * 1024 + (tch + 1) * 128],
                     src_tm[:, dc * 128:(dc + 1) * 128],
                     waits if dc == 0 else [], transpose=True)
        return ev

    ld_wsf = dma("Y", "dIY", wsf.rearrange("p (kc n) -> p kc n", n=16),
                 wsT.rearrange("(kc p) n -> p kc n", p=P))
    ld_wif = dma("Y", "dIY", wif.rearrange("p (kc n) -> p kc n", n=16),
                 wiT.rearrange("(kc p) n -> p kc n", p=P))
    ld_bsbi = dma("Y", "dIY", bsbi_sb[:], bsbi)

    ld_a = dma("P", "dIP", a_sb[:], a_st)
    ld_gbr = dma("P", "dIP", gb_row[:], gb8)
    ld_eye = dma("P", "dIP", eye_b[:], eye_f)
    ld_ones = dma("P", "dIP", ones_b[:], ones_f)
    ld_um = dma("P", "dIP", umask_sb[32:48, :], umask_d)
    ld_lnwf = dma("P", "dIP", lnwf_bt[:], lnwf_d)
    ld_lnbf = dma("P", "dIP", lnbf_bt[:], lnbf_d)
    ld_gw = dma("P", "dIP", gwTs[:].rearrange("p (kc n) -> p kc n", n=16),
                gwT.rearrange("(kc p) n -> p kc n", p=P))
    ld_wo = dma("P", "dIP", woT[:], woT_d)
    ld_wg = dma("P", "dW13b", w13sb[1][:].rearrange("p (kc m) -> p kc m", m=1024),
                wgT.rearrange("(kc p) m -> p kc m", p=P))
    ld_w13_0 = dma("P", "dW13a", w13sb[0][:].rearrange("p (kc m) -> p kc m", m=1024),
                   w13T[0].rearrange("(kc p) m -> p kc m", p=P))
    ld_w2_0 = dma("P", "dW2a", w2sb[0][:].rearrange("p (ic m) -> p ic m", m=1024),
                  w2T[0].rearrange("(ic p) m -> p ic m", p=P))

    ev_initY = ("dIY", cnt["dIY"])
    ev_initP = ("dIP", cnt["dIP"])
    ev_eps = op("V", lambda v: v.memset(epsc[:], float(EPS)), [], "V")
    ev_ub = op("V", lambda v: v.tensor_tensor(ub_sb[:], bsbi_sb[:, 0:1], bsbi_sb[:, 1:2], op=ALU.add),
               [ev_initY], "V")
    ev_wsumf = op("V", lambda v: v.tensor_tensor(wsumf, wsf, wif, op=ALU.add),
                  [ev_initY], "V")
    ev_wsumT = op("S", lambda s: s.copy(wsumT[:], wsumf), [ev_wsumf], "S")

    def bc_half(dst_ap, row_ap, waits, prev_free):
        mm = op("T", (lambda: (lambda t: t.matmul(
            CBp[:, 0:row_ap.shape[-1]], ones_b[:], row_ap, start=True, stop=True)))(),
            list(waits) + ([prev_free] if prev_free else []), "T")
        return op("V", (lambda: (lambda v: v.tensor_copy(dst_ap, CBp[:, 0:row_ap.shape[-1]])))(),
                  [mm], "V")

    ev_gb = bc_half(gb_bt[:], gb_row[:], [ev_initP], None)

    ev_mz0 = op("S", lambda s: s.memzero(M_t[0][:]), [], "S")
    ev_mz1 = op("S", lambda s: s.memzero(M_t[1][:]), [], "S")
    dma("P", "dTR0", M_t[0][16:17, :], m1r, [ev_mz0])
    ev_mz1 = dma("P", "dTR0", M_t[1][16:17, :], m1r, [ev_mz1])
    ev_mz1 = ("dTR0", 32)

    st_ev = {
        "xslot": [ev_wsumT, None, None],
        "xslot_w": [None] * 3,
        "trp_free": None,
        "cbp_free": ev_gb,
        "ga_free": [None, None], "up_free": [None, None],
        "y2_free": [None, None],
        "w13_ld": [ld_w13_0, ld_wg],
        "w2_ld": [ld_w2_0, None],
        "w13_rdT": [None, None], "w2_rdT": [None, None],
        "wg_rdT": None,
        "xn_tm_rd": None,
        "xn_fm_rd": None,
        "xn2_rd": None,
        "s_rd": None,
        "y_free": None,
        "silu_free": [None, None],
        "h1_free": [None, None],
        "hp_free": [None, None],
        "cbrow_free": [None, None],
        "cb_evs": [None, None],
        "cb_lastread": [None, None],
        "rext_ev": [None, None],
        "tr_rot": [0],
        "gbuf_free": [None, None],
        "fbuf_free": [None, None],
        "xn2c_free": [None, None],
        "t2c_free": [None, None],
        "m_rd": ev_mz1,
        "stats_free": None,
        "stats_free2": None,
        "hfin_prev": None,
        "out_dma": [],
    }

    def xrow0(mt):
        return W + mt * MT

    def stats_chain(ME, MS, NV, SD, RS, NM, SU, SQ, c0, c1, last_sum, last_sq):
        def sl(i):
            return stats[:, i * 8 + c0: i * 8 + c1]
        m_ev = op("P", lambda g: g.tensor_scalar_mul(sl(ME), sl(SU), -1.0 / D), [last_sum], "Pc")
        q_ev = op("P", lambda g: g.tensor_scalar_mul(sl(MS), sl(SQ), 1.0 / D), [last_sq, m_ev], "Pc")
        m2_ev = op("P", lambda g: g.tensor_tensor(sl(NV), sl(ME), sl(ME), op=ALU.mult),
                   [m_ev, q_ev], "Pc")
        nv_ev = op("P", lambda g: g.tensor_tensor(sl(NV), sl(NV), sl(MS), op=ALU.subtract),
                   [m2_ev], "Pc")
        sd_ev = op("S", lambda s: s.activation(sl(SD), sl(NV), AF.Sqrt,
                                               bias=epsc[:, 0:1], scale=-1.0),
                   [nv_ev, ev_eps], "S")
        rs_ev = op("V", lambda v: v.reciprocal(sl(RS), sl(SD)), [sd_ev], "V")
        nm_ev = op("P", lambda g: g.tensor_tensor(sl(NM), sl(ME), sl(RS), op=ALU.mult),
                   [rs_ev], "Pc")
        return rs_ev, nm_ev

    def gen_A(mt, C):
        warm = (mt == 0)
        x_ev = [None] * TCH

        def issue_x(tch, extra):
            slot = tch % 3
            x_ev[tch] = dma("Y", f"dX{slot}",
                            xarena[:, slot * 1024:(slot + 1) * 1024],
                            xh[xrow0(mt) + tch * P: xrow0(mt) + (tch + 1) * P, :],
                            [st_ev["xslot"][slot], st_ev["xslot_w"][slot]] + extra)
            st_ev["xslot_w"][slot] = x_ev[tch]
            st_ev["xslot"][slot] = x_ev[tch]

        for t0 in range(3):
            issue_x(t0, [])
        if warm:
            wx_ev = dma("Y", "dWX", y_sb[0:W, 7168:8192], xh[0:W, :], [st_ev["y_free"]])
        yield
        wstate = {"uw": None}

        def gen_warm():
            wxs = y_sb[0:W, 7168:8192]
            wscr = y_sb[0:W, 6144:7168]
            a1 = op("V", lambda v: v.tensor_reduce(wstat[0:W, 0:1], wxs, axis=AX.X, op=ALU.add),
                    [wx_ev], "V")
            yield
            a3 = op("S", lambda s: s.activation(wscr, wxs, AF.Square, accum_out=wstat[0:W, 1:2]),
                    [wx_ev], "S")
            yield
            a4 = op("P", lambda g: g.tensor_scalar_mul(wstat[0:W, 2:3], wstat[0:W, 0:1], -1.0 / D),
                    [a1], "Pc")
            yield
            a5 = op("P", lambda g: g.tensor_scalar_mul(wstat[0:W, 3:4], wstat[0:W, 1:2], 1.0 / D),
                    [a3, a4], "Pc")
            yield
            a6 = op("P", lambda g: g.tensor_tensor(wstat[0:W, 4:5], wstat[0:W, 2:3],
                                                   wstat[0:W, 2:3], op=ALU.mult), [a5], "Pc")
            yield
            a7 = op("P", lambda g: g.tensor_tensor(wstat[0:W, 4:5], wstat[0:W, 4:5],
                                                   wstat[0:W, 3:4], op=ALU.subtract), [a6], "Pc")
            yield
            a8 = op("S", lambda s: s.activation(wstat[0:W, 5:6], wstat[0:W, 4:5], AF.Sqrt,
                                                bias=epsc[0:W, 0:1], scale=-1.0), [a7, ev_eps], "S")
            yield
            a9 = op("V", lambda v: v.reciprocal(wstat[0:W, 6:7], wstat[0:W, 5:6]), [a8], "V")
            yield
            a10 = op("P", lambda g: g.tensor_tensor(wstat[0:W, 7:8], wstat[0:W, 2:3],
                                                    wstat[0:W, 6:7], op=ALU.mult), [a9], "Pc")
            yield
            wxn_ev = op("S", lambda s: s.activation(wxn[0:W, :], wxs, AF.Identity,
                                                    bias=wstat[0:W, 7:8], scale=wstat[0:W, 6:7]),
                        [a10], "S")
            yield
            b1 = op("V", lambda v: v.tensor_reduce(wstat[0:W, 8:9], wxn[0:W, :], axis=AX.X,
                                                   op=ALU.add), [wxn_ev], "V")
            yield
            b3 = op("S", lambda s: s.activation(wscr, wxn[0:W, :], AF.Square,
                                                accum_out=wstat[0:W, 9:10]), [wxn_ev], "S")
            yield
            b4 = op("P", lambda g: g.tensor_scalar_mul(wstat[0:W, 10:11], wstat[0:W, 8:9],
                                                       -1.0 / D), [b1], "Pc")
            yield
            b5 = op("P", lambda g: g.tensor_scalar_mul(wstat[0:W, 11:12], wstat[0:W, 9:10],
                                                       1.0 / D), [b3, b4], "Pc")
            yield
            b6 = op("P", lambda g: g.tensor_tensor(wstat[0:W, 12:13], wstat[0:W, 10:11],
                                                   wstat[0:W, 10:11], op=ALU.mult), [b5], "Pc")
            yield
            b7 = op("P", lambda g: g.tensor_tensor(wstat[0:W, 12:13], wstat[0:W, 12:13],
                                                   wstat[0:W, 11:12], op=ALU.subtract), [b6], "Pc")
            yield
            b8 = op("S", lambda s: s.activation(wstat[0:W, 13:14], wstat[0:W, 12:13], AF.Sqrt,
                                                bias=epsc[0:W, 0:1], scale=-1.0), [b7, ev_eps], "S")
            yield
            b9 = op("V", lambda v: v.reciprocal(wstat[0:W, 14:15], wstat[0:W, 13:14]), [b8], "V")
            yield
            b10 = op("P", lambda g: g.tensor_tensor(wstat[0:W, 15:16], wstat[0:W, 10:11],
                                                    wstat[0:W, 14:15], op=ALU.mult), [b9], "Pc")
            yield
            wt1_ev = op("S", lambda s: s.activation(wxn2[0:W, :], wxn[0:W, :], AF.Identity,
                                                    bias=wstat[0:W, 15:16], scale=wstat[0:W, 14:15]),
                        [b10], "S")
            yield
            wstate["wxn2"] = wt1_ev
            yield

        wg_gen = gen_warm() if warm else iter(())

        def pump(gen, n):
            for _ in range(n):
                if next(gen, "END") == "END":
                    break

        sum_ev = [None] * TCH
        sqs_ev = [None] * TCH
        xn_ev = [None] * TCH
        for g in range(4):
            for tch in (2 * g, 2 * g + 1):
                xs = xarena[:, (tch % 3) * 1024:((tch % 3) + 1) * 1024]
                scr = xn_tm[:, tch * 1024:(tch + 1) * 1024]
                sum_ev[tch] = op("V", (lambda xs, tch: lambda v: v.tensor_reduce(
                    stc(SUMS, tch), xs, axis=AX.X, op=ALU.add))(xs, tch),
                    [x_ev[tch], st_ev["stats_free"], st_ev["stats_free2"]], "V")
                sqs_ev[tch] = op("S", (lambda xs, scr, tch: lambda s: s.activation(
                    scr, xs, AF.Square, accum_out=stc(SQS, tch)))(xs, scr, tch),
                    [x_ev[tch], st_ev["xn_tm_rd"], st_ev["stats_free"],
                     st_ev["stats_free2"]], "S")
            rs1_ev, nm1_ev = stats_chain(MEANB, MSQB, NVAR, STDB, RSTD, NMR, SUMS, SQS,
                                         2 * g, 2 * g + 2,
                                         sum_ev[2 * g + 1], sqs_ev[2 * g + 1])
            for tch in (2 * g, 2 * g + 1):
                xs = xarena[:, (tch % 3) * 1024:((tch % 3) + 1) * 1024]
                xn_ev[tch] = op("S", (lambda xs, tch: lambda s: s.activation(
                    xn_tm[:, tch * 1024:(tch + 1) * 1024], xs, AF.Identity,
                    bias=stc(NMR, tch), scale=stc(RSTD, tch)))(xs, tch),
                    [nm1_ev, rs1_ev], "S")
                st_ev["xslot"][tch % 3] = xn_ev[tch]
            for t in (2 * g + 3, 2 * g + 4):
                if t < TCH and x_ev[t] is None:
                    issue_x(t, [xn_ev[t - 3]])
            pump(wg_gen, 7)
            yield

        sum2_ev = [None] * TCH
        sqs2_ev = [None] * TCH
        for tch in range(TCH):
            xs = xn_tm[:, tch * 1024:(tch + 1) * 1024]
            scr = xn2c[tch % 2][:]
            scr_war = sqs2_ev[tch - 2] if tch >= 2 else st_ev["xn2c_free"][tch % 2]
            sum2_ev[tch] = op("V", (lambda xs, tch: lambda v: v.tensor_reduce(
                stc(SUMS2, tch), xs, axis=AX.X, op=ALU.add))(xs, tch), [xn_ev[tch]], "V")
            sqs2_ev[tch] = op("S", (lambda xs, scr, tch: lambda s: s.activation(
                scr, xs, AF.Square, accum_out=stc(SQS2, tch)))(xs, scr, tch),
                [xn_ev[tch], scr_war], "S")
            if tch % 2 == 1:
                pump(wg_gen, 6)
                yield
        rs2_ev, nm2_ev = stats_chain(MEAN2, MSQ2, NVAR2, STD2, RSTD2, NMR2, SUMS2, SQS2,
                                     0, 8, sum2_ev[-1], sqs2_ev[-1])
        pump(wg_gen, 99)
        uw_src_ev = wstate.get("wxn2")

        C.update(x_ev=x_ev, xn_ev=xn_ev, uw_src_ev=uw_src_ev,
                 nm2_ev=nm2_ev, rs2_ev=rs2_ev)
        yield

    def gen_B1(mt, C):
        xn_ev = C["xn_ev"]
        nm2_ev = C["nm2_ev"]
        rs2_ev = C["rs2_ev"]
        xn2_fm_ev = [None] * TCH
        t1_last = None
        for tch in range(TCH):
            c = tch % 2
            t1_ev = op("S", (lambda tch, c: lambda s: s.activation(
                xn2c[c][:], xn_tm[:, tch * 1024:(tch + 1) * 1024], AF.Identity,
                bias=stc(NMR2, tch), scale=stc(RSTD2, tch)))(tch, c),
                [nm2_ev, rs2_ev, st_ev["xn2c_free"][c]], "S")
            t1_last = t1_ev
            mm = None
            for dc in range(DC):
                mm = op("T", (lambda c, dc: lambda t: t.transpose(
                    TRp[:, dc * 128:(dc + 1) * 128],
                    xn2c[c][:, dc * 128:(dc + 1) * 128], eye_b[:]))(c, dc),
                    [t1_ev, st_ev["trp_free"], ev_initP] if dc == 0 else [],
                    "T" if dc == DC - 1 else None)
            ev2 = op("V", (lambda tch: lambda v: v.tensor_tensor(
                xn2_fm[:].rearrange("p (dc t) -> p dc t", t=1024)[:, :, tch * 128:(tch + 1) * 128],
                TRp[:].rearrange("p (dc t) -> p dc t", t=128),
                lnwf_bt[:].rearrange("p (dc t) -> p dc t", t=128),
                op=ALU.mult))(tch),
                [mm, st_ev["xn2_rd"]], "V")
            ev2b = op("P", (lambda tch: lambda g: g.tensor_tensor(
                xn2_fm[:].rearrange("p (dc t) -> p dc t", t=1024)[:, :, tch * 128:(tch + 1) * 128],
                xn2_fm[:].rearrange("p (dc t) -> p dc t", t=1024)[:, :, tch * 128:(tch + 1) * 128],
                lnbf_bt[:].rearrange("p (dc t) -> p dc t", t=128),
                op=ALU.add))(tch),
                [ev2], "Pc")
            xn2_fm_ev[tch] = ev2b
            st_ev["trp_free"] = ev2
            st_ev["xn2c_free"][c] = mm
            yield
        st_ev["stats_free"] = nm2_ev
        st_ev["stats_free2"] = t1_last
        C.update(xn2_fm_ev=xn2_fm_ev)
        yield

    def gen_B2(mt, C):
        xn_ev = C["xn_ev"]
        xn_fm_ev = [None] * TCH
        scores_mm_ev = None
        for tch in range(TCH):
            mm = None
            for dc in range(DC):
                mm = op("T", (lambda tch, dc: lambda t: t.transpose(
                    TRp[:, dc * 128:(dc + 1) * 128],
                    xn_tm[:, tch * 1024 + dc * 128: tch * 1024 + (dc + 1) * 128],
                    eye_b[:]))(tch, dc),
                    [xn_ev[tch], st_ev["trp_free"], ev_initP] if dc == 0 else [],
                    "T" if dc == DC - 1 else None)
            ev1 = op("V", (lambda tch: lambda v: v.tensor_copy(
                xn_fm[:].rearrange("p (dc t) -> p dc t", t=1024)[:, :, tch * 128:(tch + 1) * 128],
                TRp[:].rearrange("p (dc t) -> p dc t", t=128)))(tch),
                [mm, st_ev["xn_fm_rd"]], "V")
            xn_fm_ev[tch] = ev1
            st_ev["trp_free"] = ev1
            yield
        C.update(xn_fm_ev=xn_fm_ev)
        for tch in range(TCH):
            smm = None
            for kc in range(DC):
                smm = op("T", (lambda tch, kc: lambda t: t.matmul(
                    CBp[:, tch * 16:(tch + 1) * 16],
                    xn_fm[:, kc * 1024 + tch * 128: kc * 1024 + (tch + 1) * 128],
                    gwTs[:, kc * 16:(kc + 1) * 16],
                    start=(kc == 0), stop=(kc == DC - 1)))(tch, kc),
                    [xn_fm_ev[tch], ev_initP] + ([st_ev["cbp_free"]] if tch == 0 else []),
                    "T" if kc == DC - 1 else None)
            scores_mm_ev = smm
        C.update(scores_mm_ev=scores_mm_ev)
        yield

    def emit_R(mt, C):
        warm = (mt == 0)
        Lw = (W if warm else 0) + MT
        u_chunks = ([("w", 0, W)] if warm else []) + \
                   [("m", (W if warm else 0) + 0, 512), ("m", (W if warm else 0) + 512, 512)]
        sc_chunks = []
        c0 = 0
        while c0 < Lw:
            cw = min(512, Lw - c0)
            sc_chunks.append((c0, cw))
            c0 += cw
        x_ev = C["x_ev"]
        xn_ev = C["xn_ev"]
        xn2_fm_ev = C["xn2_fm_ev"]
        xn_fm_ev = C["xn_fm_ev"]
        scores_mm_ev = C["scores_mm_ev"]
        uw_src_ev = C["uw_src_ev"]
        if warm:
            trw_mm = None
            for dc in range(DC):
                trw_mm = op("T", (lambda dc: lambda t: t.transpose(
                    TRp[:, dc * W:(dc + 1) * W],
                    wxn2[0:W, dc * 128:(dc + 1) * 128], eye_b[0:W, 0:W]))(dc),
                    [uw_src_ev, st_ev["trp_free"], ev_initP] if dc == 0 else [],
                    "T" if dc == DC - 1 else None)
            uw_m = op("V", lambda v: v.tensor_tensor(
                xnw2_fm[:].rearrange("p (dc w) -> p dc w", w=W),
                TRp[:, 0:DC * W].rearrange("p (dc w) -> p dc w", w=W),
                lnwf_bt[:].rearrange("p (dc t) -> p dc t", t=128)[:, :, 0:W],
                op=ALU.mult), [trw_mm], "V")
            uw_src_ev = op("P", lambda g: g.tensor_tensor(
                xnw2_fm[:].rearrange("p (dc w) -> p dc w", w=W),
                xnw2_fm[:].rearrange("p (dc w) -> p dc w", w=W),
                lnbf_bt[:].rearrange("p (dc t) -> p dc t", t=128)[:, :, 0:W],
                op=ALU.add), [uw_m], "Pc")
            st_ev["trp_free"] = uw_m
        u_evs = []
        ping = 0
        m_write_evs = []
        for kind, c0, cw in u_chunks:
            if kind == "w":
                rhs = xnw2_fm[:, 0:DC * W].rearrange("p (kc t) -> p kc t", t=W)
                src_ev = uw_src_ev
            else:
                t0 = c0 - (W if warm else 0)
                rhs = xn2_fm[:].rearrange("p (kc t) -> p kc t", t=1024)[:, :, t0:t0 + cw]
                src_ev = xn2_fm_ev[-1]
            mm = None
            for kc in range(DC):
                mm = op("T", (lambda rhs, kc, cw, ping: lambda t: t.matmul(
                    Y2[ping][0:16, 0:cw], wsumT[:, kc * 16:(kc + 1) * 16], rhs[:, kc, :],
                    start=(kc == 0), stop=(kc == DC - 1)))(rhs, kc, cw, ping),
                    [src_ev, ev_wsumT, st_ev["y2_free"][ping]] if kc == 0 else [],
                    "T" if kc == DC - 1 else None)
            ev_e = op("S", (lambda c0, cw, ping: lambda s: s.activation(
                M_t[0][32:48, c0:c0 + cw], Y2[ping][0:16, 0:cw], AF.Identity,
                bias=ub_sb[:]))(c0, cw, ping), [mm, ev_ub, st_ev["m_rd"]], "S")
            st_ev["y2_free"][ping] = ev_e
            u_evs.append(ev_e)
            m_write_evs.append(ev_e)
            ping ^= 1
        if warm:
            mask_ev = op("V", lambda v: v.tensor_tensor(
                M_t[0][32:48, 0:W], M_t[0][32:48, 0:W], umask_sb[32:48, :], op=ALU.mult),
                [u_evs[0], ev_initP], "V")
            m_write_evs.append(mask_ev)
        ucopy_ev = op("P", (lambda Lw: lambda g: g.tensor_copy(
            M_t[1][32:48, 0:Lw], M_t[0][32:48, 0:Lw]))(Lw), m_write_evs + [st_ev["m_rd"]], "Pc")

        if warm:
            pass
        else:
            hb = st_ev["hfin_prev"]
            z0 = op("V", lambda v: v.tensor_copy(M_t[0][0:16, 0:1], M_t[0][0:16, 1048:1049]),
                    [hb, st_ev["m_rd"]], "V")
            z1 = op("V", lambda v: v.tensor_copy(M_t[1][0:16, 0:1], M_t[0][0:16, 1048:1049]),
                    [z0], "V")
            m_write_evs += [z0, z1]

        sg_ev = op("S", lambda s: s.activation(scores_sb[:], CBp[:, 0:128], AF.Sigmoid),
                   [scores_mm_ev], "S")
        bi_ev = op("V", lambda v: v.tensor_tensor(biased, scores_sb[:], gb_bt[:], op=ALU.add),
                   [sg_ev, ev_gb, st_ev["xslot"][2], st_ev["xslot_w"][2]], "V")
        m8_ev = None
        for tch in range(TCH):
            m8_ev = op("V", (lambda tch: lambda v: v.max(
                out=m8s[:, tch * 8:(tch + 1) * 8], in_=biased[:, tch * 16:(tch + 1) * 16]))(tch),
                [bi_ev], "V")
        mk_ev = None
        for tch in range(TCH):
            mk_ev = op("V", (lambda tch: lambda v: v.tensor_scalar(
                masks[:, tch * 16:(tch + 1) * 16], biased[:, tch * 16:(tch + 1) * 16],
                m8s[:, tch * 8 + 3: tch * 8 + 4], None, op0=ALU.is_ge))(tch),
                [m8_ev], "V")
        wr_ev = op("V", lambda v: v.tensor_tensor(wraw, masks, scores_sb[:], op=ALU.mult),
                   [mk_ev], "V")
        sw_ev = op("V", lambda v: v.tensor_reduce(
            sumw, wraw.rearrange("p (a b) -> p a b", b=16), axis=AX.X, op=ALU.add),
            [wr_ev], "V")
        rs_ev = op("V", lambda v: v.reciprocal(rsw, sumw), [sw_ev], "V")
        cmb_ev = None
        for tch in range(TCH):
            cmb_ev = op("V", (lambda tch: lambda v: v.tensor_scalar(
                comb_tm[:, tch * 16:(tch + 1) * 16], wraw[:, tch * 16:(tch + 1) * 16],
                rsw[:, tch:tch + 1], 2.5, op0=ALU.mult, op1=ALU.mult))(tch),
                [rs_ev], "V")

        comb_done = [None]

        def emit_comb_tr():
            mm = None
            for tch in range(TCH):
                mm = op("T", (lambda tch: lambda t: t.transpose(
                    TRp[0:16, tch * 128:(tch + 1) * 128],
                    comb_tm[:, tch * 16:(tch + 1) * 16], eye_b[:]))(tch),
                    [cmb_ev, st_ev["trp_free"]] if tch == 0 else [],
                    "T" if tch == TCH - 1 else None)
            ev = op("V", lambda v: v.tensor_copy(comb_fm[:], TRp[0:16, 0:1024]), [mm], "V")
            st_ev["trp_free"] = ev
            comb_done[0] = ev
        st_ev["cbp_free"] = sg_ev
        st_ev["xslot"][2] = cmb_ev
        st_ev["xslot_w"][2] = rs_ev

        def emit_rext(ex):
            b = ex % 2
            ev = dma("Y", f"dCB{b}", cbrow[b][:], comb_fm[ex:ex + 1, :],
                     [comb_done[0], st_ev["cbrow_free"][b]])
            st_ev["rext_ev"][b] = ev

        def emit_cb_half(ex, h):
            b = ex % 2
            mmcb = op("T", (lambda b, h: lambda t: t.matmul(
                CBp[:], ones_b[:], cbrow[b][:, h * 512:(h + 1) * 512],
                start=True, stop=True))(b, h),
                [st_ev["rext_ev"][b], st_ev["cbp_free"]], "T")
            cbe = op("S", (lambda b, h: lambda s: s.activation(
                cb_sb[b][:, h * 512:(h + 1) * 512], CBp[:], AF.Identity))(b, h),
                [mmcb, st_ev["cb_lastread"][b]], "S")
            st_ev["cbp_free"] = cbe
            if h == 0:
                st_ev["cb_evs"][b] = [cbe, None]
            else:
                st_ev["cb_evs"][b][1] = cbe
                st_ev["cbrow_free"][b] = mmcb


        ghosts = [cb_sb[0][:], cb_sb[1][:], xn2c[0][:], xn2c[1][:], t2c[0][:]] + \
                 [hp[0][:, t * 1024:(t + 1) * 1024] for t in range(3)]
        ghost_free = [st_ev["cb_lastread"][0], st_ev["cb_lastread"][1],
                      st_ev["xn2c_free"][0], st_ev["xn2c_free"][1], st_ev["t2c_free"][0],
                      st_ev["hp_free"][0], st_ev["hp_free"][1], st_ev["hp_free"][0]]
        sg_evs = [[None, None] for _ in range(TCH)]
        gate_last_mm = None

        def emit_gate_group(tch, h):
            nonlocal gate_last_mm
            pair = [GA[0], GA[1]] if tch % 2 == 0 else [UP[0], UP[1]]
            pfree = st_ev["ga_free"] if tch % 2 == 0 else st_ev["up_free"]
            mm = None
            for kc in range(DC):
                mm = op("T", (lambda tch, h, kc, pair: lambda t: t.matmul(
                    pair[h][:], xn2_fm[:, kc * 1024 + tch * 128: kc * 1024 + (tch + 1) * 128],
                    w13sb[1][:, kc * 1024 + h * 512: kc * 1024 + (h + 1) * 512],
                    start=(kc == 0), stop=(kc == DC - 1)))(tch, h, kc, pair),
                    ([xn2_fm_ev[tch], st_ev["w13_ld"][1], pfree[h]] if kc == 0 else []),
                    "T" if kc == DC - 1 else None)
            gate_last_mm = mm
            sgm = op("S", (lambda tch, h, pair: lambda s: s.activation(
                ghosts[tch][:, h * 512:(h + 1) * 512], pair[h][:], AF.Sigmoid))(tch, h, pair),
                [mm, ghost_free[tch]], "S")
            pfree[h] = sgm
            sg_evs[tch][h] = sgm

        gate_seq = [(t, hh) for t in range(TCH) for hh in range(2)]
        gi = 0

        nch = len(sc_chunks)
        scan_tanh_ev = {}
        for j in range(1, K_J + 1):
            for _ in range(2):
                if gi < len(gate_seq):
                    emit_gate_group(*gate_seq[gi])
                    gi += 1
            src_t = M_t[(j + 1) % 2]
            dst = M_t[j % 2]
            for ci, (c0, cw) in enumerate(sc_chunks):
                pg = ci % 2
                if j == 1:
                    waits = list(m_write_evs)
                elif j == 2:
                    waits = [scan_tanh_ev[(j - 1, ci)], ucopy_ev]
                    if ci > 0:
                        waits.append(scan_tanh_ev[(j - 1, ci - 1)])
                else:
                    waits = [scan_tanh_ev[(j - 1, ci)]]
                    if ci > 0:
                        waits.append(scan_tanh_ev[(j - 1, ci - 1)])
                waits.append(st_ev["y2_free"][pg])
                mm = op("T", (lambda src_t, c0, cw, pg: lambda t: t.matmul(
                    Y2[pg][0:16, 0:cw], a_sb[:], src_t[0:64, c0:c0 + cw],
                    start=True, stop=True))(src_t, c0, cw, pg),
                    waits + ([ev_initP] if (j, ci) == (1, 0) else []), "T")
                th = op("S", (lambda dst, c0, cw, pg: lambda s: s.activation(
                    dst[0:16, c0 + 1:c0 + 1 + cw], Y2[pg][0:16, 0:cw], AF.Tanh))(dst, c0, cw, pg),
                    [mm], "S")
                st_ev["y2_free"][pg] = th
                scan_tanh_ev[(j, ci)] = th
            if j == 2:
                emit_comb_tr()
        while gi < len(gate_seq):
            emit_gate_group(*gate_seq[gi])
            gi += 1
        hfin = M_t[K_J % 2]
        hfin_ev = [scan_tanh_ev[(K_J, ci)] for ci in range(nch)]
        st_ev["hfin_prev"] = hfin_ev[-1]
        h_off = (W if warm else 0) + 1

        emit_rext(0)

        if mt == 1:
            ld1 = dma("P", "dW13b",
                      w13sb[1][:].rearrange("p (kc m) -> p kc m", m=1024),
                      w13T[0].rearrange("(kc p) m -> p kc m", p=P),
                      [gate_last_mm])
            ld2 = dma("P", "dW2b",
                      w2sb[1][:].rearrange("p (ic m) -> p ic m", m=1024),
                      w2T[0].rearrange("(ic p) m -> p ic m", p=P),
                      [st_ev["w2_rdT"][1]])
            st_ev["w13_ld"][1] = ld1
            st_ev["w2_ld"][1] = ld2
        st_ev["wg_rdT"] = gate_last_mm
        st_ev["xn2_rd"] = gate_last_mm
        C.update(hfin=hfin, hfin_ev=hfin_ev, h_off=h_off, sg_evs=sg_evs,
                 ghosts=ghosts, xn_ev=C["xn_ev"], sc_chunks=sc_chunks,
                 comb_ev=comb_done[0], emit_rext=emit_rext, emit_cb_half=emit_cb_half)

    def gen_R2(mt, C):
        hfin = C["hfin"]
        hfin_ev = C["hfin_ev"]
        h_off = C["h_off"]
        sg_evs = C["sg_evs"]
        ghosts = C["ghosts"]
        xn_ev = C["xn_ev"]
        sc_chunks = C["sc_chunks"]
        r2_slots = [Y2[0], Y2[1], CBp, GA[0]]

        def r2_free_get(k):
            if k < 2:
                return st_ev["y2_free"][k]
            return st_ev["cbp_free"] if k == 2 else st_ev["ga_free"][0]

        def r2_free_set(k, ev):
            if k < 2:
                st_ev["y2_free"][k] = ev
            elif k == 2:
                st_ev["cbp_free"] = ev
            else:
                st_ev["ga_free"][0] = ev
        s_ev = [None] * TCH
        t2_by_tch = [None] * TCH
        for tch in range(TCH):
            lo = h_off + tch * 128
            hi = lo + 128
            hw = [hfin_ev[ci] for ci, (c0, cw) in enumerate(sc_chunks)
                  if c0 + 1 < hi and c0 + 1 + cw > lo]
            for h in range(2):
                k = (tch * 2 + h) % 4
                ps = r2_slots[k]
                mm = op("T", (lambda tch, h, ps: lambda t: t.matmul(
                    ps[:], hfin[0:17, h_off + tch * 128: h_off + (tch + 1) * 128],
                    woT[:, h * 512:(h + 1) * 512], start=True, stop=True))(tch, h, ps),
                    hw + [ev_initP, r2_free_get(k)], "T")
                tbuf = xarena[:, 2048 + h * 512: 2048 + (h + 1) * 512]
                t1e = op("S", (lambda tbuf, ps: lambda s: s.activation(
                    tbuf, ps[:], AF.Identity))(tbuf, ps),
                    [mm, st_ev["xslot_w"][2], st_ev["xslot"][2]], "S")
                r2_free_set(k, t1e)
                t2e = op("P", (lambda tbuf, tch, h: lambda g: g.tensor_tensor(
                    tbuf, tbuf, ghosts[tch][:, h * 512:(h + 1) * 512], op=ALU.mult))(tbuf, tch, h),
                    [t1e, sg_evs[tch][h]], "Pc")
                s_e = op("V", (lambda tbuf, tch, h: lambda v: v.tensor_tensor(
                    s_tm[:, tch * 1024 + h * 512: tch * 1024 + (h + 1) * 512],
                    tbuf,
                    xn_tm[:, tch * 1024 + h * 512: tch * 1024 + (h + 1) * 512],
                    op=ALU.add))(tbuf, tch, h),
                    [t2e, xn_ev[tch], st_ev["s_rd"]], "V")
                st_ev["xslot"][2] = s_e
                st_ev["xslot_w"][2] = t2e
            s_ev[tch] = s_e
            t2_by_tch[tch] = t2e
            yield
        st_ev["cb_lastread"][0] = t2_by_tch[0]
        st_ev["cb_lastread"][1] = t2_by_tch[1]
        st_ev["xn2c_free"][0] = t2_by_tch[2]
        st_ev["xn2c_free"][1] = t2_by_tch[3]
        st_ev["t2c_free"][0] = t2_by_tch[4]
        st_ev["t2c_free"][1] = t2_by_tch[4]
        st_ev["hp_free"][0] = t2_by_tch[7]
        st_ev["hp_free"][1] = t2_by_tch[7]
        st_ev["xn_tm_rd"] = s_ev[-1]
        C.update(s_ev=s_ev)
        yield


    def emit_E(mt, C, gen1=None, gen2=None):
        xn_fm_ev = C["xn_fm_ev"]
        comb_ev = C["comb_ev"]
        emit_rext = C["emit_rext"]
        emit_cb_half = C["emit_cb_half"]

        def pump(gen, n):
            if gen is None:
                return
            for _ in range(n):
                if next(gen, "END") == "END":
                    break
        y_acc_ev = [[None, None] for _ in range(TCH)]
        h2_last_by_buf = [None, None]
        st1_lastmm_by_buf = [None, None]
        st2_lastmm_by_buf = [None, None]

        for ex in range(NEXP):
            if ex + 1 < 16:
                emit_rext(ex + 1)
            pump(gen1, 1)
            buf = (ex + mt) % 2
            if ex + 1 < NEXP:
                nbuf = buf ^ 1
                free_w13 = st_ev["wg_rdT"] if (mt == 0 and ex == 0) else st1_lastmm_by_buf[nbuf]
                if free_w13 is None:
                    free_w13 = st_ev["w13_rdT"][nbuf]
                ld1 = dma("P", "dW13a" if nbuf == 0 else "dW13b",
                          w13sb[nbuf][:].rearrange("p (kc m) -> p kc m", m=1024),
                          w13T[ex + 1].rearrange("(kc p) m -> p kc m", p=P),
                          [free_w13])
                ld2 = dma("P", "dW2a" if nbuf == 0 else "dW2b",
                          w2sb[nbuf][:].rearrange("p (ic m) -> p ic m", m=1024),
                          w2T[ex + 1].rearrange("(ic p) m -> p ic m", p=P),
                          [st2_lastmm_by_buf[nbuf] or st_ev["w2_rdT"][nbuf]])
                st_ev["w13_ld"][nbuf] = ld1
                st_ev["w2_ld"][nbuf] = ld2
            elif mt == 0:
                ldwg = dma("P", "dW13b",
                           w13sb[1][:].rearrange("p (kc m) -> p kc m", m=1024),
                           wgT.rearrange("(kc p) m -> p kc m", p=P),
                           [st1_lastmm_by_buf[1]])
                st_ev["w13_ld"][1] = ldwg

            h2_evs = []
            h2_last_half = [None, None]
            gidx = 0
            for h in range(2):
                for i in range(4):
                    if (h, i) == (1, 0):
                        pump(gen1, 1)
                    if (h, i) == (1, 2) and ex < 8:
                        pump(gen1, 1)
                    if ex == 0 and (h, i) == (0, 0):
                        emit_cb_half(0, 0)
                    if ex == 0 and (h, i) == (0, 1):
                        emit_cb_half(0, 1)
                    if ex + 1 < 16 and (h, i) == ((1, 0) if ex == 0 else (0, 1)):
                        emit_cb_half(ex + 1, 0)
                    if ex + 1 < 16 and (h, i) == ((1, 2) if ex == 0 else (0, 2)):
                        emit_cb_half(ex + 1, 1)
                    s1 = gidx % 2
                    gidx += 1
                    mm = None
                    for kc in range(DC):
                        mm = op("T", (lambda buf, i, h, kc, s1: lambda t: t.matmul(
                            GA[s1][:],
                            w13sb[buf][:, kc * 1024 + i * 128: kc * 1024 + (i + 1) * 128],
                            xn_fm[:, kc * 1024 + h * 512: kc * 1024 + (h + 1) * 512],
                            start=(kc == 0), stop=(kc == DC - 1)))(buf, i, h, kc, s1),
                            ([st_ev["w13_ld"][buf], xn_fm_ev[-1], st_ev["ga_free"][s1]]
                             if kc == 0 else []),
                            "T" if kc == DC - 1 else None)
                    gate_mm = mm
                    mm = None
                    for kc in range(DC):
                        mm = op("T", (lambda buf, i, h, kc, s1: lambda t: t.matmul(
                            UP[s1][:],
                            w13sb[buf][:, kc * 1024 + (i + 4) * 128: kc * 1024 + (i + 5) * 128],
                            xn_fm[:, kc * 1024 + h * 512: kc * 1024 + (h + 1) * 512],
                            start=(kc == 0), stop=(kc == DC - 1)))(buf, i, h, kc, s1),
                            ([st_ev["up_free"][s1]] if kc == 0 else []),
                            "T" if kc == DC - 1 else None)
                    up_mm = mm
                    st1_lastmm_by_buf[buf] = up_mm
                    sil = op("S", (lambda s1: lambda s: s.activation(
                        silu_sb[s1][:], GA[s1][:], AF.Silu))(s1),
                        [gate_mm, st_ev["silu_free"][s1]], "S")
                    st_ev["ga_free"][s1] = sil
                    h1e = op("V", (lambda s1: lambda v: v.tensor_tensor(
                        h1b[s1][:], silu_sb[s1][:], UP[s1][:], op=ALU.mult))(s1),
                        [sil, up_mm, st_ev["h1_free"][s1]], "V")
                    st_ev["up_free"][s1] = h1e
                    st_ev["silu_free"][s1] = h1e
                    if ex < 16:
                        h2e = op("P", (lambda ex, buf, i, h, s1: lambda g: g.tensor_tensor(
                            hp[buf][:, i * 1024 + h * 512: i * 1024 + (h + 1) * 512],
                            h1b[s1][:], cb_sb[ex % 2][:, h * 512:(h + 1) * 512],
                            op=ALU.mult))(ex, buf, i, h, s1),
                            [h1e, st_ev["cb_evs"][ex % 2][h], st_ev["hp_free"][0],
                             st_ev["hp_free"][1]], "Pc")
                        st_ev["cb_lastread"][ex % 2] = h2e
                    else:
                        h2e = op("P", (lambda buf, i, h, s1: lambda g: g.tensor_copy(
                            hp[buf][:, i * 1024 + h * 512: i * 1024 + (h + 1) * 512],
                            h1b[s1][:]))(buf, i, h, s1),
                            [h1e, st_ev["hp_free"][0], st_ev["hp_free"][1]], "Pc")
                    st_ev["h1_free"][s1] = h2e
                    h2_evs.append(h2e)
                    h2_last_half[h] = h2e
            h2_last_by_buf[buf] = h2_evs[-1]

            if ex == NEXP - 1:
                pump(gen1, 99)
                st_ev["xn_fm_rd"] = st1_lastmm_by_buf[buf]
            st2_slots = [Y2[0], Y2[1], CBp]
            nslot = 2 if ex == NEXP - 1 else 3
            for tch in range(TCH):
                if ex == NEXP - 1:
                    pump(gen2, 1)
                for dh in range(2):
                    s2 = (tch * 2 + dh) % nslot
                    ps = st2_slots[s2]
                    free_ev = st_ev["y2_free"][s2] if s2 < 2 else st_ev["cbp_free"]
                    mm = None
                    for ic in range(4):
                        mm = op("T", (lambda buf, tch, dh, ic, ps: lambda t: t.matmul(
                            ps[:],
                            hp[buf][:, ic * 1024 + tch * 128: ic * 1024 + (tch + 1) * 128],
                            w2sb[buf][:, ic * 1024 + dh * 512: ic * 1024 + (dh + 1) * 512],
                            start=(ic == 0), stop=(ic == 3)))(buf, tch, dh, ic, ps),
                            ([h2_last_half[tch // 4], st_ev["w2_ld"][buf],
                              free_ev] if ic == 0 else []),
                            "T" if ic == 3 else None)
                    st2_lastmm_by_buf[buf] = mm
                    if ex == 0:
                        acc = op("V", (lambda tch, dh, ps: lambda v: v.tensor_copy(
                            y_sb[:, tch * 1024 + dh * 512: tch * 1024 + (dh + 1) * 512],
                            ps[:]))(tch, dh, ps),
                            [mm, st_ev["y_free"]], "V")
                    else:
                        acc = op("V", (lambda tch, dh, ps: lambda v: v.tensor_tensor(
                            y_sb[:, tch * 1024 + dh * 512: tch * 1024 + (dh + 1) * 512],
                            y_sb[:, tch * 1024 + dh * 512: tch * 1024 + (dh + 1) * 512],
                            ps[:], op=ALU.add))(tch, dh, ps),
                            [mm, y_acc_ev[tch][dh]], "V")
                    y_acc_ev[tch][dh] = acc
                    if s2 < 2:
                        st_ev["y2_free"][s2] = acc
                    else:
                        st_ev["cbp_free"] = acc
            st_ev["hp_free"][0] = st2_lastmm_by_buf[buf]
            st_ev["hp_free"][1] = st2_lastmm_by_buf[buf]
            st_ev["w13_rdT"][buf] = st1_lastmm_by_buf[buf]
            st_ev["w2_rdT"][buf] = st2_lastmm_by_buf[buf]

        pump(gen2, 99)
        st_ev["xn_fm_rd"] = st1_lastmm_by_buf[(16 + mt) % 2]
        st_ev["m_rd"] = st2_lastmm_by_buf[(16 + mt) % 2]
        C.update(y_acc_ev=y_acc_ev)

    def emit_tail(mt, C):
        y_acc_ev = C["y_acc_ev"]
        s_ev = C["s_ev"]
        out_dma_evs = []
        gx = [hp[0][:, 3072:4096], fbuf[0][:]]
        gx_free = [st_ev["hp_free"][0], st_ev["fbuf_free"][0]]
        f1_prev = None
        f2m2 = [None, None]
        for tch in range(TCH):
            b = tch % 2
            ge = op("S", (lambda tch, b: lambda s: s.activation(
                gx[b], y_sb[:, tch * 1024:(tch + 1) * 1024], AF.Silu))(tch, b),
                [y_acc_ev[tch][0], y_acc_ev[tch][1], gx_free[b], f2m2[b]], "S")
            slot = tch % 3
            xe = dma("Y", f"dX{slot}",
                     xarena[:, slot * 1024:(slot + 1) * 1024],
                     xh[xrow0(mt) + tch * P: xrow0(mt) + (tch + 1) * P, :],
                     [st_ev["xslot"][slot], st_ev["xslot_w"][slot]])
            st_ev["xslot_w"][slot] = xe
            f1 = op("V", (lambda tch, b: lambda v: v.tensor_tensor(
                gx[b], s_tm[:, tch * 1024:(tch + 1) * 1024], gx[b], op=ALU.mult))(tch, b),
                [ge, s_ev[tch], st_ev["s_rd"]], "V")
            f1_prev = f1
            f2 = op("P", (lambda tch, slot, b: lambda g: g.tensor_tensor(
                y_sb[:, tch * 1024:(tch + 1) * 1024],
                gx[b], xarena[:, slot * 1024:(slot + 1) * 1024],
                op=ALU.add))(tch, slot, b),
                [f1, xe, ge], "Pc")
            f2m2[b] = f2
            st_ev["xslot"][slot] = f2
            ode = dma("P", "dO", out_d[mt * MT + tch * P: mt * MT + (tch + 1) * P, :],
                      y_sb[:, tch * 1024:(tch + 1) * 1024], [f2])
            out_dma_evs.append(ode)
        st_ev["y_free"] = out_dma_evs[-1]
        st_ev["hp_free"][0] = f2m2[0]
        st_ev["hp_free"][1] = f2m2[0]
        st_ev["fbuf_free"][0] = f2m2[1]
        st_ev["fbuf_free"][1] = f2m2[1]
        st_ev["s_rd"] = f1_prev

    import itertools
    C0 = {}
    for _ in gen_A(0, C0):
        pass
    for _ in gen_B1(0, C0):
        pass
    for _ in gen_B2(0, C0):
        pass
    emit_R(0, C0)
    for _ in gen_R2(0, C0):
        pass
    if n_mts > 1:
        C1 = {}
        g1 = itertools.chain(gen_A(1, C1), gen_B1(1, C1))
        g2 = gen_B2(1, C1)
        emit_E(0, C0, gen1=g1, gen2=g2)
        emit_tail(0, C0)
        emit_R(1, C1)
        for _ in gen_R2(1, C1):
            pass
        emit_E(1, C1)
        emit_tail(1, C1)
    else:
        for _ in gen_R2(0, C0):
            pass
        emit_E(0, C0)
        emit_tail(0, C0)

    if dbg_taps:
        allev = [("T", cnt["T"]), ("S", cnt["S"]), ("V", cnt["V"]), ("Pc", cnt["Pc"])]
        tensors = dict(xn_tm=xn_tm, xn_fm=xn_fm, xn2_fm=xn2_fm, s_tm=s_tm, y_sb=y_sb,
                       comb_tm=comb_tm, comb_fm=comb_fm, scores_sb=scores_sb, M0=M_t[0],
                       M1=M_t[1], stats=stats, biased=biased, m8s=m8s, wraw=wraw,
                       xnw2_fm=xnw2_fm, hp0=hp[0], hp1=hp[1], cb_sb=cb_sb)
        for name, shape, dt in dbg_taps:
            ap = tensors[name]
            dma("Y", "dDBG", dbg_drams[name], ap[0:shape[0], 0:shape[1]], allev)

    n_out = cnt["dO"]
    ops["Y"].append(([("dO", n_out), ("dDBG", cnt["dDBG"])], None, None, 0))
    ops["V"].append(([("dO", n_out), ("dDBG", cnt["dDBG"])], None, None, 0))

    with nc.Block() as block:
        def emit_stream(eng, stream):
            waited = {}
            for waits, fn, inc, inck in ops[stream]:
                for wv in waits:
                    if wv is None:
                        continue
                    sname, val = wv
                    if val <= 0 or waited.get(sname, 0) >= val:
                        continue
                    eng.wait_ge(sems[sname], val)
                    waited[sname] = val
                if fn is None:
                    continue
                inst = fn(eng)
                if inc is not None:
                    inst.then_inc(sems[inc], inck)

        block.tensor(lambda t: emit_stream(t, "T"))
        block.scalar(lambda s: emit_stream(s, "S"))
        block.vector(lambda v: emit_stream(v, "V"))
        block.gpsimd(lambda g: emit_stream(g, "P"))
        block.sync(lambda s: emit_stream(s, "Y"))

    ctx.close()
    return nc


_NC_CACHE = {}


def _get_nc():
    if "nc" not in _NC_CACHE:
        _NC_CACHE["nc"] = build_program()
    return _NC_CACHE["nc"]


def make_in_maps(L0_attn_x, L0_attn_Ws, L0_attn_bs, L0_attn_Wi, L0_attn_bi,
                 L0_attn_Wo, L0_attn_bo, L0_attn_Wg, L0_attn_bg, L0_attn_ln_w,
                 L0_attn_ln_b, L0_attn_A, L0_ffn_gate_w, L0_ffn_gate_b,
                 L0_ffn_w13, L0_ffn_w2, L0_ffn_w13_s, L0_ffn_w2_s):
    f = np.float32
    x = np.asarray(L0_attn_x, f)
    w13T = np.concatenate([np.transpose(np.asarray(L0_ffn_w13, f), (0, 2, 1)),
                           np.asarray(L0_ffn_w13_s, f).T[None]], 0)
    w13T = np.ascontiguousarray(w13T)
    w2T = np.concatenate([np.transpose(np.asarray(L0_ffn_w2, f), (0, 2, 1)),
                          np.asarray(L0_ffn_w2_s, f).T[None]], 0)
    w2T = np.ascontiguousarray(w2T)
    wgT = np.ascontiguousarray(np.asarray(L0_attn_Wg, f).T)
    woT = np.ascontiguousarray(np.concatenate(
        [np.asarray(L0_attn_Wo, f).T,
         np.asarray(L0_attn_bo, f)[None]], 0))
    wsT = np.ascontiguousarray(np.asarray(L0_attn_Ws, f).T)
    wiT = np.ascontiguousarray(np.asarray(L0_attn_Wi, f).T)
    gwT = np.ascontiguousarray(np.asarray(L0_ffn_gate_w, f).T)
    a_st = np.zeros((64, 16), f)
    a_st[0:16, :] = np.asarray(L0_attn_A, f).T
    a_st[32:48, :] = np.eye(16, dtype=f)
    bsbi = np.stack([np.asarray(L0_attn_bs, f), np.asarray(L0_attn_bi, f)], 1)
    gb8 = np.tile(np.asarray(L0_ffn_gate_b, f), 8)[None]
    lnwf = np.repeat(np.asarray(L0_attn_ln_w, f).reshape(8, 128).T, 128, axis=1)
    lnbf = np.repeat(np.asarray(L0_attn_ln_b, f).reshape(8, 128).T, 128, axis=1)
    lnwf = np.ascontiguousarray(lnwf)
    lnbf = np.ascontiguousarray(lnbf)
    eye_f = np.eye(P, dtype=f)
    ones_f = np.ones((1, P), f)

    in_maps = []
    for core in range(8):
        b = core // 4
        q = core % 4
        t0 = q * NTOK
        xs = np.empty((W + NTOK, D), f)
        if q == 0:
            xs[0:W] = 0.0
            um = np.zeros((16, W), f)
        else:
            xs[0:W] = x[b, t0 - W:t0]
            um = np.ones((16, W), f)
        xs[W:] = x[b, t0:t0 + NTOK]
        in_maps.append({
            "xh": xs, "w13T": w13T, "w2T": w2T, "wgT": wgT, "woT": woT,
            "wsT": wsT, "wiT": wiT, "gwT": gwT, "a_st": a_st, "bsbi": bsbi,
            "gb8": gb8, "lnwf": lnwf, "lnbf": lnbf,
            "m1r": np.ones((1, 1056), f),
            "eye_f": eye_f, "ones_f": ones_f, "umask": um,
        })
    return in_maps


def kernel(L0_attn_x, L0_attn_Ws, L0_attn_bs, L0_attn_Wi, L0_attn_bi,
           L0_attn_Wo, L0_attn_bo, L0_attn_Wg, L0_attn_bg, L0_attn_ln_w,
           L0_attn_ln_b, L0_attn_A, L0_ffn_input_ids, L0_ffn_gate_w,
           L0_ffn_gate_b, L0_ffn_w13, L0_ffn_w2, L0_ffn_w13_s, L0_ffn_w2_s):
    in_maps = make_in_maps(L0_attn_x, L0_attn_Ws, L0_attn_bs, L0_attn_Wi,
                           L0_attn_bi, L0_attn_Wo, L0_attn_bo, L0_attn_Wg,
                           L0_attn_bg, L0_attn_ln_w, L0_attn_ln_b, L0_attn_A,
                           L0_ffn_gate_w, L0_ffn_gate_b, L0_ffn_w13,
                           L0_ffn_w2, L0_ffn_w13_s, L0_ffn_w2_s)
    nc = _get_nc()
    res = run_bass_kernel_spmd(nc, in_maps, core_ids=list(range(8)))
    out = np.empty((2, 8192, 1024), np.float32)
    for core in range(8):
        b = core // 4
        q = core % 4
        out[b, q * NTOK:(q + 1) * NTOK] = res.results[core]["out"]
    return out


# version 23
# speedup vs baseline: 1.0742x; 1.0254x over previous
import numpy as np
import ml_dtypes
from contextlib import ExitStack

import concourse.bass as bass
import concourse.mybir as mybir
from concourse.bass_utils import run_bass_kernel_spmd

F32 = mybir.dt.float32
BF16 = mybir.dt.bfloat16
ALU = mybir.AluOpType
AF = mybir.ActivationFunctionType
AX = mybir.AxisListType

P = 128
D = 1024
DC = 8
NTOK = 2048
MT = 1024
TCH = 8
NEXP = 17
W = 24
K_J = 8
EPS = 1e-5


def build_program(n_mts=2, dbg_taps=None):
    nc = bass.Bass("TRN2", target_bir_lowering=False)

    xh = nc.dram_tensor("xh", [W + NTOK, D], F32, kind="ExternalInput").ap()
    w13T = nc.dram_tensor("w13T", [NEXP, D, D], F32, kind="ExternalInput").ap()
    w2T = nc.dram_tensor("w2T", [NEXP, 512, D], F32, kind="ExternalInput").ap()
    wgT = nc.dram_tensor("wgT", [D, D], F32, kind="ExternalInput").ap()
    woT_d = nc.dram_tensor("woT", [17, D], F32, kind="ExternalInput").ap()
    wsT = nc.dram_tensor("wsT", [D, 16], F32, kind="ExternalInput").ap()
    wiT = nc.dram_tensor("wiT", [D, 16], F32, kind="ExternalInput").ap()
    gwT = nc.dram_tensor("gwT", [D, 16], F32, kind="ExternalInput").ap()
    a_st = nc.dram_tensor("a_st", [64, 16], F32, kind="ExternalInput").ap()
    bsbi = nc.dram_tensor("bsbi", [16, 2], F32, kind="ExternalInput").ap()
    gb8 = nc.dram_tensor("gb8", [1, 128], F32, kind="ExternalInput").ap()
    lnwf_d = nc.dram_tensor("lnwf", [P, D], F32, kind="ExternalInput").ap()
    lnbf_d = nc.dram_tensor("lnbf", [P, D], F32, kind="ExternalInput").ap()
    m1r = nc.dram_tensor("m1r", [1, 1056], F32, kind="ExternalInput").ap()
    eye_f = nc.dram_tensor("eye_f", [P, P], F32, kind="ExternalInput").ap()
    ones_f = nc.dram_tensor("ones_f", [1, P], F32, kind="ExternalInput").ap()
    umask_d = nc.dram_tensor("umask", [16, W], F32, kind="ExternalInput").ap()
    out_d = nc.dram_tensor("out", [NTOK, D], F32, kind="ExternalOutput").ap()

    dbg_drams = {}
    if dbg_taps:
        for name, shape, dt in dbg_taps:
            dbg_drams[name] = nc.dram_tensor("dbg_" + name, shape, dt, kind="ExternalOutput").ap()

    ctx = ExitStack()
    e = ctx.enter_context

    xarena = e(nc.sbuf_tensor("xarena", [P, 3072], F32))
    xn_tm = e(nc.sbuf_tensor("xn_tm", [P, 8192], BF16))
    xn_fm = e(nc.sbuf_tensor("xn_fm", [P, 8192], BF16))
    xn2_fm = e(nc.sbuf_tensor("xn2_fm", [P, 8192], BF16))
    s_tm = e(nc.sbuf_tensor("s_tm", [P, 8192], BF16))
    y_sb = e(nc.sbuf_tensor("y_sb", [P, 8192], F32))
    w13sb = [e(nc.sbuf_tensor(f"w13sb{i}", [P, 8192], BF16)) for i in range(2)]
    w2sb = [e(nc.sbuf_tensor(f"w2sb{i}", [P, 4096], BF16)) for i in range(2)]
    hp0 = e(nc.sbuf_tensor("hp0", [P, 4096], BF16))
    hp = [hp0, hp0]
    gbuf0 = e(nc.sbuf_tensor("gbuf0", [P, 1024], BF16))
    gbuf = [gbuf0, gbuf0]
    fbuf0 = e(nc.sbuf_tensor("fbuf0", [P, 1024], BF16))
    fbuf = [fbuf0, fbuf0]
    silu_sb = [e(nc.sbuf_tensor(f"silu{i}", [P, 512], BF16)) for i in range(2)]
    h1b = [e(nc.sbuf_tensor(f"h1b{i}", [P, 512], BF16)) for i in range(2)]
    cb_sb = [e(nc.sbuf_tensor(f"cb_sb{i}", [P, 1024], BF16)) for i in range(2)]
    xn2c = [e(nc.sbuf_tensor(f"xn2c{i}", [P, 1024], BF16)) for i in range(2)]
    t2c0 = e(nc.sbuf_tensor("t2c0", [P, 1024], BF16))
    t2c = [t2c0, t2c0]
    M_t = [e(nc.sbuf_tensor(f"M{i}", [64, 1056], BF16)) for i in range(2)]
    woT = e(nc.sbuf_tensor("woTs", [17, 1024], BF16))
    gwTs = e(nc.sbuf_tensor("gwTs", [P, 128], BF16))
    wsumT = e(nc.sbuf_tensor("wsumT", [P, 128], BF16))
    wsf = xarena[:, 0:128]
    wif = xarena[:, 128:256]
    wsumf = xarena[:, 256:384]
    a_sb = e(nc.sbuf_tensor("a_sb", [64, 16], BF16))
    eye_b = e(nc.sbuf_tensor("eye_b", [P, P], BF16))
    ones_b = e(nc.sbuf_tensor("ones_b", [1, P], BF16))
    umask_sb = e(nc.sbuf_tensor("umask_sb", [48, W], BF16))
    lnwf_bt = e(nc.sbuf_tensor("lnwf_bt", [P, D], BF16))
    lnbf_bt = e(nc.sbuf_tensor("lnbf_bt", [P, D], BF16))
    gb_bt = e(nc.sbuf_tensor("gb_bt", [P, 128], F32))
    lnw_row = e(nc.sbuf_tensor("lnw_row", [1, D], BF16))
    lnb_row = e(nc.sbuf_tensor("lnb_row", [1, D], BF16))
    cbrow = [lnw_row, lnb_row]
    gb_row = e(nc.sbuf_tensor("gb_row", [1, 128], BF16))
    bsbi_sb = e(nc.sbuf_tensor("bsbi_sb", [16, 2], F32))
    epsc = e(nc.sbuf_tensor("epsc", [P, 1], F32))
    ub_sb = e(nc.sbuf_tensor("ub_sb", [16, 1], F32))
    stats = e(nc.sbuf_tensor("stats", [P, 128], F32))
    wstat = e(nc.sbuf_tensor("wstat", [32, 32], F32))
    wxn = s_tm[0:32, 0:1024]
    wxn2 = s_tm[0:32, 1024:2048]
    wt2 = s_tm[0:32, 2048:3072]
    xnw2_fm = e(nc.sbuf_tensor("xnw2_fm", [P, DC * W], BF16))
    scores_sb = e(nc.sbuf_tensor("scores_sb", [P, 128], F32))
    biased = xarena[:, 2048:2176]
    m8s = xarena[:, 2432:2496]
    masks = xarena[:, 2176:2304]
    wraw = xarena[:, 2304:2432]
    sumw = xarena[:, 2496:2504]
    rsw = xarena[:, 2504:2512]
    comb_tm = e(nc.sbuf_tensor("comb_tm", [P, 128], BF16))
    comb_fm = e(nc.sbuf_tensor("comb_fm", [16, 1024], BF16))

    SUMS, SQS, MEANB, MSQB, NVAR, STDB, RSTD, NMR = range(8)
    SUMS2, SQS2, MEAN2, MSQ2, NVAR2, STD2, RSTD2, NMR2 = range(8, 16)

    def st(i):
        return stats[:, i * 8:(i + 1) * 8]

    def stc(i, tch):
        return stats[:, i * 8 + tch: i * 8 + tch + 1]

    GA = [e(nc.psum_tensor(f"GA{i}", [P, 512], F32)) for i in range(2)]
    UP = [e(nc.psum_tensor(f"UP{i}", [P, 512], F32)) for i in range(2)]
    Y2 = [e(nc.psum_tensor(f"Y2{i}", [P, 512], F32)) for i in range(2)]
    CBp = e(nc.psum_tensor("CBp", [P, 512], F32))
    TRp = e(nc.psum_tensor("TRp", [P, 1024], BF16))

    sems = {}
    cnt = {}
    for name in ("T", "S", "V", "Pc", "dIY", "dIP", "dWX", "dX0", "dX1", "dX2",
                 "dW13a", "dW13b", "dW2a", "dW2b", "dCB0", "dCB1", "dO", "dDBG",
                 "dTR0", "dTR1", "dTR2", "dTR3", "dTR4", "dTR5", "dTR6", "dTR7"):
        sems[name] = nc.alloc_semaphore(name="sem_" + name)
        cnt[name] = 0
    ops = {"T": [], "S": [], "V": [], "P": [], "Y": []}
    ops["C"] = ops["S"]

    def op(stream, emitfn, waits=(), inc=None, inck=1):
        evt = None
        if inc is not None:
            cnt[inc] += inck
            evt = (inc, cnt[inc])
        ops[stream].append((list(waits), emitfn, inc, inck))
        return evt

    def dma(stream, sem_name, out_ap, in_ap, waits=(), transpose=False):
        return op(stream, lambda g: g.dma_start(out_ap, in_ap, transpose=transpose),
                  waits, sem_name, 16)

    def tr_batch(src_tm, dst_fm, tch, waits, rot):
        sname = f"dTR{rot[0] % 8}"
        rot[0] += 1
        ops["C"].append(([(sname, cnt[sname])], None, None, 0))
        ev = None
        for dc in range(DC):
            ev = dma("C", sname,
                     dst_fm[:, dc * 1024 + tch * 128: dc * 1024 + (tch + 1) * 128],
                     src_tm[:, dc * 128:(dc + 1) * 128],
                     waits if dc == 0 else [], transpose=True)
        return ev

    ld_wsf = dma("Y", "dIY", wsf.rearrange("p (kc n) -> p kc n", n=16),
                 wsT.rearrange("(kc p) n -> p kc n", p=P))
    ld_wif = dma("Y", "dIY", wif.rearrange("p (kc n) -> p kc n", n=16),
                 wiT.rearrange("(kc p) n -> p kc n", p=P))
    ld_bsbi = dma("Y", "dIY", bsbi_sb[:], bsbi)

    ld_a = dma("P", "dIP", a_sb[:], a_st)
    ld_gbr = dma("P", "dIP", gb_row[:], gb8)
    ld_eye = dma("P", "dIP", eye_b[:], eye_f)
    ld_ones = dma("P", "dIP", ones_b[:], ones_f)
    ld_um = dma("P", "dIP", umask_sb[32:48, :], umask_d)
    ld_lnwf = dma("P", "dIP", lnwf_bt[:], lnwf_d)
    ld_lnbf = dma("P", "dIP", lnbf_bt[:], lnbf_d)
    ld_gw = dma("P", "dIP", gwTs[:].rearrange("p (kc n) -> p kc n", n=16),
                gwT.rearrange("(kc p) n -> p kc n", p=P))
    ld_wo = dma("P", "dIP", woT[:], woT_d)
    ld_wg = dma("P", "dW13b", w13sb[1][:].rearrange("p (kc m) -> p kc m", m=1024),
                wgT.rearrange("(kc p) m -> p kc m", p=P))
    ld_w13_0 = dma("P", "dW13a", w13sb[0][:].rearrange("p (kc m) -> p kc m", m=1024),
                   w13T[0].rearrange("(kc p) m -> p kc m", p=P))
    ld_w2_0 = dma("P", "dW2a", w2sb[0][:].rearrange("p (ic m) -> p ic m", m=1024),
                  w2T[0].rearrange("(ic p) m -> p ic m", p=P))

    ev_initY = ("dIY", cnt["dIY"])
    ev_initP = ("dIP", cnt["dIP"])
    ev_eps = op("V", lambda v: v.memset(epsc[:], float(EPS)), [], "V")
    ev_ub = op("V", lambda v: v.tensor_tensor(ub_sb[:], bsbi_sb[:, 0:1], bsbi_sb[:, 1:2], op=ALU.add),
               [ev_initY], "V")
    ev_wsumf = op("V", lambda v: v.tensor_tensor(wsumf, wsf, wif, op=ALU.add),
                  [ev_initY], "V")
    ev_wsumT = op("S", lambda s: s.copy(wsumT[:], wsumf), [ev_wsumf], "S")

    def bc_half(dst_ap, row_ap, waits, prev_free):
        mm = op("T", (lambda: (lambda t: t.matmul(
            CBp[:, 0:row_ap.shape[-1]], ones_b[:], row_ap, start=True, stop=True)))(),
            list(waits) + ([prev_free] if prev_free else []), "T")
        return op("V", (lambda: (lambda v: v.tensor_copy(dst_ap, CBp[:, 0:row_ap.shape[-1]])))(),
                  [mm], "V")

    ev_gb = bc_half(gb_bt[:], gb_row[:], [ev_initP], None)

    ev_mz0 = op("S", lambda s: s.memzero(M_t[0][:]), [], "S")
    ev_mz1 = op("S", lambda s: s.memzero(M_t[1][:]), [], "S")
    dma("P", "dTR0", M_t[0][16:17, :], m1r, [ev_mz0])
    ev_mz1 = dma("P", "dTR0", M_t[1][16:17, :], m1r, [ev_mz1])
    ev_mz1 = ("dTR0", 32)

    st_ev = {
        "xslot": [ev_wsumT, None, None],
        "xslot_w": [None] * 3,
        "trp_free": None,
        "cbp_free": ev_gb,
        "ga_free": [None, None], "up_free": [None, None],
        "y2_free": [None, None],
        "w13_ld": [ld_w13_0, ld_wg],
        "w2_ld": [ld_w2_0, None],
        "w13_rdT": [None, None], "w2_rdT": [None, None],
        "wg_rdT": None,
        "xn_tm_rd": None,
        "xn_fm_rd": None,
        "xn2_rd": None,
        "s_rd": None,
        "y_free": None,
        "out_dma0": None,
        "silu_free": [None, None],
        "h1_free": [None, None],
        "hp_free": [None, None],
        "cbrow_free": [None, None],
        "cb_evs": [None, None],
        "cb_lastread": [None, None],
        "rext_ev": [None, None],
        "tr_rot": [0],
        "gbuf_free": [None, None],
        "fbuf_free": [None, None],
        "xn2c_free": [None, None],
        "t2c_free": [None, None],
        "m_rd": ev_mz1,
        "stats_free": None,
        "stats_free2": None,
        "hfin_prev": None,
        "out_dma": [],
    }

    def xrow0(mt):
        return W + mt * MT

    def stats_chain(ME, MS, NV, SD, RS, NM, SU, SQ, c0, c1, last_sum, last_sq):
        def sl(i):
            return stats[:, i * 8 + c0: i * 8 + c1]
        m_ev = op("P", lambda g: g.tensor_scalar_mul(sl(ME), sl(SU), -1.0 / D), [last_sum], "Pc")
        q_ev = op("P", lambda g: g.tensor_scalar_mul(sl(MS), sl(SQ), 1.0 / D), [last_sq, m_ev], "Pc")
        m2_ev = op("P", lambda g: g.tensor_tensor(sl(NV), sl(ME), sl(ME), op=ALU.mult),
                   [m_ev, q_ev], "Pc")
        nv_ev = op("P", lambda g: g.tensor_tensor(sl(NV), sl(NV), sl(MS), op=ALU.subtract),
                   [m2_ev], "Pc")
        sd_ev = op("S", lambda s: s.activation(sl(SD), sl(NV), AF.Sqrt,
                                               bias=epsc[:, 0:1], scale=-1.0),
                   [nv_ev, ev_eps], "S")
        rs_ev = op("V", lambda v: v.reciprocal(sl(RS), sl(SD)), [sd_ev], "V")
        nm_ev = op("P", lambda g: g.tensor_tensor(sl(NM), sl(ME), sl(RS), op=ALU.mult),
                   [rs_ev], "Pc")
        return rs_ev, nm_ev

    def gen_A(mt, C):
        warm = (mt == 0)
        x_ev = [None] * TCH

        def issue_x(tch, extra):
            slot = tch % 3
            x_ev[tch] = dma("Y", f"dX{slot}",
                            xarena[:, slot * 1024:(slot + 1) * 1024],
                            xh[xrow0(mt) + tch * P: xrow0(mt) + (tch + 1) * P, :],
                            [st_ev["xslot"][slot], st_ev["xslot_w"][slot]] + extra)
            st_ev["xslot_w"][slot] = x_ev[tch]
            st_ev["xslot"][slot] = x_ev[tch]

        for t0 in range(3):
            issue_x(t0, [])
        if warm:
            wx_ev = dma("Y", "dWX", y_sb[0:W, 7168:8192], xh[0:W, :], [st_ev["y_free"]])
        yield
        wstate = {"uw": None}

        def gen_warm():
            wxs = y_sb[0:W, 7168:8192]
            wscr = y_sb[0:W, 6144:7168]
            a1 = op("V", lambda v: v.tensor_reduce(wstat[0:W, 0:1], wxs, axis=AX.X, op=ALU.add),
                    [wx_ev], "V")
            yield
            a3 = op("S", lambda s: s.activation(wscr, wxs, AF.Square, accum_out=wstat[0:W, 1:2]),
                    [wx_ev], "S")
            yield
            a4 = op("P", lambda g: g.tensor_scalar_mul(wstat[0:W, 2:3], wstat[0:W, 0:1], -1.0 / D),
                    [a1], "Pc")
            yield
            a5 = op("P", lambda g: g.tensor_scalar_mul(wstat[0:W, 3:4], wstat[0:W, 1:2], 1.0 / D),
                    [a3, a4], "Pc")
            yield
            a6 = op("P", lambda g: g.tensor_tensor(wstat[0:W, 4:5], wstat[0:W, 2:3],
                                                   wstat[0:W, 2:3], op=ALU.mult), [a5], "Pc")
            yield
            a7 = op("P", lambda g: g.tensor_tensor(wstat[0:W, 4:5], wstat[0:W, 4:5],
                                                   wstat[0:W, 3:4], op=ALU.subtract), [a6], "Pc")
            yield
            a8 = op("S", lambda s: s.activation(wstat[0:W, 5:6], wstat[0:W, 4:5], AF.Sqrt,
                                                bias=epsc[0:W, 0:1], scale=-1.0), [a7, ev_eps], "S")
            yield
            a9 = op("V", lambda v: v.reciprocal(wstat[0:W, 6:7], wstat[0:W, 5:6]), [a8], "V")
            yield
            a10 = op("P", lambda g: g.tensor_tensor(wstat[0:W, 7:8], wstat[0:W, 2:3],
                                                    wstat[0:W, 6:7], op=ALU.mult), [a9], "Pc")
            yield
            wxn_ev = op("S", lambda s: s.activation(wxn[0:W, :], wxs, AF.Identity,
                                                    bias=wstat[0:W, 7:8], scale=wstat[0:W, 6:7]),
                        [a10], "S")
            yield
            b1 = op("V", lambda v: v.tensor_reduce(wstat[0:W, 8:9], wxn[0:W, :], axis=AX.X,
                                                   op=ALU.add), [wxn_ev], "V")
            yield
            b3 = op("S", lambda s: s.activation(wscr, wxn[0:W, :], AF.Square,
                                                accum_out=wstat[0:W, 9:10]), [wxn_ev], "S")
            yield
            b4 = op("P", lambda g: g.tensor_scalar_mul(wstat[0:W, 10:11], wstat[0:W, 8:9],
                                                       -1.0 / D), [b1], "Pc")
            yield
            b5 = op("P", lambda g: g.tensor_scalar_mul(wstat[0:W, 11:12], wstat[0:W, 9:10],
                                                       1.0 / D), [b3, b4], "Pc")
            yield
            b6 = op("P", lambda g: g.tensor_tensor(wstat[0:W, 12:13], wstat[0:W, 10:11],
                                                   wstat[0:W, 10:11], op=ALU.mult), [b5], "Pc")
            yield
            b7 = op("P", lambda g: g.tensor_tensor(wstat[0:W, 12:13], wstat[0:W, 12:13],
                                                   wstat[0:W, 11:12], op=ALU.subtract), [b6], "Pc")
            yield
            b8 = op("S", lambda s: s.activation(wstat[0:W, 13:14], wstat[0:W, 12:13], AF.Sqrt,
                                                bias=epsc[0:W, 0:1], scale=-1.0), [b7, ev_eps], "S")
            yield
            b9 = op("V", lambda v: v.reciprocal(wstat[0:W, 14:15], wstat[0:W, 13:14]), [b8], "V")
            yield
            b10 = op("P", lambda g: g.tensor_tensor(wstat[0:W, 15:16], wstat[0:W, 10:11],
                                                    wstat[0:W, 14:15], op=ALU.mult), [b9], "Pc")
            yield
            wt1_ev = op("S", lambda s: s.activation(wxn2[0:W, :], wxn[0:W, :], AF.Identity,
                                                    bias=wstat[0:W, 15:16], scale=wstat[0:W, 14:15]),
                        [b10], "S")
            yield
            wstate["wxn2"] = wt1_ev
            yield

        wg_gen = gen_warm() if warm else iter(())

        def pump(gen, n):
            for _ in range(n):
                if next(gen, "END") == "END":
                    break

        sum_ev = [None] * TCH
        sqs_ev = [None] * TCH
        xn_ev = [None] * TCH
        for g in range(4):
            for tch in (2 * g, 2 * g + 1):
                xs = xarena[:, (tch % 3) * 1024:((tch % 3) + 1) * 1024]
                scr = xn_tm[:, tch * 1024:(tch + 1) * 1024]
                sum_ev[tch] = op("V", (lambda xs, tch: lambda v: v.tensor_reduce(
                    stc(SUMS, tch), xs, axis=AX.X, op=ALU.add))(xs, tch),
                    [x_ev[tch], st_ev["stats_free"], st_ev["stats_free2"]], "V")
                sqs_ev[tch] = op("S", (lambda xs, scr, tch: lambda s: s.activation(
                    scr, xs, AF.Square, accum_out=stc(SQS, tch)))(xs, scr, tch),
                    [x_ev[tch], st_ev["xn_tm_rd"], st_ev["stats_free"],
                     st_ev["stats_free2"]], "S")
            rs1_ev, nm1_ev = stats_chain(MEANB, MSQB, NVAR, STDB, RSTD, NMR, SUMS, SQS,
                                         2 * g, 2 * g + 2,
                                         sum_ev[2 * g + 1], sqs_ev[2 * g + 1])
            for tch in (2 * g, 2 * g + 1):
                xs = xarena[:, (tch % 3) * 1024:((tch % 3) + 1) * 1024]
                xn_ev[tch] = op("S", (lambda xs, tch: lambda s: s.activation(
                    xn_tm[:, tch * 1024:(tch + 1) * 1024], xs, AF.Identity,
                    bias=stc(NMR, tch), scale=stc(RSTD, tch)))(xs, tch),
                    [nm1_ev, rs1_ev], "S")
                st_ev["xslot"][tch % 3] = xn_ev[tch]
            for t in (2 * g + 3, 2 * g + 4):
                if t < TCH and x_ev[t] is None:
                    issue_x(t, [xn_ev[t - 3]])
            pump(wg_gen, 7)
            yield

        sum2_ev = [None] * TCH
        sqs2_ev = [None] * TCH
        for tch in range(TCH):
            xs = xn_tm[:, tch * 1024:(tch + 1) * 1024]
            scr = xn2c[tch % 2][:]
            scr_war = sqs2_ev[tch - 2] if tch >= 2 else st_ev["xn2c_free"][tch % 2]
            sum2_ev[tch] = op("V", (lambda xs, tch: lambda v: v.tensor_reduce(
                stc(SUMS2, tch), xs, axis=AX.X, op=ALU.add))(xs, tch), [xn_ev[tch]], "V")
            sqs2_ev[tch] = op("S", (lambda xs, scr, tch: lambda s: s.activation(
                scr, xs, AF.Square, accum_out=stc(SQS2, tch)))(xs, scr, tch),
                [xn_ev[tch], scr_war], "S")
            if tch % 2 == 1:
                pump(wg_gen, 6)
                yield
        rs2_ev, nm2_ev = stats_chain(MEAN2, MSQ2, NVAR2, STD2, RSTD2, NMR2, SUMS2, SQS2,
                                     0, 8, sum2_ev[-1], sqs2_ev[-1])
        pump(wg_gen, 99)
        uw_src_ev = wstate.get("wxn2")

        C.update(x_ev=x_ev, xn_ev=xn_ev, uw_src_ev=uw_src_ev,
                 nm2_ev=nm2_ev, rs2_ev=rs2_ev)
        yield

    def gen_B1(mt, C):
        xn_ev = C["xn_ev"]
        nm2_ev = C["nm2_ev"]
        rs2_ev = C["rs2_ev"]
        xn2_fm_ev = [None] * TCH
        t1_last = None
        for tch in range(TCH):
            c = tch % 2
            t1_ev = op("S", (lambda tch, c: lambda s: s.activation(
                xn2c[c][:], xn_tm[:, tch * 1024:(tch + 1) * 1024], AF.Identity,
                bias=stc(NMR2, tch), scale=stc(RSTD2, tch)))(tch, c),
                [nm2_ev, rs2_ev, st_ev["xn2c_free"][c]], "S")
            t1_last = t1_ev
            mm = None
            for dc in range(DC):
                mm = op("T", (lambda c, dc: lambda t: t.transpose(
                    TRp[:, dc * 128:(dc + 1) * 128],
                    xn2c[c][:, dc * 128:(dc + 1) * 128], eye_b[:]))(c, dc),
                    [t1_ev, st_ev["trp_free"], ev_initP] if dc == 0 else [],
                    "T" if dc == DC - 1 else None)
            ev2 = op("V", (lambda tch: lambda v: v.tensor_tensor(
                xn2_fm[:].rearrange("p (dc t) -> p dc t", t=1024)[:, :, tch * 128:(tch + 1) * 128],
                TRp[:].rearrange("p (dc t) -> p dc t", t=128),
                lnwf_bt[:].rearrange("p (dc t) -> p dc t", t=128),
                op=ALU.mult))(tch),
                [mm, st_ev["xn2_rd"]], "V")
            ev2b = op("P", (lambda tch: lambda g: g.tensor_tensor(
                xn2_fm[:].rearrange("p (dc t) -> p dc t", t=1024)[:, :, tch * 128:(tch + 1) * 128],
                xn2_fm[:].rearrange("p (dc t) -> p dc t", t=1024)[:, :, tch * 128:(tch + 1) * 128],
                lnbf_bt[:].rearrange("p (dc t) -> p dc t", t=128),
                op=ALU.add))(tch),
                [ev2], "Pc")
            xn2_fm_ev[tch] = ev2b
            st_ev["trp_free"] = ev2
            st_ev["xn2c_free"][c] = mm
            yield
        st_ev["stats_free"] = nm2_ev
        st_ev["stats_free2"] = t1_last
        C.update(xn2_fm_ev=xn2_fm_ev)
        yield

    def gen_B2(mt, C):
        xn_ev = C["xn_ev"]
        xn_fm_ev = [None] * TCH
        scores_mm_ev = None
        for tch in range(TCH):
            mm = None
            for dc in range(DC):
                mm = op("T", (lambda tch, dc: lambda t: t.transpose(
                    TRp[:, dc * 128:(dc + 1) * 128],
                    xn_tm[:, tch * 1024 + dc * 128: tch * 1024 + (dc + 1) * 128],
                    eye_b[:]))(tch, dc),
                    [xn_ev[tch], st_ev["trp_free"], ev_initP] if dc == 0 else [],
                    "T" if dc == DC - 1 else None)
            ev1 = op("V", (lambda tch: lambda v: v.tensor_copy(
                xn_fm[:].rearrange("p (dc t) -> p dc t", t=1024)[:, :, tch * 128:(tch + 1) * 128],
                TRp[:].rearrange("p (dc t) -> p dc t", t=128)))(tch),
                [mm, st_ev["xn_fm_rd"]], "V")
            xn_fm_ev[tch] = ev1
            st_ev["trp_free"] = ev1
            yield
        C.update(xn_fm_ev=xn_fm_ev)
        for tch in range(TCH):
            smm = None
            for kc in range(DC):
                smm = op("T", (lambda tch, kc: lambda t: t.matmul(
                    CBp[:, tch * 16:(tch + 1) * 16],
                    xn_fm[:, kc * 1024 + tch * 128: kc * 1024 + (tch + 1) * 128],
                    gwTs[:, kc * 16:(kc + 1) * 16],
                    start=(kc == 0), stop=(kc == DC - 1)))(tch, kc),
                    [xn_fm_ev[tch], ev_initP] + ([st_ev["cbp_free"]] if tch == 0 else []),
                    "T" if kc == DC - 1 else None)
            scores_mm_ev = smm
        C.update(scores_mm_ev=scores_mm_ev)
        yield

    def emit_R(mt, C):
        warm = (mt == 0)
        Lw = (W if warm else 0) + MT
        u_chunks = ([("w", 0, W)] if warm else []) + \
                   [("m", (W if warm else 0) + 0, 512), ("m", (W if warm else 0) + 512, 512)]
        sc_chunks = []
        c0 = 0
        while c0 < Lw:
            cw = min(512, Lw - c0)
            sc_chunks.append((c0, cw))
            c0 += cw
        x_ev = C["x_ev"]
        xn_ev = C["xn_ev"]
        xn2_fm_ev = C["xn2_fm_ev"]
        xn_fm_ev = C["xn_fm_ev"]
        scores_mm_ev = C["scores_mm_ev"]
        uw_src_ev = C["uw_src_ev"]
        if warm:
            trw_mm = None
            for dc in range(DC):
                trw_mm = op("T", (lambda dc: lambda t: t.transpose(
                    TRp[:, dc * W:(dc + 1) * W],
                    wxn2[0:W, dc * 128:(dc + 1) * 128], eye_b[0:W, 0:W]))(dc),
                    [uw_src_ev, st_ev["trp_free"], ev_initP] if dc == 0 else [],
                    "T" if dc == DC - 1 else None)
            uw_m = op("V", lambda v: v.tensor_tensor(
                xnw2_fm[:].rearrange("p (dc w) -> p dc w", w=W),
                TRp[:, 0:DC * W].rearrange("p (dc w) -> p dc w", w=W),
                lnwf_bt[:].rearrange("p (dc t) -> p dc t", t=128)[:, :, 0:W],
                op=ALU.mult), [trw_mm], "V")
            uw_src_ev = op("P", lambda g: g.tensor_tensor(
                xnw2_fm[:].rearrange("p (dc w) -> p dc w", w=W),
                xnw2_fm[:].rearrange("p (dc w) -> p dc w", w=W),
                lnbf_bt[:].rearrange("p (dc t) -> p dc t", t=128)[:, :, 0:W],
                op=ALU.add), [uw_m], "Pc")
            st_ev["trp_free"] = uw_m
        u_evs = []
        ping = 0
        m_write_evs = []
        for kind, c0, cw in u_chunks:
            if kind == "w":
                rhs = xnw2_fm[:, 0:DC * W].rearrange("p (kc t) -> p kc t", t=W)
                src_ev = uw_src_ev
            else:
                t0 = c0 - (W if warm else 0)
                rhs = xn2_fm[:].rearrange("p (kc t) -> p kc t", t=1024)[:, :, t0:t0 + cw]
                src_ev = xn2_fm_ev[-1]
            mm = None
            for kc in range(DC):
                mm = op("T", (lambda rhs, kc, cw, ping: lambda t: t.matmul(
                    Y2[ping][0:16, 0:cw], wsumT[:, kc * 16:(kc + 1) * 16], rhs[:, kc, :],
                    start=(kc == 0), stop=(kc == DC - 1)))(rhs, kc, cw, ping),
                    [src_ev, ev_wsumT, st_ev["y2_free"][ping]] if kc == 0 else [],
                    "T" if kc == DC - 1 else None)
            ev_e = op("S", (lambda c0, cw, ping: lambda s: s.activation(
                M_t[0][32:48, c0:c0 + cw], Y2[ping][0:16, 0:cw], AF.Identity,
                bias=ub_sb[:]))(c0, cw, ping), [mm, ev_ub, st_ev["m_rd"]], "S")
            st_ev["y2_free"][ping] = ev_e
            u_evs.append(ev_e)
            m_write_evs.append(ev_e)
            ping ^= 1
        if warm:
            mask_ev = op("V", lambda v: v.tensor_tensor(
                M_t[0][32:48, 0:W], M_t[0][32:48, 0:W], umask_sb[32:48, :], op=ALU.mult),
                [u_evs[0], ev_initP], "V")
            m_write_evs.append(mask_ev)
        ucopy_ev = op("P", (lambda Lw: lambda g: g.tensor_copy(
            M_t[1][32:48, 0:Lw], M_t[0][32:48, 0:Lw]))(Lw), m_write_evs + [st_ev["m_rd"]], "Pc")

        if warm:
            pass
        else:
            hb = st_ev["hfin_prev"]
            z0 = op("V", lambda v: v.tensor_copy(M_t[0][0:16, 0:1], M_t[0][0:16, 1048:1049]),
                    [hb, st_ev["m_rd"]], "V")
            z1 = op("V", lambda v: v.tensor_copy(M_t[1][0:16, 0:1], M_t[0][0:16, 1048:1049]),
                    [z0], "V")
            m_write_evs += [z0, z1]

        sg_ev = op("S", lambda s: s.activation(scores_sb[:], CBp[:, 0:128], AF.Sigmoid),
                   [scores_mm_ev], "S")
        bi_ev = op("V", lambda v: v.tensor_tensor(biased, scores_sb[:], gb_bt[:], op=ALU.add),
                   [sg_ev, ev_gb, st_ev["xslot"][2], st_ev["xslot_w"][2]], "V")
        m8_ev = None
        for tch in range(TCH):
            m8_ev = op("V", (lambda tch: lambda v: v.max(
                out=m8s[:, tch * 8:(tch + 1) * 8], in_=biased[:, tch * 16:(tch + 1) * 16]))(tch),
                [bi_ev], "V")
        mk_ev = None
        for tch in range(TCH):
            mk_ev = op("V", (lambda tch: lambda v: v.tensor_scalar(
                masks[:, tch * 16:(tch + 1) * 16], biased[:, tch * 16:(tch + 1) * 16],
                m8s[:, tch * 8 + 3: tch * 8 + 4], None, op0=ALU.is_ge))(tch),
                [m8_ev], "V")
        wr_ev = op("V", lambda v: v.tensor_tensor(wraw, masks, scores_sb[:], op=ALU.mult),
                   [mk_ev], "V")
        sw_ev = op("V", lambda v: v.tensor_reduce(
            sumw, wraw.rearrange("p (a b) -> p a b", b=16), axis=AX.X, op=ALU.add),
            [wr_ev], "V")
        rs_ev = op("V", lambda v: v.reciprocal(rsw, sumw), [sw_ev], "V")
        cmb_ev = None
        for tch in range(TCH):
            cmb_ev = op("V", (lambda tch: lambda v: v.tensor_scalar(
                comb_tm[:, tch * 16:(tch + 1) * 16], wraw[:, tch * 16:(tch + 1) * 16],
                rsw[:, tch:tch + 1], 2.5, op0=ALU.mult, op1=ALU.mult))(tch),
                [rs_ev], "V")

        comb_done = [None]

        def emit_comb_tr():
            mm = None
            for tch in range(TCH):
                mm = op("T", (lambda tch: lambda t: t.transpose(
                    TRp[0:16, tch * 128:(tch + 1) * 128],
                    comb_tm[:, tch * 16:(tch + 1) * 16], eye_b[:]))(tch),
                    [cmb_ev, st_ev["trp_free"]] if tch == 0 else [],
                    "T" if tch == TCH - 1 else None)
            ev = op("V", lambda v: v.tensor_copy(comb_fm[:], TRp[0:16, 0:1024]), [mm], "V")
            st_ev["trp_free"] = ev
            comb_done[0] = ev
        st_ev["cbp_free"] = sg_ev
        st_ev["xslot"][2] = cmb_ev
        st_ev["xslot_w"][2] = rs_ev

        def emit_rext(ex):
            b = ex % 2
            ev = dma("Y", f"dCB{b}", cbrow[b][:], comb_fm[ex:ex + 1, :],
                     [comb_done[0], st_ev["cbrow_free"][b]])
            st_ev["rext_ev"][b] = ev

        def emit_cb_half(ex, h):
            b = ex % 2
            mmcb = op("T", (lambda b, h: lambda t: t.matmul(
                CBp[:], ones_b[:], cbrow[b][:, h * 512:(h + 1) * 512],
                start=True, stop=True))(b, h),
                [st_ev["rext_ev"][b], st_ev["cbp_free"]], "T")
            cbe = op("S", (lambda b, h: lambda s: s.activation(
                cb_sb[b][:, h * 512:(h + 1) * 512], CBp[:], AF.Identity))(b, h),
                [mmcb, st_ev["cb_lastread"][b]], "S")
            st_ev["cbp_free"] = cbe
            if h == 0:
                st_ev["cb_evs"][b] = [cbe, None]
            else:
                st_ev["cb_evs"][b][1] = cbe
                st_ev["cbrow_free"][b] = mmcb


        ghosts = [cb_sb[0][:], cb_sb[1][:], xn2c[0][:], xn2c[1][:], t2c[0][:]] + \
                 [hp[0][:, t * 1024:(t + 1) * 1024] for t in range(3)]
        ghost_free = [st_ev["cb_lastread"][0], st_ev["cb_lastread"][1],
                      st_ev["xn2c_free"][0], st_ev["xn2c_free"][1], st_ev["t2c_free"][0],
                      st_ev["hp_free"][0], st_ev["hp_free"][1], st_ev["hp_free"][0]]
        sg_evs = [[None, None] for _ in range(TCH)]
        gate_last_mm = None

        def emit_gate_group(tch, h):
            nonlocal gate_last_mm
            pair = [GA[0], GA[1]] if tch % 2 == 0 else [UP[0], UP[1]]
            pfree = st_ev["ga_free"] if tch % 2 == 0 else st_ev["up_free"]
            mm = None
            for kc in range(DC):
                mm = op("T", (lambda tch, h, kc, pair: lambda t: t.matmul(
                    pair[h][:], xn2_fm[:, kc * 1024 + tch * 128: kc * 1024 + (tch + 1) * 128],
                    w13sb[1][:, kc * 1024 + h * 512: kc * 1024 + (h + 1) * 512],
                    start=(kc == 0), stop=(kc == DC - 1)))(tch, h, kc, pair),
                    ([xn2_fm_ev[tch], st_ev["w13_ld"][1], pfree[h]] if kc == 0 else []),
                    "T" if kc == DC - 1 else None)
            gate_last_mm = mm
            sgm = op("S", (lambda tch, h, pair: lambda s: s.activation(
                ghosts[tch][:, h * 512:(h + 1) * 512], pair[h][:], AF.Sigmoid))(tch, h, pair),
                [mm, ghost_free[tch]], "S")
            pfree[h] = sgm
            sg_evs[tch][h] = sgm

        gate_seq = [(t, hh) for t in range(TCH) for hh in range(2)]
        gi = 0

        nch = len(sc_chunks)
        scan_tanh_ev = {}
        for j in range(1, K_J + 1):
            for _ in range(2):
                if gi < len(gate_seq):
                    emit_gate_group(*gate_seq[gi])
                    gi += 1
            src_t = M_t[(j + 1) % 2]
            dst = M_t[j % 2]
            for ci, (c0, cw) in enumerate(sc_chunks):
                pg = ci % 2
                if j == 1:
                    waits = list(m_write_evs)
                elif j == 2:
                    waits = [scan_tanh_ev[(j - 1, ci)], ucopy_ev]
                    if ci > 0:
                        waits.append(scan_tanh_ev[(j - 1, ci - 1)])
                else:
                    waits = [scan_tanh_ev[(j - 1, ci)]]
                    if ci > 0:
                        waits.append(scan_tanh_ev[(j - 1, ci - 1)])
                waits.append(st_ev["y2_free"][pg])
                mm = op("T", (lambda src_t, c0, cw, pg: lambda t: t.matmul(
                    Y2[pg][0:16, 0:cw], a_sb[:], src_t[0:64, c0:c0 + cw],
                    start=True, stop=True))(src_t, c0, cw, pg),
                    waits + ([ev_initP] if (j, ci) == (1, 0) else []), "T")
                th = op("S", (lambda dst, c0, cw, pg: lambda s: s.activation(
                    dst[0:16, c0 + 1:c0 + 1 + cw], Y2[pg][0:16, 0:cw], AF.Tanh))(dst, c0, cw, pg),
                    [mm], "S")
                st_ev["y2_free"][pg] = th
                scan_tanh_ev[(j, ci)] = th
            if j == 2:
                emit_comb_tr()
        while gi < len(gate_seq):
            emit_gate_group(*gate_seq[gi])
            gi += 1
        hfin = M_t[K_J % 2]
        hfin_ev = [scan_tanh_ev[(K_J, ci)] for ci in range(nch)]
        st_ev["hfin_prev"] = hfin_ev[-1]
        h_off = (W if warm else 0) + 1

        emit_rext(0)

        if mt == 1:
            ld1 = dma("P", "dW13b",
                      w13sb[1][:].rearrange("p (kc m) -> p kc m", m=1024),
                      w13T[0].rearrange("(kc p) m -> p kc m", p=P),
                      [gate_last_mm])
            ld2 = dma("P", "dW2b",
                      w2sb[1][:].rearrange("p (ic m) -> p ic m", m=1024),
                      w2T[0].rearrange("(ic p) m -> p ic m", p=P),
                      [st_ev["w2_rdT"][1]])
            st_ev["w13_ld"][1] = ld1
            st_ev["w2_ld"][1] = ld2
        st_ev["wg_rdT"] = gate_last_mm
        st_ev["xn2_rd"] = gate_last_mm
        C.update(hfin=hfin, hfin_ev=hfin_ev, h_off=h_off, sg_evs=sg_evs,
                 ghosts=ghosts, xn_ev=C["xn_ev"], sc_chunks=sc_chunks,
                 comb_ev=comb_done[0], emit_rext=emit_rext, emit_cb_half=emit_cb_half)

    def gen_R2(mt, C):
        hfin = C["hfin"]
        hfin_ev = C["hfin_ev"]
        h_off = C["h_off"]
        sg_evs = C["sg_evs"]
        ghosts = C["ghosts"]
        xn_ev = C["xn_ev"]
        sc_chunks = C["sc_chunks"]
        r2_slots = [Y2[0], Y2[1], CBp, GA[0]]

        def r2_free_get(k):
            if k < 2:
                return st_ev["y2_free"][k]
            return st_ev["cbp_free"] if k == 2 else st_ev["ga_free"][0]

        def r2_free_set(k, ev):
            if k < 2:
                st_ev["y2_free"][k] = ev
            elif k == 2:
                st_ev["cbp_free"] = ev
            else:
                st_ev["ga_free"][0] = ev
        tb = [xarena[:, 2048:2560], xarena[:, 2560:3072],
              y_sb[:, 0:512], y_sb[:, 512:1024]]
        tb_free = [[st_ev["xslot_w"][2], st_ev["xslot"][2]],
                   [st_ev["xslot_w"][2], st_ev["xslot"][2]],
                   [st_ev["out_dma0"]], [st_ev["out_dma0"]]]
        s_ev = [None] * TCH
        t2_by_tch = [None] * TCH
        for tch in range(TCH):
            lo = h_off + tch * 128
            hi = lo + 128
            hw = [hfin_ev[ci] for ci, (c0, cw) in enumerate(sc_chunks)
                  if c0 + 1 < hi and c0 + 1 + cw > lo]
            for h in range(2):
                k = (tch * 2 + h) % 4
                ps = r2_slots[k]
                mm = op("T", (lambda tch, h, ps: lambda t: t.matmul(
                    ps[:], hfin[0:17, h_off + tch * 128: h_off + (tch + 1) * 128],
                    woT[:, h * 512:(h + 1) * 512], start=True, stop=True))(tch, h, ps),
                    hw + [ev_initP, r2_free_get(k)], "T")
                tbuf = tb[k]
                t1e = op("S", (lambda tbuf, ps: lambda s: s.activation(
                    tbuf, ps[:], AF.Identity))(tbuf, ps),
                    [mm] + tb_free[k], "S")
                r2_free_set(k, t1e)
                t2e = op("P", (lambda tbuf, tch, h: lambda g: g.tensor_tensor(
                    tbuf, tbuf, ghosts[tch][:, h * 512:(h + 1) * 512], op=ALU.mult))(tbuf, tch, h),
                    [t1e, sg_evs[tch][h]], "Pc")
                s_e = op("V", (lambda tbuf, tch, h: lambda v: v.tensor_tensor(
                    s_tm[:, tch * 1024 + h * 512: tch * 1024 + (h + 1) * 512],
                    tbuf,
                    xn_tm[:, tch * 1024 + h * 512: tch * 1024 + (h + 1) * 512],
                    op=ALU.add))(tbuf, tch, h),
                    [t2e, xn_ev[tch], st_ev["s_rd"]], "V")
                tb_free[k] = [s_e]
                if k < 2:
                    st_ev["xslot"][2] = s_e
                    st_ev["xslot_w"][2] = t2e
            s_ev[tch] = s_e
            t2_by_tch[tch] = t2e
            yield
        st_ev["cb_lastread"][0] = t2_by_tch[0]
        st_ev["cb_lastread"][1] = t2_by_tch[1]
        st_ev["xn2c_free"][0] = t2_by_tch[2]
        st_ev["xn2c_free"][1] = t2_by_tch[3]
        st_ev["t2c_free"][0] = t2_by_tch[4]
        st_ev["t2c_free"][1] = t2_by_tch[4]
        st_ev["hp_free"][0] = t2_by_tch[7]
        st_ev["hp_free"][1] = t2_by_tch[7]
        st_ev["xn_tm_rd"] = s_ev[-1]
        C.update(s_ev=s_ev)
        yield


    def emit_E(mt, C, gen1=None, gen2=None):
        xn_fm_ev = C["xn_fm_ev"]
        comb_ev = C["comb_ev"]
        emit_rext = C["emit_rext"]
        emit_cb_half = C["emit_cb_half"]

        def pump(gen, n):
            if gen is None:
                return
            for _ in range(n):
                if next(gen, "END") == "END":
                    break
        y_acc_ev = [[None, None] for _ in range(TCH)]
        h2_last_by_buf = [None, None]
        st1_lastmm_by_buf = [None, None]
        st2_lastmm_by_buf = [None, None]

        for ex in range(NEXP):
            if ex + 1 < 16:
                emit_rext(ex + 1)
            pump(gen1, 1)
            buf = (ex + mt) % 2
            if ex + 1 < NEXP:
                nbuf = buf ^ 1
                free_w13 = st_ev["wg_rdT"] if (mt == 0 and ex == 0) else st1_lastmm_by_buf[nbuf]
                if free_w13 is None:
                    free_w13 = st_ev["w13_rdT"][nbuf]
                ld1 = dma("P", "dW13a" if nbuf == 0 else "dW13b",
                          w13sb[nbuf][:].rearrange("p (kc m) -> p kc m", m=1024),
                          w13T[ex + 1].rearrange("(kc p) m -> p kc m", p=P),
                          [free_w13])
                ld2 = dma("P", "dW2a" if nbuf == 0 else "dW2b",
                          w2sb[nbuf][:].rearrange("p (ic m) -> p ic m", m=1024),
                          w2T[ex + 1].rearrange("(ic p) m -> p ic m", p=P),
                          [st2_lastmm_by_buf[nbuf] or st_ev["w2_rdT"][nbuf]])
                st_ev["w13_ld"][nbuf] = ld1
                st_ev["w2_ld"][nbuf] = ld2
            elif mt == 0:
                ldwg = dma("P", "dW13b",
                           w13sb[1][:].rearrange("p (kc m) -> p kc m", m=1024),
                           wgT.rearrange("(kc p) m -> p kc m", p=P),
                           [st1_lastmm_by_buf[1]])
                st_ev["w13_ld"][1] = ldwg

            h2_evs = []
            h2_last_half = [None, None]
            gidx = 0
            for h in range(2):
                for i in range(4):
                    if (h, i) == (1, 0):
                        pump(gen1, 1)
                    if (h, i) == (1, 2) and ex < 8:
                        pump(gen1, 1)
                    if ex == 0 and (h, i) == (0, 0):
                        emit_cb_half(0, 0)
                    if ex == 0 and (h, i) == (0, 1):
                        emit_cb_half(0, 1)
                    if ex + 1 < 16 and (h, i) == ((1, 0) if ex == 0 else (0, 1)):
                        emit_cb_half(ex + 1, 0)
                    if ex + 1 < 16 and (h, i) == ((1, 2) if ex == 0 else (0, 2)):
                        emit_cb_half(ex + 1, 1)
                    s1 = gidx % 2
                    gidx += 1
                    mm = None
                    for kc in range(DC):
                        mm = op("T", (lambda buf, i, h, kc, s1: lambda t: t.matmul(
                            GA[s1][:],
                            w13sb[buf][:, kc * 1024 + i * 128: kc * 1024 + (i + 1) * 128],
                            xn_fm[:, kc * 1024 + h * 512: kc * 1024 + (h + 1) * 512],
                            start=(kc == 0), stop=(kc == DC - 1)))(buf, i, h, kc, s1),
                            ([st_ev["w13_ld"][buf], xn_fm_ev[-1], st_ev["ga_free"][s1]]
                             if kc == 0 else []),
                            "T" if kc == DC - 1 else None)
                    gate_mm = mm
                    mm = None
                    for kc in range(DC):
                        mm = op("T", (lambda buf, i, h, kc, s1: lambda t: t.matmul(
                            UP[s1][:],
                            w13sb[buf][:, kc * 1024 + (i + 4) * 128: kc * 1024 + (i + 5) * 128],
                            xn_fm[:, kc * 1024 + h * 512: kc * 1024 + (h + 1) * 512],
                            start=(kc == 0), stop=(kc == DC - 1)))(buf, i, h, kc, s1),
                            ([st_ev["up_free"][s1]] if kc == 0 else []),
                            "T" if kc == DC - 1 else None)
                    up_mm = mm
                    st1_lastmm_by_buf[buf] = up_mm
                    sil = op("S", (lambda s1: lambda s: s.activation(
                        silu_sb[s1][:], GA[s1][:], AF.Silu))(s1),
                        [gate_mm, st_ev["silu_free"][s1]], "S")
                    st_ev["ga_free"][s1] = sil
                    h1e = op("V", (lambda s1: lambda v: v.tensor_tensor(
                        h1b[s1][:], silu_sb[s1][:], UP[s1][:], op=ALU.mult))(s1),
                        [sil, up_mm, st_ev["h1_free"][s1]], "V")
                    st_ev["up_free"][s1] = h1e
                    st_ev["silu_free"][s1] = h1e
                    if ex < 16:
                        h2e = op("P", (lambda ex, buf, i, h, s1: lambda g: g.tensor_tensor(
                            hp[buf][:, i * 1024 + h * 512: i * 1024 + (h + 1) * 512],
                            h1b[s1][:], cb_sb[ex % 2][:, h * 512:(h + 1) * 512],
                            op=ALU.mult))(ex, buf, i, h, s1),
                            [h1e, st_ev["cb_evs"][ex % 2][h], st_ev["hp_free"][0],
                             st_ev["hp_free"][1]], "Pc")
                        st_ev["cb_lastread"][ex % 2] = h2e
                    else:
                        h2e = op("P", (lambda buf, i, h, s1: lambda g: g.tensor_copy(
                            hp[buf][:, i * 1024 + h * 512: i * 1024 + (h + 1) * 512],
                            h1b[s1][:]))(buf, i, h, s1),
                            [h1e, st_ev["hp_free"][0], st_ev["hp_free"][1]], "Pc")
                    st_ev["h1_free"][s1] = h2e
                    h2_evs.append(h2e)
                    h2_last_half[h] = h2e
            h2_last_by_buf[buf] = h2_evs[-1]

            if ex == NEXP - 1:
                pump(gen1, 99)
                st_ev["xn_fm_rd"] = st1_lastmm_by_buf[buf]
            st2_slots = [Y2[0], Y2[1], CBp]
            nslot = 2 if ex == NEXP - 1 else 3
            for tch in range(TCH):
                if ex == NEXP - 1:
                    pump(gen2, 1)
                for dh in range(2):
                    s2 = (tch * 2 + dh) % nslot
                    ps = st2_slots[s2]
                    free_ev = st_ev["y2_free"][s2] if s2 < 2 else st_ev["cbp_free"]
                    mm = None
                    for ic in range(4):
                        mm = op("T", (lambda buf, tch, dh, ic, ps: lambda t: t.matmul(
                            ps[:],
                            hp[buf][:, ic * 1024 + tch * 128: ic * 1024 + (tch + 1) * 128],
                            w2sb[buf][:, ic * 1024 + dh * 512: ic * 1024 + (dh + 1) * 512],
                            start=(ic == 0), stop=(ic == 3)))(buf, tch, dh, ic, ps),
                            ([h2_last_half[tch // 4], st_ev["w2_ld"][buf],
                              free_ev] if ic == 0 else []),
                            "T" if ic == 3 else None)
                    st2_lastmm_by_buf[buf] = mm
                    if ex == 0:
                        acc = op("V", (lambda tch, dh, ps: lambda v: v.tensor_copy(
                            y_sb[:, tch * 1024 + dh * 512: tch * 1024 + (dh + 1) * 512],
                            ps[:]))(tch, dh, ps),
                            [mm, st_ev["y_free"]], "V")
                    else:
                        acc = op("V", (lambda tch, dh, ps: lambda v: v.tensor_tensor(
                            y_sb[:, tch * 1024 + dh * 512: tch * 1024 + (dh + 1) * 512],
                            y_sb[:, tch * 1024 + dh * 512: tch * 1024 + (dh + 1) * 512],
                            ps[:], op=ALU.add))(tch, dh, ps),
                            [mm, y_acc_ev[tch][dh]], "V")
                    y_acc_ev[tch][dh] = acc
                    if s2 < 2:
                        st_ev["y2_free"][s2] = acc
                    else:
                        st_ev["cbp_free"] = acc
            st_ev["hp_free"][0] = st2_lastmm_by_buf[buf]
            st_ev["hp_free"][1] = st2_lastmm_by_buf[buf]
            st_ev["w13_rdT"][buf] = st1_lastmm_by_buf[buf]
            st_ev["w2_rdT"][buf] = st2_lastmm_by_buf[buf]

        pump(gen2, 99)
        st_ev["xn_fm_rd"] = st1_lastmm_by_buf[(16 + mt) % 2]
        st_ev["m_rd"] = st2_lastmm_by_buf[(16 + mt) % 2]
        C.update(y_acc_ev=y_acc_ev)

    def emit_tail(mt, C):
        y_acc_ev = C["y_acc_ev"]
        s_ev = C["s_ev"]
        out_dma_evs = []
        gx = [hp[0][:, 3072:4096], fbuf[0][:]]
        gx_free = [st_ev["hp_free"][0], st_ev["fbuf_free"][0]]
        f1_prev = None
        f2m2 = [None, None]
        for tch in range(TCH):
            b = tch % 2
            ge = op("S", (lambda tch, b: lambda s: s.activation(
                gx[b], y_sb[:, tch * 1024:(tch + 1) * 1024], AF.Silu))(tch, b),
                [y_acc_ev[tch][0], y_acc_ev[tch][1], gx_free[b], f2m2[b]], "S")
            slot = tch % 3
            xe = dma("Y", f"dX{slot}",
                     xarena[:, slot * 1024:(slot + 1) * 1024],
                     xh[xrow0(mt) + tch * P: xrow0(mt) + (tch + 1) * P, :],
                     [st_ev["xslot"][slot], st_ev["xslot_w"][slot]])
            st_ev["xslot_w"][slot] = xe
            f1 = op("V", (lambda tch, b: lambda v: v.tensor_tensor(
                gx[b], s_tm[:, tch * 1024:(tch + 1) * 1024], gx[b], op=ALU.mult))(tch, b),
                [ge, s_ev[tch], st_ev["s_rd"]], "V")
            f1_prev = f1
            f2 = op("P", (lambda tch, slot, b: lambda g: g.tensor_tensor(
                y_sb[:, tch * 1024:(tch + 1) * 1024],
                gx[b], xarena[:, slot * 1024:(slot + 1) * 1024],
                op=ALU.add))(tch, slot, b),
                [f1, xe, ge], "Pc")
            f2m2[b] = f2
            st_ev["xslot"][slot] = f2
            ode = dma("P", "dO", out_d[mt * MT + tch * P: mt * MT + (tch + 1) * P, :],
                      y_sb[:, tch * 1024:(tch + 1) * 1024], [f2])
            out_dma_evs.append(ode)
        st_ev["y_free"] = out_dma_evs[-1]
        st_ev["out_dma0"] = out_dma_evs[-1]
        st_ev["hp_free"][0] = f2m2[0]
        st_ev["hp_free"][1] = f2m2[0]
        st_ev["fbuf_free"][0] = f2m2[1]
        st_ev["fbuf_free"][1] = f2m2[1]
        st_ev["s_rd"] = f1_prev

    import itertools
    C0 = {}
    for _ in gen_A(0, C0):
        pass
    for _ in gen_B1(0, C0):
        pass
    for _ in gen_B2(0, C0):
        pass
    emit_R(0, C0)
    for _ in gen_R2(0, C0):
        pass
    if n_mts > 1:
        C1 = {}
        g1 = itertools.chain(gen_A(1, C1), gen_B1(1, C1))
        g2 = gen_B2(1, C1)
        emit_E(0, C0, gen1=g1, gen2=g2)
        emit_tail(0, C0)
        emit_R(1, C1)
        for _ in gen_R2(1, C1):
            pass
        emit_E(1, C1)
        emit_tail(1, C1)
    else:
        for _ in gen_R2(0, C0):
            pass
        emit_E(0, C0)
        emit_tail(0, C0)

    if dbg_taps:
        allev = [("T", cnt["T"]), ("S", cnt["S"]), ("V", cnt["V"]), ("Pc", cnt["Pc"])]
        tensors = dict(xn_tm=xn_tm, xn_fm=xn_fm, xn2_fm=xn2_fm, s_tm=s_tm, y_sb=y_sb,
                       comb_tm=comb_tm, comb_fm=comb_fm, scores_sb=scores_sb, M0=M_t[0],
                       M1=M_t[1], stats=stats, biased=biased, m8s=m8s, wraw=wraw,
                       xnw2_fm=xnw2_fm, hp0=hp[0], hp1=hp[1], cb_sb=cb_sb)
        for name, shape, dt in dbg_taps:
            ap = tensors[name]
            dma("Y", "dDBG", dbg_drams[name], ap[0:shape[0], 0:shape[1]], allev)

    n_out = cnt["dO"]
    ops["Y"].append(([("dO", n_out), ("dDBG", cnt["dDBG"])], None, None, 0))
    ops["V"].append(([("dO", n_out), ("dDBG", cnt["dDBG"])], None, None, 0))

    with nc.Block() as block:
        def emit_stream(eng, stream):
            waited = {}
            for waits, fn, inc, inck in ops[stream]:
                for wv in waits:
                    if wv is None:
                        continue
                    sname, val = wv
                    if val <= 0 or waited.get(sname, 0) >= val:
                        continue
                    eng.wait_ge(sems[sname], val)
                    waited[sname] = val
                if fn is None:
                    continue
                inst = fn(eng)
                if inc is not None:
                    inst.then_inc(sems[inc], inck)

        block.tensor(lambda t: emit_stream(t, "T"))
        block.scalar(lambda s: emit_stream(s, "S"))
        block.vector(lambda v: emit_stream(v, "V"))
        block.gpsimd(lambda g: emit_stream(g, "P"))
        block.sync(lambda s: emit_stream(s, "Y"))

    ctx.close()
    return nc


_NC_CACHE = {}


def _get_nc():
    if "nc" not in _NC_CACHE:
        _NC_CACHE["nc"] = build_program()
    return _NC_CACHE["nc"]


def make_in_maps(L0_attn_x, L0_attn_Ws, L0_attn_bs, L0_attn_Wi, L0_attn_bi,
                 L0_attn_Wo, L0_attn_bo, L0_attn_Wg, L0_attn_bg, L0_attn_ln_w,
                 L0_attn_ln_b, L0_attn_A, L0_ffn_gate_w, L0_ffn_gate_b,
                 L0_ffn_w13, L0_ffn_w2, L0_ffn_w13_s, L0_ffn_w2_s):
    f = np.float32
    x = np.asarray(L0_attn_x, f)
    w13T = np.concatenate([np.transpose(np.asarray(L0_ffn_w13, f), (0, 2, 1)),
                           np.asarray(L0_ffn_w13_s, f).T[None]], 0)
    w13T = np.ascontiguousarray(w13T)
    w2T = np.concatenate([np.transpose(np.asarray(L0_ffn_w2, f), (0, 2, 1)),
                          np.asarray(L0_ffn_w2_s, f).T[None]], 0)
    w2T = np.ascontiguousarray(w2T)
    wgT = np.ascontiguousarray(np.asarray(L0_attn_Wg, f).T)
    woT = np.ascontiguousarray(np.concatenate(
        [np.asarray(L0_attn_Wo, f).T,
         np.asarray(L0_attn_bo, f)[None]], 0))
    wsT = np.ascontiguousarray(np.asarray(L0_attn_Ws, f).T)
    wiT = np.ascontiguousarray(np.asarray(L0_attn_Wi, f).T)
    gwT = np.ascontiguousarray(np.asarray(L0_ffn_gate_w, f).T)
    a_st = np.zeros((64, 16), f)
    a_st[0:16, :] = np.asarray(L0_attn_A, f).T
    a_st[32:48, :] = np.eye(16, dtype=f)
    bsbi = np.stack([np.asarray(L0_attn_bs, f), np.asarray(L0_attn_bi, f)], 1)
    gb8 = np.tile(np.asarray(L0_ffn_gate_b, f), 8)[None]
    lnwf = np.repeat(np.asarray(L0_attn_ln_w, f).reshape(8, 128).T, 128, axis=1)
    lnbf = np.repeat(np.asarray(L0_attn_ln_b, f).reshape(8, 128).T, 128, axis=1)
    lnwf = np.ascontiguousarray(lnwf)
    lnbf = np.ascontiguousarray(lnbf)
    eye_f = np.eye(P, dtype=f)
    ones_f = np.ones((1, P), f)

    in_maps = []
    for core in range(8):
        b = core // 4
        q = core % 4
        t0 = q * NTOK
        xs = np.empty((W + NTOK, D), f)
        if q == 0:
            xs[0:W] = 0.0
            um = np.zeros((16, W), f)
        else:
            xs[0:W] = x[b, t0 - W:t0]
            um = np.ones((16, W), f)
        xs[W:] = x[b, t0:t0 + NTOK]
        in_maps.append({
            "xh": xs, "w13T": w13T, "w2T": w2T, "wgT": wgT, "woT": woT,
            "wsT": wsT, "wiT": wiT, "gwT": gwT, "a_st": a_st, "bsbi": bsbi,
            "gb8": gb8, "lnwf": lnwf, "lnbf": lnbf,
            "m1r": np.ones((1, 1056), f),
            "eye_f": eye_f, "ones_f": ones_f, "umask": um,
        })
    return in_maps


def kernel(L0_attn_x, L0_attn_Ws, L0_attn_bs, L0_attn_Wi, L0_attn_bi,
           L0_attn_Wo, L0_attn_bo, L0_attn_Wg, L0_attn_bg, L0_attn_ln_w,
           L0_attn_ln_b, L0_attn_A, L0_ffn_input_ids, L0_ffn_gate_w,
           L0_ffn_gate_b, L0_ffn_w13, L0_ffn_w2, L0_ffn_w13_s, L0_ffn_w2_s):
    in_maps = make_in_maps(L0_attn_x, L0_attn_Ws, L0_attn_bs, L0_attn_Wi,
                           L0_attn_bi, L0_attn_Wo, L0_attn_bo, L0_attn_Wg,
                           L0_attn_bg, L0_attn_ln_w, L0_attn_ln_b, L0_attn_A,
                           L0_ffn_gate_w, L0_ffn_gate_b, L0_ffn_w13,
                           L0_ffn_w2, L0_ffn_w13_s, L0_ffn_w2_s)
    nc = _get_nc()
    res = run_bass_kernel_spmd(nc, in_maps, core_ids=list(range(8)))
    out = np.empty((2, 8192, 1024), np.float32)
    for core in range(8):
        b = core // 4
        q = core % 4
        out[b, q * NTOK:(q + 1) * NTOK] = res.results[core]["out"]
    return out
